# version 12
# baseline (speedup 1.0000x reference)
import contextlib
import numpy as np
import ml_dtypes
import concourse.bass as bass
import concourse.mybir as mybir
from concourse.bass_utils import run_bass_kernel_spmd

F32 = mybir.dt.float32
BF16 = mybir.dt.bfloat16
F16 = mybir.dt.float16
I32 = mybir.dt.int32
AF = mybir.ActivationFunctionType
ALU = mybir.AluOpType
BF = ml_dtypes.bfloat16

T = 4096
D = 1024
PAD = 4
NT = T // 128
ND = D // 128
TC = 512

ENGINES = ("tensor", "vector", "scalar", "gpsimd", "sync")


class ColView:
    def __init__(self, buf, c0):
        self.key = buf.key
        self.h = buf.h
        self.c0 = c0

    def __getitem__(self, idx):
        rows, cols = idx
        return self.h[rows, cols.start + self.c0:cols.stop + self.c0]

    @property
    def ap(self):
        raise RuntimeError("use sliced access")


class Buf:
    __slots__ = ("key", "h")

    def __init__(self, key, h):
        self.key = key
        self.h = h

    def __getitem__(self, idx):
        return self.h[idx]

    @property
    def ap(self):
        return self.h[:]


class Prog:

    def __init__(self, nc, stack):
        self.nc = nc
        self.stack = stack
        self.ops = {e: [] for e in ENGINES}
        self.count = {}
        self.state = {}
        self.waited = {e: {} for e in ENGINES}
        self.sems = {}
        self.dma_chan = {}
        self.n_dma_chans = 0
        self.n_names = 0
        self.same_engine_sync = True
        self.same_engine_window = 6
        self.pending_barrier = {}

    def _name(self, prefix):
        self.n_names += 1
        return f"{prefix}{self.n_names}"

    def sbuf(self, shape, dtype, name=None):
        stk = self.scopes[-1] if getattr(self, "scopes", None) else self.stack
        h = stk.enter_context(self.nc.sbuf_tensor(name or self._name("sb"), list(shape), dtype))
        return Buf(id(h), h)

    @contextlib.contextmanager
    def unit_scope(self):
        if not hasattr(self, "scopes"):
            self.scopes = []
        with contextlib.ExitStack() as stk:
            self.scopes.append(stk)
            try:
                yield
            finally:
                self.scopes.pop()

    def psum(self, shape, dtype, name=None):
        h = self.stack.enter_context(self.nc.psum_tensor(name or self._name("ps"), list(shape), dtype))
        return Buf(id(h), h)

    def din(self, name, shape, dtype):
        return self.nc.dram_tensor(name, list(shape), dtype, kind="ExternalInput").ap()

    def dout(self, name, shape, dtype):
        return self.nc.dram_tensor(name, list(shape), dtype, kind="ExternalOutput").ap()

    def dscratch(self, name, shape, dtype):
        return self.nc.dram_tensor(name, list(shape), dtype, kind="Internal").ap()

    def _keys(self, bufs):
        out = []
        for b in bufs:
            if b is None:
                continue
            out.append(getattr(b, "key", b) if not isinstance(b, (str, tuple)) else b)
        return out

    def _chan(self, key):
        if key not in self.dma_chan:
            self.dma_chan[key] = f"dma{len(self.dma_chan)}"
        return self.dma_chan[key]

    def call(self, eng, method, kwargs, reads=(), writes=(), dma=False, init=False):
        return self.op(eng, (method, kwargs), reads=reads, writes=writes, dma=dma, init=init)

    def op(self, eng, fn, reads=(), writes=(), dma=False, init=False):
        rk, wk = self._keys(reads), self._keys(writes)
        need = {}
        pend = self.pending_barrier.pop(eng, None)
        if pend:
            for stream, val in pend:
                if stream != eng and need.get(stream, 0) < val:
                    need[stream] = val
        for k in rk:
            st = self.state.get(k)
            if st and st[0]:
                self._need(eng, st[0], need)
        for k in wk:
            st = self.state.get(k)
            if st:
                if st[0]:
                    self._need(eng, st[0], need)
                for t in st[1]:
                    self._need(eng, t, need)
        waits = []
        wd = self.waited[eng]
        for stream, val in need.items():
            if wd.get(stream, 0) < val:
                wd[stream] = val
                waits.append((stream, val))
        if dma:
            if init:
                stream = "dinit"
            else:
                assert wk, "dma op must declare the written buffer"
                stream = self._chan(wk[0])
            val = self.count.get(stream, 0) + 16
            self.count[stream] = val
            token = (stream, val)
            incs = [(stream, 16)]
        else:
            val = self.count.get(eng, 0) + 1
            self.count[eng] = val
            token = (eng, val)
            incs = [(eng, 1)]
        self.ops[eng].append((waits, fn, incs))
        for k in rk:
            st = self.state.setdefault(k, (None, []))
            st[1].append(token)
        for k in wk:
            if dma and init:
                self.state[k] = (("dinit", self.count["dinit"]), [])
            else:
                self.state[k] = (token, [])
        return token

    def _need(self, eng, token, need):
        stream, val = token
        if stream == eng == "tensor" and not getattr(self, "pe_sync", False):
            return
        if stream == eng:
            if not self.same_engine_sync:
                return
            if self.count.get(eng, 0) - val >= self.same_engine_window:
                return
        if stream == "dinit":
            val = self.count.get("dinit", 0)
        if need.get(stream, 0) < val:
            need[stream] = val

    def _mk_dummies(self):
        if getattr(self, "_dumb", None) is None:
            self._dumb = {e: self.sbuf([1, 16], BF16, "dummy_" + e) for e in ("tensor", "vector", "scalar", "gpsimd")}
            self._dumps = self.psum([2, 8], F32, "dummyps")
            dt_ = self._dumb["tensor"]
            self._dumb_init = self.op("vector", lambda eh: eh.memset(dt_.ap, 0.0), writes=[dt_])

    def barrier(self):
        self._mk_dummies()
        participants = [e for e in ENGINES if self.ops[e]]
        if not participants:
            return
        dma_final = {}
        for k, chan in self.dma_chan.items():
            dma_final[chan] = self.count.get(chan, 0)
        if self.count.get("dinit", 0):
            dma_final["dinit"] = self.count["dinit"]
        ps = self._dumps
        for e in participants:
            wd = self.waited[e]
            waits = []
            if self.count.get(e, 0) > 0:
                waits.append((e, self.count[e]))
            if self.count.get("bar", 0) > 0:
                waits.append(("bar", self.count["bar"]))
            for chan, val in dma_final.items():
                if val > 0 and wd.get(chan, 0) < val:
                    wd[chan] = val
                    waits.append((chan, val))
            if e == "tensor":
                iv, it = self._dumb_init
                if wd.get(iv, 0) < it:
                    wd[iv] = it
                    waits.append((iv, it))
                dd = self._dumb[e]
                fn = lambda eh, dd=dd: eh.matmul(out=ps[0:1, 0:1], lhsT=dd[0:1, 0:1], rhs=dd[0:1, 0:1], start=True, stop=True)
            elif e == "vector":
                dd = self._dumb[e]
                fn = lambda eh, dd=dd: eh.memset(dd[0:1, 0:4], 0.0)
            elif e == "scalar":
                dd = self._dumb[e]
                fn = lambda eh, dd=dd: eh.memzero(dd[0:1, 0:4])
            elif e == "gpsimd":
                dd = self._dumb[e]
                fn = lambda eh, dd=dd: eh.memset(dd[0:1, 0:4], 0.0)
            else:
                fn = lambda eh: eh.nop()
            self.ops[e].append((waits, fn, [("bar", 1)]))
        self.count["bar"] = self.count.get("bar", 0) + len(participants)
        target = self.count["bar"]
        snapshot = [("bar", target)]
        for e in ENGINES:
            self.pending_barrier[e] = snapshot
        self.state = {}
        self.dma_chan = {}
        for e in ENGINES:
            self.waited[e] = dict(self.waited[e])
            self.waited[e]["bar"] = 0

    def finalize(self):
        fin = [(s, v) for s, v in self.count.items() if s != "sync" and v > 0]
        self.count["sync"] = self.count.get("sync", 0) + 1
        self.ops["sync"].append(
            ([(s, v) for (s, v) in fin], lambda eh: eh.nop(), [("sync", 1)]))
        for stream in self.count:
            self.sems[stream] = self.stack.enter_context(self.nc.semaphore("sem_" + stream))
        block = self.stack.enter_context(self.nc.Block())
        for e in ENGINES:
            if not self.ops[e]:
                continue
            ops = self.ops[e]
            sems = self.sems

            def body(engh, ops=ops, sems=sems):
                for waits, fn, incs in ops:
                    for stream, val in waits:
                        engh.wait_ge(sems[stream], val)
                    if isinstance(fn, tuple):
                        inst = getattr(engh, fn[0])(**fn[1])
                    else:
                        inst = fn(engh)
                    for stream, n in incs:
                        inst.then_inc(sems[stream], n)
            getattr(block, e)(body)



def host_constants():
    c = {}
    c["identity"] = np.eye(128, dtype=np.float16)
    c["ones128"] = np.ones((128, 128), dtype=BF)
    c["ones128f"] = np.ones((128, 128), dtype=np.float32)
    c["negones_div"] = np.full((128, 128), -1.0 / 128, dtype=BF)
    c["ones1"] = np.ones((1, 128), dtype=np.float32)
    c["ones1b"] = np.ones((1, 128), dtype=BF)
    misc = np.zeros((128, 8), dtype=np.float32)
    misc[:, 0] = 1e-5
    misc[:, 1] = 1e-24
    misc[:, 2] = 1e-6
    misc[:, 3] = 0.5
    misc[:, 4] = -0.5
    misc[:, 5] = -50.0
    misc[:, 6] = float(40 * np.log(2.0))
    misc[:, 7] = 1.0
    c["misc"] = misc
    c["identityf"] = np.eye(128, dtype=np.float32)
    selg = np.zeros((128, 16 * 128), dtype=BF)
    sumsel = np.zeros((128, 16 * 128), dtype=BF)
    for g in range(16):
        for s in range(16):
            for di in range(8):
                selg[g * 8 + di, g * 128 + s * 8 + di] = 1
                sumsel[s * 8 + di, g * 128 + g * 8 + di] = 1
    c["L3a_selg"] = selg
    c["L3a_sumsel"] = sumsel
    selbc = np.zeros((33, 3 * 128), dtype=BF)
    for mcol in range(128):
        selbc[32, 0 + mcol] = 1
        selbc[mcol // 8, 128 + mcol] = 1
        selbc[16 + mcol // 8, 256 + mcol] = 1
    c["L3a_selbc"] = selbc
    selB16 = np.zeros((16, 128), dtype=BF)
    for mcol in range(128):
        selB16[mcol // 8, mcol] = 1
    c["selB16"] = selB16
    m = np.zeros((128, 4 * 512), dtype=BF)
    kk = np.arange(128)[:, None]
    qq = np.arange(512)[None, :]
    for dq in range(4):
        blk = (kk + dq * 128 <= qq).astype(BF)
        m[:, dq * 512:(dq + 1) * 512] = blk
    c["causal01"] = m
    mf = np.zeros((128, 4 * 512), dtype=BF)
    for blk in range(4):
        mf[:, blk * 512:(blk + 1) * 512] = (kk + blk * 128 >= qq).astype(BF)
    c["causal01_far"] = mf
    ca = np.where(np.concatenate([(kk + blk * 128 <= qq) for blk in range(4)], axis=1), 0.0, -1e9).astype(BF)
    cfa = np.where(np.concatenate([(kk + blk * 128 >= qq) for blk in range(4)], axis=1), 0.0, -1e9).astype(BF)
    c["causaladd"] = ca
    c["causaladd_far"] = cfa
    return c



class MS:
    pass


LAZY_CONSTS = {"causal01", "causal01_far", "causaladd", "causaladd_far", "L3a_selg", "L3a_sumsel", "L3a_selbc"}


def load_lazy_const(p, ms, name):
    arr = ms.cdict[name]
    dt = {np.dtype(np.float32): F32, np.dtype(BF): BF16, np.dtype(np.int32): I32, np.dtype(np.float16): F16}[arr.dtype]
    dram = ms.c_drams[name]
    b = p.sbuf(arr.shape, dt)
    p.call("sync", "dma_start", dict(out=b.ap, in_=dram), writes=[b], dma=True, init=True)
    return b


def emit_load_consts(p, ms, consts):
    nc = p.nc
    ms.c = {}
    ms.cdict = consts
    ms.c_drams = {}
    for name, arr in consts.items():
        dt = {np.dtype(np.float32): F32, np.dtype(BF): BF16, np.dtype(np.int32): I32, np.dtype(np.float16): F16}[arr.dtype]
        dram = p.din("c_" + name, arr.shape, dt)
        ms.c_drams[name] = dram
        if name in LAZY_CONSTS:
            continue
        b = p.sbuf(arr.shape, dt, "cs_" + name)
        p.call("sync", "dma_start", dict(out=b.ap, in_=dram), reads=[], writes=[b], dma=True, init=True)
        ms.c[name] = b


def emit_prologue(p, ms, xin):
    for t in ms.x_cm:
        p.call("vector", "memset", dict(ap=t[:, 0:PAD], constant=0.0), writes=[t.key])
    xtm = p.sbuf([128, D], F16, "prx")
    xf = p.sbuf([128, D], F32, "prxf")
    ident = ms.c["identity"]
    ptt = ms.pbb
    for i in range(NT):
        p.call("sync", "dma_start", dict(out=xf.ap, in_=xin[i * 128:(i + 1) * 128, :]),
               writes=[xf], dma=True)
        p.call("scalar", "copy", dict(out=xtm.ap, in_=xf.ap), reads=[xf], writes=[xtm])
        for j in range(ND):
            p.call("tensor", "transpose", dict(
                out=ptt[:, j * 128:(j + 1) * 128], in_=xtm[:, j * 128:(j + 1) * 128],
                identity=ident.ap), reads=[xtm, ident], writes=[ptt])
        for j in range(ND):
            p.call("vector", "tensor_copy", dict(
                out=ms.x_cm[j][:, PAD + i * 128:PAD + (i + 1) * 128],
                in_=ptt[:, j * 128:(j + 1) * 128]),
                reads=[ptt], writes=[ms.x_cm[j].key])


def emit_epilogue(p, ms, xout):
    ident = ms.c["identity"]
    ptt = ms.pbb
    otile = p.sbuf([128, D], F32, "epo")
    for i in range(NT):
        for j in range(ND):
            p.call("tensor", "transpose", dict(
                out=ptt[:, j * 128:(j + 1) * 128],
                in_=ms.x_cm[j][:, PAD + i * 128:PAD + (i + 1) * 128], identity=ident.ap),
                reads=[ms.x_cm[j].key, ident], writes=[ptt])
        p.call("scalar", "copy", dict(out=otile.ap, in_=ptt.ap), reads=[ptt], writes=[otile])
        p.call("gpsimd", "dma_start", dict(out=xout[i * 128:(i + 1) * 128, :], in_=otile.ap),
               reads=[otile], writes=["xout"], dma=True)



def mk_ln_bufs(p, ms):
    if getattr(ms, "ln_r", None) is None:
        ms.ln_r = [p.sbuf([128, TC], F16, f"lnr{j}") for j in range(ND)]
        ms.ln_rsq = [p.sbuf([128, TC], BF16, "lnrsq0"), p.sbuf([128, TC], BF16, "lnrsq1")]
        ms.ln_m = p.sbuf([128, TC], F16, "lnm")
        ms.ln_msq = p.sbuf([128, TC], F32, "lnmsq")
        ms.ln_var = p.sbuf([128, TC], F32, "lnvar")
        ms.ln_rstd = p.sbuf([128, TC], F32, "lnrstd")
        ms.ps_s = ms.pb[0]
        ms.ps_q = ms.pb[1]
        ms.ps_rc = [ms.pb[2], ms.pb[3]]


def ln_chunk(p, ms, c0, w, subs, skip_sub=False):
    mk_ln_bufs(p, ms)
    ones = ms.c["ones128"]
    ident = ms.c["identity"]
    ndiv = ms.c["negones_div"]
    misc = ms.c["misc"]
    for j in range(ND):
        xs = ms.x_cm[j]
        rj = ms.ln_r[j]
        if skip_sub:
            p.call("vector", "tensor_copy", dict(
                out=rj[:, 0:w], in_=xs[:, PAD + c0:PAD + c0 + w]),
                reads=[xs.key], writes=[rj])
        else:
            sap, sdep = subs[j]
            p.call("vector", "tensor_tensor", dict(
                out=rj[:, 0:w], in0=xs[:, PAD + c0:PAD + c0 + w], in1=sap,
                op=ALU.add), reads=[xs.key, sdep], writes=[rj])
    for j in range(ND):
        rq = ms.ln_rsq[j % 2]
        p.call("scalar", "activation", dict(
            out=rq[:, 0:w], in_=ms.ln_r[j][:, 0:w], func=AF.Square),
            reads=[ms.ln_r[j]], writes=[rq])
        p.call("tensor", "matmul", dict(
            out=ms.ps_s[:, 0:w], lhsT=ones.ap, rhs=ms.ln_r[j][:, 0:w],
            start=(j == 0), stop=(j == ND - 1)),
            reads=[ms.ln_r[j], ones], writes=[ms.ps_s])
        p.call("tensor", "matmul", dict(
            out=ms.ps_q[:, 0:w], lhsT=ones.ap, rhs=rq[:, 0:w],
            start=(j == 0), stop=(j == ND - 1)),
            reads=[rq, ones], writes=[ms.ps_q])
    p.call("vector", "tensor_scalar_mul", dict(
        out=ms.ln_m[:, 0:w], in0=ms.ps_s[:, 0:w], scalar1=1.0 / D),
        reads=[ms.ps_s], writes=[ms.ln_m])
    p.call("vector", "tensor_tensor", dict(
        out=ms.ln_msq[:, 0:w], in0=ms.ln_m[:, 0:w], in1=ms.ln_m[:, 0:w], op=ALU.mult),
        reads=[ms.ln_m], writes=[ms.ln_msq])
    p.call("vector", "scalar_tensor_tensor", dict(
        out=ms.ln_var[:, 0:w], in0=ms.ps_q[:, 0:w], scalar=1.0 / D, in1=ms.ln_msq[:, 0:w],
        op0=ALU.mult, op1=ALU.subtract),
        reads=[ms.ps_q, ms.ln_msq], writes=[ms.ln_var])
    p.call("scalar", "activation", dict(
        out=ms.ln_var[:, 0:w], in_=ms.ln_var[:, 0:w], func=AF.Ln, bias=misc[:, 0:1]),
        reads=[ms.ln_var, misc], writes=[ms.ln_var])
    p.call("scalar", "activation", dict(
        out=ms.ln_rstd[:, 0:w], in_=ms.ln_var[:, 0:w], func=AF.Exp, scale=-0.5),
        reads=[ms.ln_var], writes=[ms.ln_rstd])
    for j in range(ND):
        rc = ms.ps_rc[j % 2]
        p.call("tensor", "matmul", dict(
            out=rc[:, 0:w], lhsT=ident.ap, rhs=ms.ln_r[j][:, 0:w], start=True, stop=False),
            reads=[ms.ln_r[j], ident], writes=[rc])
        p.call("tensor", "matmul", dict(
            out=rc[:, 0:w], lhsT=ndiv.ap, rhs=ms.ln_m[:, 0:w], start=False, stop=True),
            reads=[ms.ln_m, ndiv], writes=[rc])
        p.call("vector", "tensor_tensor", dict(
            out=ms.x_cm[j][:, PAD + c0:PAD + c0 + w], in0=rc[:, 0:w],
            in1=ms.ln_rstd[:, 0:w], op=ALU.mult),
            reads=[rc, ms.ln_rstd], writes=[ms.x_cm[j].key])



FP8 = mybir.dt.float8e4


def quantize_weight(p, ms, wT_dram, K, N, qtiles, wstage, ab):
    nk = K // 128
    NH = min(N, 512)
    nh = N // NH
    acc = p.sbuf([128, nk * nh], F32, p._name("qacc"))
    inv = p.sbuf([128, 1], F32, p._name("qinv"))
    tot = p.sbuf([128, 1], F32, p._name("qtot"))
    pst = ms.pb[5]
    for k in range(nk):
        for h in range(nh):
            st = wstage[(k * nh + h) % 2]
            p.call("sync", "dma_start", dict(
                out=st[:, 0:NH], in_=wT_dram[k * 128:(k + 1) * 128, h * NH:(h + 1) * NH]),
                writes=[st], dma=True)
            p.call("vector", "tensor_reduce", dict(
                out=acc[:, k * nh + h:k * nh + h + 1], in_=st[:, 0:NH],
                axis=mybir.AxisListType.X, op=ALU.add, apply_absolute_value=True),
                reads=[st], writes=[(acc.key, k * nh + h)])
    p.call("vector", "tensor_reduce", dict(
        out=tot.ap, in_=acc[:, 0:nk * nh], axis=mybir.AxisListType.X, op=ALU.add),
        reads=[acc] + [(acc.key, i) for i in range(nk * nh)], writes=[tot])
    p.call("tensor", "matmul", dict(out=pst[:, 0:1], lhsT=ms.c["ones128f"].ap, rhs=tot.ap,
                                    start=True, stop=True),
           reads=[tot, ms.c["ones128f"]], writes=[pst])
    p.call("vector", "tensor_scalar", dict(
        out=inv.ap, in0=pst[:, 0:1], scalar1=1.0 / (K * N), scalar2=1e-5, op0=ALU.mult, op1=ALU.add),
        reads=[pst], writes=[inv])
    p.call("vector", "reciprocal", dict(out=inv.ap, in_=inv.ap), reads=[inv], writes=[inv])
    for k in range(nk):
        for h in range(nh):
            st = wstage[(k * nh + h) % 2]
            p.call("sync", "dma_start", dict(
                out=st[:, 0:NH], in_=wT_dram[k * 128:(k + 1) * 128, h * NH:(h + 1) * NH]),
                writes=[st], dma=True)
            p.call("vector", "tensor_scalar", dict(
                out=ab[0][:, 0:NH], in0=st[:, 0:NH], scalar1=inv.ap, scalar2=0.5,
                op0=ALU.mult, op1=ALU.is_ge),
                reads=[st, inv], writes=[ab[0]])
            p.call("vector", "tensor_scalar", dict(
                out=ab[1][:, 0:NH], in0=st[:, 0:NH], scalar1=inv.ap, scalar2=-0.5,
                op0=ALU.mult, op1=ALU.is_le),
                reads=[st, inv], writes=[ab[1]])
            p.call("vector", "tensor_tensor", dict(
                out=qtiles[k][:, h * NH:(h + 1) * NH], in0=ab[0][:, 0:NH],
                in1=ab[1][:, 0:NH], op=ALU.subtract),
                reads=[ab[0], ab[1]], writes=[qtiles[k]])


def emit_l0_ffn(p, ms):
    FD = 4096
    mk_ln_bufs(p, ms)
    w1q = [p.sbuf([128, FD], FP8, f"w1q{k}") for k in range(ND)]
    w2q = [p.sbuf([128, D], FP8, f"w2q{k}") for k in range(FD // 128)]
    wstage = [p.sbuf([128, 512], F32, "l0fst0"), p.sbuf([128, 512], F32, "l0fst1")]
    w1T = p.din("L0_ffn_W1T", [D, FD], F32)
    w2T = p.din("L0_ffn_W2T", [FD, D], F32)
    b1 = ms.c["L0f_b1"]
    b2 = ms.c["L0f_b2"]
    CW = 512
    hbuf = p.sbuf([128, (FD // 128) * 512], BF16, "l0fh")
    ab = [p.sbuf([128, 512], BF16, "l0fab0"), p.sbuf([128, 512], BF16, "l0fab1")]
    quantize_weight(p, ms, w1T, D, FD, w1q, wstage, ab)
    quantize_weight(p, ms, w2T, FD, D, w2q, wstage, ab)
    xbf = [p.sbuf([128, CW], BF16, f"l0fxb{j}") for j in range(ND)]
    psh = [ms.pb[2], ms.pb[3], ms.pb[4]]
    pso = [ms.pb[5], ms.pb[0], ms.pb[1]]
    sub = [p.sbuf([128, CW], F16, f"l0fs{j}") for j in range(ND)]
    for c0 in range(0, T, CW):
        for k in range(ND):
            p.call("scalar", "copy", dict(out=xbf[k].ap, in_=ms.x_cm[k][:, PAD + c0:PAD + c0 + CW]),
                   reads=[ms.x_cm[k].key], writes=[xbf[k]])
        for ft in range(FD // 128):
            ph = psh[ft % 3]
            for k in range(ND):
                p.call("tensor", "matmul", dict(
                    out=ph[:, 0:CW], lhsT=w1q[k][:, ft * 128:(ft + 1) * 128],
                    rhs=xbf[k].ap,
                    start=(k == 0), stop=(k == ND - 1)),
                    reads=[w1q[k], xbf[k]], writes=[ph])
            p.call("scalar", "activation", dict(
                out=hbuf[:, ft * CW:(ft + 1) * CW], in_=ph[:, 0:CW], func=AF.Relu,
                bias=b1[:, ft:ft + 1]),
                reads=[ph, b1], writes=[(hbuf.key, ft)])
        for j in range(ND):
            po = pso[j % 3]
            for k in range(FD // 128):
                p.call("tensor", "matmul", dict(
                    out=po[:, 0:CW], lhsT=w2q[k][:, j * 128:(j + 1) * 128],
                    rhs=hbuf[:, k * CW:(k + 1) * CW],
                    start=(k == 0), stop=(k == FD // 128 - 1)),
                    reads=[w2q[k], (hbuf.key, k)], writes=[po])
            p.call("vector", "tensor_scalar", dict(
                out=sub[j].ap, in0=po[:, 0:CW], scalar1=b2[:, j:j + 1], scalar2=None,
                op0=ALU.add), reads=[po, b2], writes=[sub[j]])
        ln_chunk(p, ms, c0, CW, [(sub[j][:, 0:CW], sub[j]) for j in range(ND)])



def load_weight_bf16(p, ms, dram, K, N, tiles, wstage, tag):
    nk = K // 128
    for k in range(nk):
        st = wstage[k % 2]
        p.call("sync", "dma_start", dict(out=st[:, 0:N], in_=dram[k * 128:(k + 1) * 128, :]),
               writes=[st], dma=True)
        p.call("scalar", "copy", dict(out=tiles[k].ap, in_=st[:, 0:N]),
               reads=[st], writes=[tiles[k]])


def emit_l1_ffn(p, ms):
    wrT = ms.w16["L1_ffn_WrT"]
    wkT = ms.w16["L1_ffn_WkT"]
    wvT = ms.w16["L1_ffn_WvT"]
    mur = ms.c["L1f_mur"]
    muk = ms.c["L1f_muk"]
    wr = [p.sbuf([128, D], F16, f"l1fwr{k}") for k in range(ND)]
    wk = [p.sbuf([128, D], F16, f"l1fwk{k}") for k in range(ND)]
    wv = [p.sbuf([128, D], F16, f"l1fwv{k}") for k in range(ND)]
    for tiles, srcw, srck in ((wr, wrT, "w16_L1_ffn_WrT"), (wk, wkT, "w16_L1_ffn_WkT"),
                              (wv, wvT, "w16_L1_ffn_WvT")):
        for k in range(ND):
            p.call("sync", "dma_start", dict(out=tiles[k].ap, in_=srcw[k * 128:(k + 1) * 128, :]),
                   reads=[srck], writes=[tiles[k]], dma=True, init=True)
    CW = TC
    dbuf = p.sbuf([128, CW], F32, "l1fd")
    xr = [p.sbuf([128, CW], F16, f"l1fxr{j}") for j in range(ND)]
    xk = [p.sbuf([128, CW], F16, f"l1fxk{j}") for j in range(ND)]
    kact = [p.sbuf([128, CW], F16, f"l1fka{j}") for j in range(ND)]
    rbuf = [p.sbuf([128, CW], F16, f"l1fr{j}") for j in range(ND)]
    sub = [p.sbuf([128, CW], F16, f"l1fs{j}") for j in range(ND)]
    ps = [ms.pb[4], ms.pb[5], ms.pb[0]]
    xbnd = [p.sbuf([128, 8], F16, f"l1f_xb{j}") for j in range(ND)]
    for j in range(ND):
        for bi, b0 in enumerate(range(CW, T, CW)):
            p.call("vector", "tensor_copy", dict(
                out=xbnd[j][:, bi:bi + 1], in_=ms.x_cm[j][:, PAD + b0 - 1:PAD + b0]),
                reads=[ms.x_cm[j].key], writes=[xbnd[j]])
    prevb = p.sbuf([128, CW], F16, "l1f_prev")
    for c0 in range(0, T, CW):
        for j in range(ND):
            xs = ms.x_cm[j]
            cur = xs[:, PAD + c0:PAD + c0 + CW]
            p.call("vector", "tensor_copy", dict(
                out=prevb.ap, in_=xs[:, PAD + c0 - 1:PAD + c0 - 1 + CW]),
                reads=[xs.key], writes=[prevb])
            if c0 > 0:
                p.call("vector", "tensor_copy", dict(
                    out=prevb[:, 0:1], in_=xbnd[j][:, c0 // CW - 1:c0 // CW]),
                    reads=[xbnd[j]], writes=[prevb])
            prev = prevb.ap
            p.call("vector", "tensor_tensor", dict(
                out=dbuf.ap, in0=cur, in1=prev, op=ALU.subtract),
                reads=[xs.key, prevb], writes=[dbuf])
            p.call("vector", "scalar_tensor_tensor", dict(
                out=xr[j].ap, in0=dbuf.ap, scalar=mur[:, j:j + 1], in1=prev,
                op0=ALU.mult, op1=ALU.add),
                reads=[dbuf, mur, prevb], writes=[xr[j]])
            p.call("vector", "scalar_tensor_tensor", dict(
                out=xk[j].ap, in0=dbuf.ap, scalar=muk[:, j:j + 1], in1=prev,
                op0=ALU.mult, op1=ALU.add),
                reads=[dbuf, muk, prevb], writes=[xk[j]])
        for j in range(ND):
            po = ps[j % 3]
            for k in range(ND):
                p.call("tensor", "matmul", dict(
                    out=po[:, 0:CW], lhsT=wk[k][:, j * 128:(j + 1) * 128], rhs=xk[k].ap,
                    start=(k == 0), stop=(k == ND - 1)),
                    reads=[wk[k], xk[k]], writes=[po])
            p.call("scalar", "activation", dict(
                out=kact[j].ap, in_=po[:, 0:CW], func=AF.Relu),
                reads=[po], writes=[kact[j]])
            p.call("scalar", "activation", dict(
                out=kact[j].ap, in_=kact[j].ap, func=AF.Square),
                reads=[kact[j]], writes=[kact[j]])
        for j in range(ND):
            po = ps[j % 3]
            for k in range(ND):
                p.call("tensor", "matmul", dict(
                    out=po[:, 0:CW], lhsT=wr[k][:, j * 128:(j + 1) * 128], rhs=xr[k].ap,
                    start=(k == 0), stop=(k == ND - 1)),
                    reads=[wr[k], xr[k]], writes=[po])
            p.call("scalar", "activation", dict(
                out=rbuf[j].ap, in_=po[:, 0:CW], func=AF.Sigmoid),
                reads=[po], writes=[rbuf[j]])
        for j in range(ND):
            po = ps[j % 3]
            for k in range(ND):
                p.call("tensor", "matmul", dict(
                    out=po[:, 0:CW], lhsT=wv[k][:, j * 128:(j + 1) * 128], rhs=kact[k].ap,
                    start=(k == 0), stop=(k == ND - 1)),
                    reads=[wv[k], kact[k]], writes=[po])
            p.call("vector", "tensor_tensor", dict(
                out=sub[j].ap, in0=po[:, 0:CW], in1=rbuf[j].ap, op=ALU.mult),
                reads=[po, rbuf[j]], writes=[sub[j]])
        ln_chunk(p, ms, c0, CW, [(sub[j].ap, sub[j]) for j in range(ND)])



def emit_l1_attn(p, ms):
    H, HD = 16, 64
    QC = 512
    NKT = NT
    att = [p.sbuf([128, T], F16, f"l1a_att{j}") for j in range(ND)]
    ptb = p.sbuf([128, NKT * 256], BF16, "l1a_pt")
    vau = [p.sbuf([128, NKT * 128], F16, "l1a_vau0"), p.sbuf([128, NKT * 128], F16, "l1a_vau1")]
    lnn = p.sbuf([64, QC], F32, "l1a_lnn")
    recip = p.sbuf([64, QC], F32, "l1a_recip")
    ident = ms.c["identity"]
    misc = ms.c["misc"]
    mask = load_lazy_const(p, ms, "causal01")
    psS = [ms.pb[4], ms.pb[5], ms.pb[0]]
    psA = [ms.pb[2], ms.pb[3]]
    LN2_40 = float(40 * np.log(2.0))
    for jh in range(ND):
        for v in vau:
            p.call("vector", "memset", dict(ap=v.ap, constant=1.0), writes=[v])
        for i in range(NT):
            p.call("tensor", "transpose", dict(
                out=ms.pbb[:, 0:128], in_=ms.x_cm[jh][:, PAD + i * 128:PAD + (i + 1) * 128],
                identity=ident.ap), reads=[ms.x_cm[jh].key, ident], writes=[ms.pbb])
            for hp in range(2):
                p.call("vector", "tensor_copy", dict(
                    out=vau[hp][:, i * 128:i * 128 + 64],
                    in_=ms.pbb[:, hp * 64:hp * 64 + 64]),
                    reads=[ms.pbb], writes=[vau[hp]])
        for hp in range(2):
            h = 2 * jh + hp
            ro = hp * 64
            xh = ms.x_cm[jh]
            for q0 in range(0, T, QC):
                nkt = q0 // 128 + QC // 128
                pA = psA[(q0 // QC) % 2]
                for kh0 in range(0, nkt, 16):
                    kts = list(range(kh0, min(kh0 + 16, nkt)))
                    for kt in kts:
                        ki = kt - kh0
                        pS = psS[ki % 3]
                        p.call("tensor", "matmul", dict(
                            out=pS[:, 0:QC], lhsT=xh[ro:ro + 64, PAD + kt * 128:PAD + (kt + 1) * 128],
                            rhs=xh[ro:ro + 64, PAD + q0:PAD + q0 + QC], start=True, stop=True),
                            reads=[xh.key], writes=[pS])
                        p.call("scalar", "activation", dict(
                            out=ptb[:, ki * QC:(ki + 1) * QC], in_=pS[:, 0:QC], func=AF.Exp,
                            scale=0.125, bias=misc[:, 5:6]),
                            reads=[pS, misc], writes=[(ptb.key, ki)])
                        dq = kt * 128 - q0
                        if 0 <= dq < QC:
                            p.call("vector", "tensor_tensor", dict(
                                out=ptb[:, ki * QC:(ki + 1) * QC],
                                in0=ptb[:, ki * QC:(ki + 1) * QC],
                                in1=mask[:, (dq // 128) * 512:(dq // 128) * 512 + QC],
                                op=ALU.mult),
                                reads=[(ptb.key, ki), mask], writes=[(ptb.key, ki)])
                    for kt in kts:
                        ki = kt - kh0
                        p.call("tensor", "matmul", dict(
                            out=pA[:, 0:QC], lhsT=vau[hp][:, kt * 128:(kt + 1) * 128],
                            rhs=ptb[:, ki * QC:(ki + 1) * QC],
                            start=(kt == 0), stop=(kt == nkt - 1)),
                            reads=[vau[hp], (ptb.key, ki)], writes=[pA])
                p.call("scalar", "activation", dict(
                    out=lnn.ap, in_=pA[64:128, 0:QC], func=AF.Ln, scale=float(2.0 ** 40)),
                    reads=[pA], writes=[lnn])
                p.call("scalar", "activation", dict(
                    out=recip.ap, in_=lnn.ap, func=AF.Exp, scale=-1.0, bias=misc[0:64, 6:7]),
                    reads=[lnn, misc], writes=[recip])
                p.call("vector", "tensor_tensor", dict(
                    out=att[jh][ro:ro + 64, q0:q0 + QC], in0=pA[0:64, 0:QC],
                    in1=recip.ap, op=ALU.mult),
                    reads=[pA, recip], writes=[att[jh].key])
    for c0 in range(0, T, TC):
        ln_chunk(p, ms, c0, TC, [(att[j][:, c0:c0 + TC], att[j].key) for j in range(ND)])



def emit_l2_attn(p, ms):
    H, HD = 16, 64
    QC = 512
    wq = ms.w16["L2_attn_Wre"]
    wqk = "w16_L2_attn_Wre"
    att = [p.sbuf([128, T], F16, f"l2a_att{j}") for j in range(ND)]
    ptb = p.sbuf([128, 16 * QC], BF16, "l2a_pt")
    qbuf = p.sbuf([128, T], F16, "l2a_q")
    kbuf = p.sbuf([128, T], F16, "l2a_k")
    vau = p.sbuf([128, NT * 128], F16, "l2a_vau")
    wst = p.sbuf([128, 256], F32, "l2a_wst")
    wpair = [p.sbuf([128, 256], F16, f"l2a_wp{k}") for k in range(ND)]
    wvh = [p.sbuf([128, 64], F16, f"l2a_wv{k}") for k in range(ND)]
    lnn = p.sbuf([64, QC], F32, "l2a_lnn")
    recip = p.sbuf([64, QC], F32, "l2a_recip")
    misc = ms.c["misc"]
    psS = [ms.pb[4], ms.pb[5], ms.pb[1]]
    psA = [ms.pb[2], ms.pb[3]]
    psV = ms.pb[0]
    for pr in range(H // 2):
        for k in range(ND):
            p.call("sync", "dma_start", dict(
                out=wpair[k][:, 0:128], in_=wq[k * 128:(k + 1) * 128, pr * 128:(pr + 1) * 128]),
                reads=[wqk], writes=[(wpair[k].key, 0)], dma=True)
            p.call("sync", "dma_start", dict(
                out=wpair[k][:, 128:256], in_=wq[k * 128:(k + 1) * 128, D + pr * 128:D + (pr + 1) * 128]),
                reads=[wqk], writes=[(wpair[k].key, 1)], dma=True)
        for c0 in range(0, T, TC):
            for dst, co in ((qbuf, 0), (kbuf, 128)):
                ps = psS[co // 128]
                for k in range(ND):
                    p.call("tensor", "matmul", dict(
                        out=ps[:, 0:TC], lhsT=wpair[k][:, co:co + 128],
                        rhs=ms.x_cm[k][:, PAD + c0:PAD + c0 + TC],
                        start=(k == 0), stop=(k == ND - 1)),
                        reads=[(wpair[k].key, co // 128), ms.x_cm[k].key], writes=[ps])
                p.call("scalar", "copy", dict(out=dst[:, c0:c0 + TC], in_=ps[:, 0:TC]),
                       reads=[ps], writes=[dst.key])
        for hp in range(2):
            h = 2 * pr + hp
            ro = hp * 64
            p.call("vector", "memset", dict(ap=vau.ap, constant=1.0), writes=[vau])
            for k in range(ND):
                p.call("sync", "dma_start", dict(
                    out=wvh[k].ap, in_=wq[k * 128:(k + 1) * 128, 2 * D + h * 64:2 * D + (h + 1) * 64]),
                    reads=[wqk], writes=[wvh[k]], dma=True)
            for i in range(NT):
                for k in range(ND):
                    p.call("tensor", "matmul", dict(
                        out=psV[:, 0:64], lhsT=ms.x_cm[k][:, PAD + i * 128:PAD + (i + 1) * 128],
                        rhs=wvh[k].ap, start=(k == 0), stop=(k == ND - 1)),
                        reads=[ms.x_cm[k].key, wvh[k]], writes=[psV])
                p.call("vector", "tensor_copy", dict(
                    out=vau[:, i * 128:i * 128 + 64], in_=psV[:, 0:64]),
                    reads=[psV], writes=[vau])
            for q0 in range(0, T, QC):
                pA = psA[(q0 // QC) % 2]
                for kh in range(2):
                    for kti in range(16):
                        kt = kh * 16 + kti
                        pS = psS[kti % 3]
                        p.call("tensor", "matmul", dict(
                            out=pS[:, 0:QC], lhsT=kbuf[ro:ro + 64, kt * 128:(kt + 1) * 128],
                            rhs=qbuf[ro:ro + 64, q0:q0 + QC], start=True, stop=True),
                            reads=[kbuf.key], writes=[pS])
                        p.call("scalar", "activation", dict(
                            out=ptb[:, kti * QC:(kti + 1) * QC], in_=pS[:, 0:QC], func=AF.Exp,
                            scale=0.125, bias=misc[:, 5:6]),
                            reads=[pS, misc], writes=[(ptb.key, kti)])
                    for kti in range(16):
                        kt = kh * 16 + kti
                        p.call("tensor", "matmul", dict(
                            out=pA[:, 0:QC], lhsT=vau[:, kt * 128:(kt + 1) * 128],
                            rhs=ptb[:, kti * QC:(kti + 1) * QC],
                            start=(kt == 0), stop=(kt == NT - 1)),
                            reads=[vau, (ptb.key, kti)], writes=[pA])
                p.call("scalar", "activation", dict(
                    out=lnn.ap, in_=pA[64:128, 0:QC], func=AF.Ln, scale=float(2.0 ** 40)),
                    reads=[pA], writes=[lnn])
                p.call("scalar", "activation", dict(
                    out=recip.ap, in_=lnn.ap, func=AF.Exp, scale=-1.0, bias=misc[0:64, 6:7]),
                    reads=[lnn, misc], writes=[recip])
                p.call("vector", "tensor_tensor", dict(
                    out=att[pr][ro:ro + 64, q0:q0 + QC], in0=pA[0:64, 0:QC],
                    in1=recip.ap, op=ALU.mult),
                    reads=[pA, recip], writes=[att[pr].key])
    for c0 in range(0, T, TC):
        ln_chunk(p, ms, c0, TC, [(att[j][:, c0:c0 + TC], att[j].key) for j in range(ND)])



def emit_l3_ffn(p, ms):
    HID = 3072
    S127 = 1.0 / 127.0
    wgT = ms.w16["L3_ffn_wgT"]
    wuT = ms.w16["L3_ffn_wuT"]
    wdT = ms.w16["L3_ffn_wdT"]
    hscr = p.dscratch("l3f_h", [HID, T], F16)
    wg2 = [[p.sbuf([128, 128], F16, f"l3f_wg{par}_{k}") for k in range(ND)] for par in range(2)]
    wu2 = [[p.sbuf([128, 128], F16, f"l3f_wu{par}_{k}") for k in range(ND)] for par in range(2)]
    gb = p.sbuf([128, TC], F16, "l3f_g")
    ub = p.sbuf([128, TC], F16, "l3f_u")
    hb = [p.sbuf([128, TC], F16, "l3f_h0"), p.sbuf([128, TC], F16, "l3f_h1")]
    psGs, psUs = [ms.pb[4], ms.pb[2]], [ms.pb[5], ms.pb[3]]
    for m in range(HID // 128):
        wg = wg2[m % 2]
        wu = wu2[m % 2]
        for k in range(ND):
            p.call("sync", "dma_start", dict(
                out=wg[k].ap, in_=wgT[k * 128:(k + 1) * 128, m * 128:(m + 1) * 128]),
                reads=["w16_L3_ffn_wgT"], writes=[wg[k]], dma=True)
            p.call("sync", "dma_start", dict(
                out=wu[k].ap, in_=wuT[k * 128:(k + 1) * 128, m * 128:(m + 1) * 128]),
                reads=["w16_L3_ffn_wuT"], writes=[wu[k]], dma=True)
        for c0 in range(0, T, TC):
            psG, psU = psGs[(c0 // TC) % 2], psUs[(c0 // TC) % 2]
            for k in range(ND):
                p.call("tensor", "matmul", dict(
                    out=psG[:, 0:TC], lhsT=wg[k].ap, rhs=ms.x_cm[k][:, PAD + c0:PAD + c0 + TC],
                    start=(k == 0), stop=(k == ND - 1)),
                    reads=[wg[k], ms.x_cm[k].key], writes=[psG])
            for k in range(ND):
                p.call("tensor", "matmul", dict(
                    out=psU[:, 0:TC], lhsT=wu[k].ap, rhs=ms.x_cm[k][:, PAD + c0:PAD + c0 + TC],
                    start=(k == 0), stop=(k == ND - 1)),
                    reads=[wu[k], ms.x_cm[k].key], writes=[psU])
            p.call("scalar", "activation", dict(
                out=gb.ap, in_=psG[:, 0:TC], func=AF.Silu, scale=S127),
                reads=[psG], writes=[gb])
            p.call("scalar", "activation", dict(
                out=ub.ap, in_=psU[:, 0:TC], func=AF.Copy, scale=S127),
                reads=[psU], writes=[ub])
            h = hb[(c0 // TC) % 2]
            p.call("vector", "tensor_tensor", dict(
                out=h.ap, in0=gb.ap, in1=ub.ap, op=ALU.mult),
                reads=[gb, ub], writes=[h])
            p.call("gpsimd", "dma_start", dict(
                out=hscr[m * 128:(m + 1) * 128, c0:c0 + TC], in_=h.ap),
                reads=[h], writes=["l3f_h"], dma=True)
    wd = [p.sbuf([128, D], F16, f"l3f_wd{k}") for k in range(HID // 128)]
    for k in range(HID // 128):
        p.call("sync", "dma_start", dict(
            out=wd[k].ap, in_=wdT[k * 128:(k + 1) * 128, :]),
            reads=["w16_L3_ffn_wdT"], writes=[wd[k]], dma=True, init=True)
    hbuf = p.sbuf([128, (HID // 128) * TC], F16, "l3f_hb")
    sub = [p.sbuf([128, TC], F16, f"l3f_s{j}") for j in range(ND)]
    pso = [ms.pb[2], ms.pb[3], ms.pb[4]]
    for c0 in range(0, T, TC):
        for k in range(HID // 128):
            p.call("sync", "dma_start", dict(
                out=hbuf[:, k * TC:(k + 1) * TC], in_=hscr[k * 128:(k + 1) * 128, c0:c0 + TC]),
                reads=["l3f_h"], writes=[hbuf], dma=True)
        for j in range(ND):
            po = pso[j % 3]
            for k in range(HID // 128):
                p.call("tensor", "matmul", dict(
                    out=po[:, 0:TC], lhsT=wd[k][:, j * 128:(j + 1) * 128],
                    rhs=hbuf[:, k * TC:(k + 1) * TC],
                    start=(k == 0), stop=(k == HID // 128 - 1)),
                    reads=[wd[k], hbuf], writes=[po])
            p.call("scalar", "activation", dict(
                out=sub[j].ap, in_=po[:, 0:TC], func=AF.Copy, scale=S127),
                reads=[po], writes=[sub[j]])
        ln_chunk(p, ms, c0, TC, [(sub[j].ap, sub[j]) for j in range(ND)])



def _elu_p1(p, ms, ps_in, cols, tmp, out_ap, out_deps):
    p.call("vector", "tensor_scalar", dict(
        out=tmp[:, 0:cols], in0=ps_in, scalar1=0.0, scalar2=None, op0=ALU.min),
        reads=[out_deps[0]], writes=[tmp])
    p.call("scalar", "activation", dict(
        out=tmp[:, 0:cols], in_=tmp[:, 0:cols], func=AF.Exp),
        reads=[tmp], writes=[tmp])
    p.call("vector", "scalar_tensor_tensor", dict(
        out=out_ap, in0=ps_in, scalar=0.0, in1=tmp[:, 0:cols],
        op0=ALU.max, op1=ALU.add),
        reads=[out_deps[0], tmp], writes=[out_deps[1]])


def emit_l0_attn(p, ms):
    H, HD, FD0 = 16, 64, 256
    wqT = ms.w16["L0_attn_WqT"]
    wkT = ms.w16["L0_attn_WkT"]
    wvT = ms.w16["L0_attn_WvT"]
    woT = ms.w16["L0_attn_WoT"]
    wqTk, wkTk, wvTk, woTk = ("w16_L0_attn_WqT", "w16_L0_attn_WkT",
                              "w16_L0_attn_WvT", "w16_L0_attn_WoT")
    acat = [p.sbuf([128, T], F16, f"l0a_ac{j}") for j in range(ND)]
    inner = p.unit_scope(); inner.__enter__()
    qpair = p.sbuf([128, T], F16, "l0a_q")
    kpair = p.sbuf([128, T], F16, "l0a_k")
    vau = p.sbuf([128, NT * 66], F16, "l0a_vau")
    wst = p.sbuf([128, 256], F32, "l0a_wst")
    wpr = [p.sbuf([128, 256], F16, f"l0a_wp{k}") for k in range(ND)]
    wvh = [p.sbuf([128, 64], F16, f"l0a_wv{k}") for k in range(ND)]
    wf2 = p.sbuf([128, 256], F16, "l0a_wf2")
    kftr = [p.sbuf([128, 256], F16, "l0a_kft0"), p.sbuf([128, 256], F16, "l0a_kft1")]
    kvh = p.sbuf([128, 132], F16, "l0a_kv")
    tmpr = [p.sbuf([128, 256], F32, "l0a_tmp0"), p.sbuf([128, 256], F32, "l0a_tmp1")]
    nrmr = [p.sbuf([128, 1], F32, "l0a_nrm0"), p.sbuf([128, 1], F32, "l0a_nrm1")]
    atr = [p.sbuf([128, 64], F16, "l0a_at0"), p.sbuf([128, 64], F16, "l0a_at1")]
    ident = ms.c["identity"]
    misc = ms.c["misc"]
    p.call("scalar", "copy", dict(out=wf2.ap, in_=ms.c["L0a_wf2"].ap),
           reads=[ms.c["L0a_wf2"]], writes=[wf2])
    psS = [ms.pb[4], ms.pb[5]]
    psQ2 = [ms.pb[4], ms.pb[5]]
    psKV = [ms.pb[0], ms.pb[1]]
    psOr = [ms.pb[2], ms.pb[3]]
    for pr in range(H // 2):
        for srcw, srck, dst in ((wqT, wqTk, 0), (wkT, wkTk, 1)):
            for k in range(ND):
                p.call("sync", "dma_start", dict(
                    out=wpr[k][:, dst * 128:dst * 128 + 128],
                    in_=srcw[k * 128:(k + 1) * 128, pr * 128:(pr + 1) * 128]),
                    reads=[srck], writes=[(wpr[k].key, dst)], dma=True)
        for c0 in range(0, T, TC):
            for dst, buf in ((0, qpair), (1, kpair)):
                ps = psS[dst]
                for k in range(ND):
                    p.call("tensor", "matmul", dict(
                        out=ps[:, 0:TC], lhsT=wpr[k][:, dst * 128:dst * 128 + 128],
                        rhs=ms.x_cm[k][:, PAD + c0:PAD + c0 + TC],
                        start=(k == 0), stop=(k == ND - 1)),
                        reads=[(wpr[k].key, dst), ms.x_cm[k].key], writes=[ps])
                p.call("scalar", "copy", dict(out=buf[:, c0:c0 + TC], in_=ps[:, 0:TC]),
                       reads=[ps], writes=[buf.key])
        for hp in range(2):
            h = 2 * pr + hp
            ro = hp * 64
            p.call("vector", "memset", dict(ap=vau.ap, constant=1.0), writes=[vau])
            for k in range(ND):
                p.call("sync", "dma_start", dict(
                    out=wvh[k].ap, in_=wvT[k * 128:(k + 1) * 128, h * 64:(h + 1) * 64]),
                    reads=[wvTk], writes=[wvh[k]], dma=True)
            for i in range(NT):
                p_v = ms.pb[3]
                for k in range(ND):
                    p.call("tensor", "matmul", dict(
                        out=p_v[:, 0:64], lhsT=ms.x_cm[k][:, PAD + i * 128:PAD + (i + 1) * 128],
                        rhs=wvh[k].ap, start=(k == 0), stop=(k == ND - 1)),
                        reads=[ms.x_cm[k].key, wvh[k]], writes=[p_v])
                p.call("vector", "tensor_copy", dict(
                    out=vau[:, i * 66:i * 66 + 64], in_=p_v[:, 0:64]),
                    reads=[p_v], writes=[vau])
            for gg in range(NT + 1):
                if gg < NT:
                    pf = psS[gg % 2]
                    p.call("tensor", "matmul", dict(
                        out=pf[:, 0:256], lhsT=kpair[ro:ro + 64, gg * 128:(gg + 1) * 128],
                        rhs=wf2[ro:ro + 64, :], start=True, stop=True),
                        reads=[kpair.key, wf2], writes=[pf])
                if gg >= 1:
                    i = gg - 1
                    pf = psS[i % 2]
                    kf_t = kftr[i % 2]
                    tmp_t = tmpr[i % 2]
                    _elu_p1(p, ms, pf[:, 0:256], 256, tmp_t, kf_t.ap, [pf, kf_t])
                    for fk in range(2):
                        p.call("tensor", "matmul", dict(
                            out=psKV[fk][:, 0:65], lhsT=kf_t[:, fk * 128:(fk + 1) * 128],
                            rhs=vau[:, i * 66:i * 66 + 65],
                            start=(i == 0), stop=(i == NT - 1)),
                            reads=[kf_t, vau], writes=[psKV[fk]])
            for fk in range(2):
                p.call("vector", "tensor_copy", dict(
                    out=kvh[:, fk * 66:fk * 66 + 65], in_=psKV[fk][:, 0:65]),
                    reads=[psKV[fk]], writes=[kvh])
            for gg in range(NT + 1):
                if gg < NT:
                    i = gg
                    pf2 = psQ2[i % 2]
                    for fk in range(2):
                        p.call("tensor", "matmul", dict(
                            out=pf2[:, fk * 128:(fk + 1) * 128],
                            lhsT=wf2[ro:ro + 64, fk * 128:(fk + 1) * 128],
                            rhs=qpair[ro:ro + 64, i * 128:(i + 1) * 128], start=True, stop=True),
                            reads=[qpair.key, wf2], writes=[pf2])
                if gg >= 1:
                    i = gg - 1
                    pf2 = psQ2[i % 2]
                    qf_t = kftr[i % 2]
                    tmp_t = tmpr[i % 2]
                    pO = psOr[i % 2]
                    nr = nrmr[i % 2]
                    at = atr[i % 2]
                    _elu_p1(p, ms, pf2[:, 0:256], 256, tmp_t, qf_t.ap, [pf2, qf_t])
                    for fk in range(2):
                        p.call("tensor", "matmul", dict(
                            out=pO[:, 0:65], lhsT=qf_t[:, fk * 128:(fk + 1) * 128],
                            rhs=kvh[:, fk * 66:fk * 66 + 65],
                            start=(fk == 0), stop=(fk == 1)),
                            reads=[qf_t, kvh], writes=[pO])
                    p.call("vector", "tensor_scalar", dict(
                        out=nr.ap, in0=pO[:, 64:65], scalar1=1e-6, scalar2=None, op0=ALU.add),
                        reads=[pO], writes=[nr])
                    p.call("vector", "reciprocal", dict(out=nr.ap, in_=nr.ap), reads=[nr], writes=[nr])
                    p.call("vector", "tensor_scalar", dict(
                        out=at.ap, in0=pO[:, 0:64], scalar1=nr.ap, scalar2=None, op0=ALU.mult),
                        reads=[pO, nr], writes=[at])
                    p.call("tensor", "transpose", dict(
                        out=ms.pbb[0:64, (i % 4) * 128:(i % 4) * 128 + 128], in_=at.ap, identity=ident.ap),
                        reads=[at, ident], writes=[(ms.pbb.key, i % 4)])
                    p.call("vector", "tensor_copy", dict(
                        out=acat[pr][ro:ro + 64, i * 128:(i + 1) * 128],
                        in_=ms.pbb[0:64, (i % 4) * 128:(i % 4) * 128 + 128]),
                        reads=[(ms.pbb.key, i % 4)], writes=[acat[pr].key])
    p.barrier()
    inner.__exit__(None, None, None)
    wo = [p.sbuf([128, D], F16, f"l0a_wo{k}") for k in range(ND)]
    for k in range(ND):
        p.call("sync", "dma_start", dict(out=wo[k].ap, in_=woT[k * 128:(k + 1) * 128, :]),
               reads=[woTk], writes=[wo[k]], dma=True, init=True)
    sub = [p.sbuf([128, TC], F16, f"l0a_s{j}") for j in range(ND)]
    pso = [ms.pb[2], ms.pb[3]]
    for c0 in range(0, T, TC):
        for j in range(ND):
            po = pso[j % 2]
            for k in range(ND):
                p.call("tensor", "matmul", dict(
                    out=po[:, 0:TC], lhsT=wo[k][:, j * 128:(j + 1) * 128],
                    rhs=acat[k][:, c0:c0 + TC],
                    start=(k == 0), stop=(k == ND - 1)),
                    reads=[wo[k], acat[k].key], writes=[po])
            p.call("scalar", "copy", dict(out=sub[j].ap, in_=po[:, 0:TC]),
                   reads=[po], writes=[sub[j]])
        ln_chunk(p, ms, c0, TC, [(sub[j].ap, sub[j]) for j in range(ND)])



def emit_l3_attn(p, ms):
    p.pe_sync = True
    DI, S = 2048, 16
    CW = 512
    NDI = DI // 128
    win = ms.w16["L3_attn_W_in"]
    wxre = ms.w16["L3a_WxT"]
    wout = ms.w16["L3_attn_W_out"]
    convw = ms.c["L3a_convw"]
    convb = ms.c["L3a_convb"]
    wdt = ms.c["L3a_wdt"]
    bdt = ms.c["L3a_bdt"]
    alog = ms.c["L3a_alog"]
    dvec = ms.c["L3a_D"]
    selg = load_lazy_const(p, ms, "L3a_selg")
    sumsel = load_lazy_const(p, ms, "L3a_sumsel")
    selbc = load_lazy_const(p, ms, "L3a_selbc")
    ident_f = ms.c["identityf"]
    misc = ms.c["misc"]
    ones1 = ms.c["ones1"]

    a_sb = p.sbuf([1, 16], F32, "l3a_asb")
    app = p.sbuf([128, 1], F32, "l3a_app")
    arow = p.sbuf([1, 128], F32, "l3a_arow")
    apat = p.sbuf([16, 128], BF16, "l3a_apat")
    selA = p.sbuf([128, 16 * 128], BF16, "l3a_selA")
    p.call("scalar", "activation", dict(out=a_sb.ap, in_=alog.ap, func=AF.Exp),
           reads=[alog], writes=[a_sb])
    p.call("vector", "tensor_scalar_mul", dict(out=a_sb.ap, in0=a_sb.ap, scalar1=-1.0),
           reads=[a_sb], writes=[a_sb])
    p.call("tensor", "matmul", dict(out=ms.pb[1][0:16, 0:1], lhsT=a_sb.ap, rhs=ones1[0:1, 0:1],
                                    start=True, stop=True), reads=[a_sb, ones1], writes=[ms.pb[1]])
    a16 = p.sbuf([16, 1], F32, "l3a_a16")
    p.call("vector", "tensor_copy", dict(out=a16.ap, in_=ms.pb[1][0:16, 0:1]),
           reads=[ms.pb[1]], writes=[a16])
    p.call("vector", "tensor_scalar", dict(
        out=apat.ap, in0=ms.c["selB16"].ap, scalar1=a16.ap, scalar2=None, op0=ALU.mult),
        reads=[a16, ms.c["selB16"]], writes=[apat])
    p.call("tensor", "matmul", dict(out=ms.pb[1][:, 0:1], lhsT=apat.ap, rhs=ms.c["ones128"][0:16, 0:1],
                                    start=True, stop=True), reads=[apat, ms.c["ones128"]], writes=[ms.pb[1]])
    p.call("vector", "tensor_copy", dict(out=app.ap, in_=ms.pb[1][:, 0:1]), reads=[ms.pb[1]], writes=[app])
    p.call("tensor", "matmul", dict(out=ms.pb[1][0:1, 0:128], lhsT=app.ap, rhs=ident_f.ap,
                                    start=True, stop=True), reads=[app, ident_f], writes=[ms.pb[1]])
    p.call("vector", "tensor_copy", dict(out=arow.ap, in_=ms.pb[1][0:1, 0:128]), reads=[ms.pb[1]], writes=[arow])
    p.call("tensor", "matmul", dict(out=ms.pb[1][:, 0:128], lhsT=ones1.ap, rhs=arow.ap,
                                    start=True, stop=True), reads=[ones1, arow], writes=[ms.pb[1]])
    for g in range(16):
        p.call("vector", "tensor_tensor", dict(
            out=selA[:, g * 128:(g + 1) * 128], in0=selg[:, g * 128:(g + 1) * 128],
            in1=ms.pb[1][:, 0:128], op=ALU.mult),
            reads=[selg, ms.pb[1]], writes=[selA])

    wmt = [p.sbuf([128, 1024], F16, f"l3a_wmt{i}") for i in range(2)]
    _wjt = p.sbuf([128, 16 * 128], F16, "l3a_wjt0")
    wjt = [_wjt, _wjt]
    wxs = [p.sbuf([128, 33], F16, f"l3a_wx{k}") for k in range(NDI)]
    raw = [p.sbuf([128, 3 + CW], F16, "l3a_raw0"), p.sbuf([128, 3 + CW], F16, "l3a_raw1")]
    tails = p.sbuf([128, NDI * 3], F16, "l3a_tails")
    xip = p.sbuf([128, NDI * CW], F16, "l3a_xip")
    zbuf = p.sbuf([128, NDI * CW], F16, "l3a_z")
    delta = p.sbuf([128, NDI * CW], F16, "l3a_delta")
    dtxm = p.sbuf([128, CW], F16, "l3a_dtxm")
    ygate = p.sbuf([128, NDI * CW], F16, "l3a_yg")
    xpsb = p.sbuf([33, CW], F16, "l3a_xp")
    bbc = p.sbuf([128, CW], F16, "l3a_bbc")
    cbc = p.sbuf([128, CW], F16, "l3a_cbc")
    dtb = p.sbuf([128, CW], F32, "l3a_dtb")
    t2 = p.sbuf([128, CW], F32, "l3a_t2")
    acc = p.sbuf([128, CW], F32, "l3a_acc")
    t3 = Buf(acc.key, acc.h)
    asb = [Buf(dtb.key, dtb.h), Buf(t2.key, t2.h)]
    usb = [p.sbuf([128, CW], F16, "l3a_u0"), p.sbuf([128, CW], F16, "l3a_u1")]
    hsb = [p.sbuf([128, CW], F32, "l3a_h0"), p.sbuf([128, CW], F32, "l3a_h1")]
    hcs = [p.sbuf([128, CW], F16, "l3a_hc0"), p.sbuf([128, CW], F16, "l3a_hc1")]
    carry = p.sbuf([128, 256], F32, "l3a_carry")
    szb = p.sbuf([128, CW], F16, "l3a_sz")
    sub = [p.sbuf([128, CW], F16, f"l3a_s{j}") for j in range(ND)]
    p.call("vector", "memset", dict(ap=carry.ap, constant=0.0), writes=[carry])
    p.call("vector", "memset", dict(ap=tails.ap, constant=0.0), writes=[tails])

    psXZ = [ms.pb[4], ms.pb[5], ms.pb[1]]
    psU2 = [ms.pb[1], ms.pb[0]]
    psB = ms.pb[0]
    psY = [ms.pb[2], ms.pb[3]]
    for ci, c0 in enumerate(range(0, T, CW)):
        for m in range(32):
            ph = psXZ[m % 3]
            wm = wmt[m % 2]
            kt3 = wm.h.rearrange("p (kt c) -> p kt c", c=128)
            p.call("sync", "dma_start", dict(
                out=kt3, in_=win[:, m * 128:(m + 1) * 128].rearrange("(kt p) c -> p kt c", p=128)),
                reads=["w16_L3_attn_W_in"], writes=[wm], dma=True)
            for k in range(ND):
                p.call("tensor", "matmul", dict(
                    out=ph[:, 0:CW], lhsT=wm[:, k * 128:(k + 1) * 128],
                    rhs=ms.x_cm[k][:, PAD + c0:PAD + c0 + CW],
                    start=(k == 0), stop=(k == ND - 1)),
                    reads=[wm, ms.x_cm[k].key], writes=[ph])
            if m < NDI:
                rb = raw[m % 2]
                p.call("vector", "tensor_copy", dict(out=rb[:, 0:3], in_=tails[:, m * 3:m * 3 + 3]),
                       reads=[tails], writes=[rb])
                p.call("scalar", "copy", dict(out=rb[:, 3:3 + CW], in_=ph[:, 0:CW]),
                       reads=[ph], writes=[(rb.key, 1)])
                p.call("vector", "tensor_copy", dict(out=tails[:, m * 3:m * 3 + 3], in_=rb[:, CW:CW + 3]),
                       reads=[(rb.key, 1)], writes=[tails])
                p.call("vector", "tensor_scalar", dict(
                    out=acc.ap, in0=rb[:, 0:CW], scalar1=convw[:, m * 4:m * 4 + 1], scalar2=None,
                    op0=ALU.mult), reads=[rb, (rb.key, 1), convw], writes=[acc])
                for j in range(1, 4):
                    p.call("vector", "scalar_tensor_tensor", dict(
                        out=acc.ap, in0=rb[:, j:j + CW], scalar=convw[:, m * 4 + j:m * 4 + j + 1],
                        in1=acc.ap, op0=ALU.mult, op1=ALU.add),
                        reads=[rb, (rb.key, 1), convw, acc], writes=[acc])
                p.call("scalar", "activation", dict(
                    out=xip[:, m * CW:(m + 1) * CW], in_=acc.ap, func=AF.Silu,
                    bias=convb[:, m:m + 1]),
                    reads=[acc, convb], writes=[(xip.key, m)])
            else:
                mz = m - NDI
                p.call("scalar", "copy", dict(out=zbuf[:, mz * CW:(mz + 1) * CW], in_=ph[:, 0:CW]),
                       reads=[ph], writes=[(zbuf.key, mz)])
        for k in range(NDI):
            if ci == 0:
                p.call("sync", "dma_start", dict(
                    out=wxs[k].ap, in_=wxre[k * 128:(k + 1) * 128, :]),
                    reads=["w16_L3a_WxT"], writes=[wxs[k]], dma=True, init=True)
            p.call("tensor", "matmul", dict(
                out=psB[0:33, 0:CW], lhsT=wxs[k].ap, rhs=xip[:, k * CW:(k + 1) * CW],
                start=(k == 0), stop=(k == NDI - 1)),
                reads=[wxs[k], (xip.key, k)], writes=[psB])
        p.call("scalar", "copy", dict(out=xpsb.ap, in_=psB[0:33, 0:CW]), reads=[psB], writes=[xpsb])
        p.call("tensor", "matmul", dict(
            out=psB[:, 0:CW], lhsT=selbc[:, 0:128], rhs=xpsb.ap, start=True, stop=True),
            reads=[selbc, xpsb], writes=[psB])
        p.call("vector", "tensor_copy", dict(out=dtb.ap, in_=psB[:, 0:CW]), reads=[psB], writes=[dtb])
        p.call("tensor", "matmul", dict(
            out=psB[:, 0:CW], lhsT=selbc[:, 128:256], rhs=xpsb.ap, start=True, stop=True),
            reads=[selbc, xpsb], writes=[psB])
        p.call("vector", "tensor_copy", dict(out=bbc.ap, in_=psB[:, 0:CW]), reads=[psB], writes=[bbc])
        p.call("tensor", "matmul", dict(
            out=psB[:, 0:CW], lhsT=selbc[:, 256:384], rhs=xpsb.ap, start=True, stop=True),
            reads=[selbc, xpsb], writes=[psB])
        p.call("vector", "tensor_copy", dict(out=cbc.ap, in_=psB[:, 0:CW]), reads=[psB], writes=[cbc])
        for m in range(NDI):
            p.call("vector", "tensor_scalar", dict(
                out=t2.ap, in0=dtb.ap, scalar1=wdt[:, m:m + 1], scalar2=None, op0=ALU.mult),
                reads=[dtb, wdt], writes=[t2])
            p.call("vector", "tensor_scalar", dict(
                out=t2.ap, in0=t2.ap, scalar1=bdt[:, m:m + 1], scalar2=None, op0=ALU.add),
                reads=[t2, bdt], writes=[t2])
            p.call("scalar", "activation", dict(out=t3.ap, in_=t2.ap, func=AF.Abs),
                reads=[t2], writes=[t3])
            p.call("scalar", "activation", dict(out=t3.ap, in_=t3.ap, func=AF.Exp, scale=-1.0),
                   reads=[t3], writes=[t3])
            p.call("scalar", "activation", dict(out=t3.ap, in_=t3.ap, func=AF.Ln, bias=misc[:, 7:8]),
                   reads=[t3, misc], writes=[t3])
            p.call("vector", "scalar_tensor_tensor", dict(
                out=delta[:, m * CW:(m + 1) * CW], in0=t2.ap, scalar=0.0, in1=t3.ap,
                op0=ALU.max, op1=ALU.add), reads=[t2, t3], writes=[(delta.key, m)])
        for m in range(NDI):
            pY = psY[m % 2]
            p.call("vector", "tensor_tensor", dict(
                out=dtxm.ap, in0=delta[:, m * CW:(m + 1) * CW],
                in1=xip[:, m * CW:(m + 1) * CW], op=ALU.mult),
                reads=[(delta.key, m), (xip.key, m)], writes=[dtxm])
            for gg in range(17):
                if gg < 16:
                    g = gg
                    psa = psXZ[g % 2]
                    psu = psU2[g % 2]
                    p.call("tensor", "matmul", dict(
                        out=psa[:, 0:CW], lhsT=selA[:, g * 128:(g + 1) * 128],
                        rhs=delta[:, m * CW:(m + 1) * CW], start=True, stop=True),
                        reads=[selA, (delta.key, m)], writes=[psa])
                    p.call("tensor", "matmul", dict(
                        out=psu[:, 0:CW], lhsT=selg[:, g * 128:(g + 1) * 128],
                        rhs=dtxm.ap, start=True, stop=True),
                        reads=[selg, dtxm], writes=[psu])
                if 1 <= gg:
                    g = gg - 1
                    psa = psXZ[g % 2]
                    psu = psU2[g % 2]
                    a_t = asb[g % 2]
                    u_t = usb[g % 2]
                    h_t = hsb[g % 2]
                    hc_t = hcs[g % 2]
                    col = m * 16 + g
                    p.call("scalar", "activation", dict(out=a_t.ap, in_=psa[:, 0:CW], func=AF.Exp),
                           reads=[psa], writes=[a_t])
                    p.call("vector", "tensor_tensor", dict(
                        out=u_t.ap, in0=psu[:, 0:CW], in1=bbc.ap, op=ALU.mult),
                        reads=[psu, bbc], writes=[u_t])
                    p.call("vector", "tensor_tensor_scan", dict(
                        out=h_t.ap, data0=a_t.ap, data1=u_t.ap, initial=carry[:, col:col + 1],
                        op0=ALU.mult, op1=ALU.add),
                        reads=[a_t, u_t, (carry.key, col)], writes=[h_t])
                    p.call("vector", "tensor_copy", dict(
                        out=carry[:, col:col + 1], in_=h_t[:, CW - 1:CW]),
                        reads=[h_t], writes=[(carry.key, col)])
                    p.call("gpsimd", "tensor_tensor", dict(
                        out=hc_t.ap, in0=h_t.ap, in1=cbc.ap, op=ALU.mult),
                        reads=[h_t, cbc], writes=[hc_t])
                    p.call("tensor", "matmul", dict(
                        out=pY[:, 0:CW], lhsT=sumsel[:, g * 128:(g + 1) * 128], rhs=hc_t.ap,
                        start=(g == 0), stop=(g == 15)),
                        reads=[sumsel, hc_t], writes=[pY])
            p.call("scalar", "activation", dict(
                out=szb.ap, in_=zbuf[:, m * CW:(m + 1) * CW], func=AF.Silu),
                reads=[(zbuf.key, m)], writes=[szb])
            p.call("vector", "scalar_tensor_tensor", dict(
                out=acc.ap, in0=xip[:, m * CW:(m + 1) * CW], scalar=dvec[:, m:m + 1],
                in1=pY[:, 0:CW], op0=ALU.mult, op1=ALU.add),
                reads=[(xip.key, m), dvec, pY], writes=[acc])
            p.call("vector", "tensor_tensor", dict(
                out=ygate[:, m * CW:(m + 1) * CW], in0=acc.ap, in1=szb.ap, op=ALU.mult),
                reads=[acc, szb], writes=[(ygate.key, m)])
        for j in range(ND):
            po = psY[j % 2]
            wj = wjt[j % 2]
            kt3 = wj.h.rearrange("p (kt c) -> p kt c", c=128)
            p.call("sync", "dma_start", dict(
                out=kt3, in_=wout[:, j * 128:(j + 1) * 128].rearrange("(kt p) c -> p kt c", p=128)),
                reads=["w16_L3_attn_W_out"], writes=[wj], dma=True)
            for k in range(NDI):
                p.call("tensor", "matmul", dict(
                    out=po[:, 0:CW], lhsT=wj[:, k * 128:(k + 1) * 128],
                    rhs=ygate[:, k * CW:(k + 1) * CW],
                    start=(k == 0), stop=(k == NDI - 1)),
                    reads=[wj, (ygate.key, k)], writes=[po])
            p.call("scalar", "copy", dict(out=sub[j].ap, in_=po[:, 0:CW]), reads=[po], writes=[sub[j]])
        ln_chunk(p, ms, c0, CW, [(sub[j].ap, sub[j]) for j in range(ND)])





def dma3(p, dstbuf, ncols_total, csz, src2d, reads=(), eng="sync"):
    kt = ncols_total // csz
    out3 = dstbuf.h[:, 0:ncols_total].rearrange("p (kt c) -> p kt c", c=csz)
    in3 = src2d.rearrange("(kt p) c -> p kt c", p=128)
    p.call(eng, "dma_start", dict(out=out3, in_=in3), reads=list(reads), writes=[dstbuf], dma=True)

def _l2f_shift_col(k, c):
    return c - 1 if k >= 4 else c


def emit_l2f_inner(p, ms, l, bufs):
    (wqkv, w1il, w2re, qsc, ksc, lng, lnb, b1il, b2c, tabs,
     qbuf, kbuf, vau, obuf, ptb, qsq, rstd, t1, t2, rot_ps, lnbuf, hm,
     w1st, w1sb, w2st, w2sb, lnn, recip, wpq, wvh, qnb) = bufs
    maskf = ms.l2f_maskf
    smt = ms.l2f_smt
    H, HD, W = 8, 64, 512
    INNER = 512
    HID22 = 22
    misc = ms.c["misc"]
    ones = ms.c["ones128"]
    mask = ms.l2f_mask
    rotP = ms.c["rotPT"]
    psA = [ms.pb[4], ms.pb[5]]
    psB = [ms.pb[2], ms.pb[3]]
    psN = ms.pb[0]
    psR = ms.pb[1]
    for pr in range(H // 2):
        for dst, buf, co in ((0, qbuf, pr * 128), (1, kbuf, INNER + pr * 128)):
            dma3(p, wpq[2], 1024, 128, wqkv[:, co:co + 128], reads=[f"w16_L2f_wqkv{l}"])
            for c0 in range(0, T, TC):
                ps = psA[dst]
                for k in range(ND):
                    sc = _l2f_shift_col(k, PAD + c0)
                    p.call("tensor", "matmul", dict(
                        out=ps[:, 0:TC], lhsT=wpq[2][:, k * 128:(k + 1) * 128],
                        rhs=ms.x_cm[k][:, sc:sc + TC],
                        start=(k == 0), stop=(k == ND - 1)),
                        reads=[wpq[2], ms.x_cm[k].key], writes=[ps])
                p.call("scalar", "copy", dict(out=buf[:, c0:c0 + TC], in_=ps[:, 0:TC]),
                       reads=[ps], writes=[buf.key])
        for qk, buf, sct, ctab, stab in ((0, qbuf, qsc, 0, 1), (1, kbuf, ksc, 2, 3)):
            for c0 in range(0, T, TC):
                p.call("sync", "dma_start", dict(
                    out=tabs[0].ap, in_=ms.l2f_tabd[:, (2 * qk) * T + c0:(2 * qk) * T + c0 + TC]),
                    writes=[tabs[0]], dma=True)
                p.call("sync", "dma_start", dict(
                    out=tabs[1].ap, in_=ms.l2f_tabd[:, (2 * qk + 1) * T + c0:(2 * qk + 1) * T + c0 + TC]),
                    writes=[tabs[1]], dma=True)
                p.call("scalar", "activation", dict(
                    out=qsq.ap, in_=buf[:, c0:c0 + TC], func=AF.Square),
                    reads=[buf.key], writes=[qsq])
                for hp in range(2):
                    ro = hp * 64
                    p.call("tensor", "matmul", dict(
                        out=psN[:, 0:TC], lhsT=ones[ro:ro + 64, :], rhs=qsq[ro:ro + 64, 0:TC],
                        start=True, stop=True),
                        reads=[qsq, ones], writes=[psN])
                    p.call("scalar", "activation", dict(
                        out=t1.ap, in_=psN[:, 0:TC], func=AF.Ln, bias=misc[:, 1:2]),
                        reads=[psN, misc], writes=[t1])
                    p.call("scalar", "activation", dict(
                        out=rstd.ap, in_=t1.ap, func=AF.Exp, scale=-0.5),
                        reads=[t1], writes=[rstd])
                    p.call("vector", "scalar_tensor_tensor", dict(
                        out=qnb[ro:ro + 64, 0:TC], in0=buf[ro:ro + 64, c0:c0 + TC],
                        scalar=sct[ro:ro + 64, l:l + 1], in1=rstd[ro:ro + 64, 0:TC],
                        op0=ALU.mult, op1=ALU.mult),
                        reads=[buf.key, sct, rstd], writes=[qnb])
                p.call("tensor", "matmul", dict(
                    out=rot_ps[:, 0:TC], lhsT=rotP.ap, rhs=qnb[:, 0:TC],
                    start=True, stop=True),
                    reads=[rotP, qnb], writes=[rot_ps])
                p.call("vector", "tensor_tensor", dict(
                    out=t1.ap, in0=qnb[:, 0:TC], in1=tabs[0].ap, op=ALU.mult),
                    reads=[qnb, tabs[0]], writes=[t1])
                p.call("vector", "tensor_tensor", dict(
                    out=t2.ap, in0=rot_ps[:, 0:TC], in1=tabs[1].ap, op=ALU.mult),
                    reads=[rot_ps, tabs[1]], writes=[t2])
                p.call("vector", "tensor_tensor", dict(
                    out=buf[:, c0:c0 + TC], in0=t1.ap, in1=t2.ap, op=ALU.add),
                    reads=[t1, t2], writes=[buf.key])
        for hp in range(2):
            h = 2 * pr + hp
            ro = hp * 64
            dma3(p, wpq[2], 512, 64, wqkv[:, 2 * INNER + h * 64:2 * INNER + (h + 1) * 64],
                 reads=[f"w16_L2f_wqkv{l}"])
            p.call("vector", "memset", dict(ap=vau.ap, constant=1.0), writes=[vau])
            for i in range(NT):
                for k in range(ND):
                    sc = _l2f_shift_col(k, PAD + i * 128)
                    p.call("tensor", "matmul", dict(
                        out=psN[:, 0:64], lhsT=ms.x_cm[k][:, sc:sc + 128],
                        rhs=wpq[2][:, k * 64:(k + 1) * 64], start=(k == 0), stop=(k == ND - 1)),
                        reads=[ms.x_cm[k].key, wpq[2]], writes=[psN])
                p.call("vector", "tensor_copy", dict(
                    out=vau[:, i * 128:i * 128 + 64], in_=psN[:, 0:64]),
                    reads=[psN], writes=[vau])
            for w in range(T // W):
                q0 = w * W
                kts = [kt for kt in range((w - 1) * 4, w * 4 + 4) if kt >= 0]
                for ki, kt in enumerate(kts):
                    pS = (psA + [psN])[ki % 3]
                    st_ = smt[ki % 3]
                    p.call("tensor", "matmul", dict(
                        out=pS[:, 0:W], lhsT=kbuf[ro:ro + 64, kt * 128:(kt + 1) * 128],
                        rhs=qbuf[ro:ro + 64, q0:q0 + W], start=True, stop=True),
                        reads=[kbuf.key], writes=[pS])
                    dq = kt * 128 - q0
                    if 0 <= dq < W:
                        madd = mask[:, (dq // 128) * 512:(dq // 128) * 512 + W]
                        mkey = mask
                    else:
                        blk = (kt * 128 - (q0 - W)) // 128
                        madd = maskf[:, blk * 512:blk * 512 + W]
                        mkey = maskf
                    p.call("vector", "tensor_tensor", dict(
                        out=st_[:, 0:W], in0=pS[:, 0:W], in1=madd, op=ALU.add),
                        reads=[pS, mkey], writes=[st_])
                    p.call("scalar", "activation", dict(
                        out=ptb[:, ki * W:(ki + 1) * W], in_=st_[:, 0:W], func=AF.Exp,
                        bias=misc[:, 5:6]),
                        reads=[st_, misc], writes=[(ptb.key, ki)])
                for ki, kt in enumerate(kts):
                    p.call("tensor", "matmul", dict(
                        out=psB[w % 2][:, 0:W], lhsT=vau[:, kt * 128:(kt + 1) * 128],
                        rhs=ptb[:, ki * W:(ki + 1) * W],
                        start=(ki == 0), stop=(ki == len(kts) - 1)),
                        reads=[vau, (ptb.key, ki)], writes=[psB[w % 2]])
                p.call("scalar", "activation", dict(
                    out=lnn.ap, in_=psB[w % 2][64:128, 0:W], func=AF.Ln, scale=float(2.0 ** 40)),
                    reads=[psB[w % 2]], writes=[lnn])
                p.call("scalar", "activation", dict(
                    out=recip.ap, in_=lnn.ap, func=AF.Exp, scale=-1.0, bias=misc[0:64, 6:7]),
                    reads=[lnn, misc], writes=[recip])
                p.call("vector", "tensor_tensor", dict(
                    out=obuf[ro:ro + 64, q0:q0 + W], in0=psB[w % 2][0:64, 0:W],
                    in1=recip.ap, op=ALU.mult),
                    reads=[psB[w % 2], recip], writes=[obuf.key])
        p.call("gpsimd", "dma_start", dict(
            out=ms.l2f_oscr[pr * 128:(pr + 1) * 128, :], in_=obuf.ap),
            reads=[obuf.key], writes=["l2f_oscr"], dma=True)
    ob4 = ms.l2f_ob4
    for c0 in range(0, T, TC):
        for k in range(4):
            p.call("sync", "dma_start", dict(
                out=ob4[k].ap, in_=ms.l2f_oscr[k * 128:(k + 1) * 128, c0:c0 + TC]),
                reads=["l2f_oscr"], writes=[ob4[k]], dma=True)
        for j in range(ND):
            po = psB[j % 2]
            dma3(p, w2sb, 512, 128, bufs_wo(ms, l)[:, j * 128:(j + 1) * 128],
                 reads=[f"w16_L2f_wo{l}"])
            for k in range(4):
                p.call("tensor", "matmul", dict(
                    out=po[:, 0:TC], lhsT=w2sb[:, k * 128:(k + 1) * 128],
                    rhs=ob4[k].ap, start=(k == 0), stop=(k == 3)),
                    reads=[w2sb, ob4[k]], writes=[po])
            p.call("vector", "tensor_tensor", dict(
                out=ms.x_cm[j][:, PAD + c0:PAD + c0 + TC],
                in0=po[:, 0:TC], in1=ms.x_cm[j][:, PAD + c0:PAD + c0 + TC], op=ALU.add),
                reads=[po, ms.x_cm[j].key], writes=[ms.x_cm[j].key])
    xbnd = ms.l2f_xbnd
    for j in range(4, ND):
        for bi, c0 in enumerate(range(TC, T, TC)):
            p.call("vector", "tensor_copy", dict(
                out=xbnd[j - 4][:, bi:bi + 1], in_=ms.x_cm[j][:, PAD + c0 - 1:PAD + c0]),
                reads=[ms.x_cm[j].key], writes=[xbnd[j - 4]])
    for c0 in range(0, T, TC):
        for j in range(ND):
            sc = _l2f_shift_col(j, PAD + c0)
            p.call("vector", "tensor_copy", dict(
                out=ms.ln_r[j][:, 0:TC], in_=ms.x_cm[j][:, sc:sc + TC]),
                reads=[ms.x_cm[j].key], writes=[ms.ln_r[j]])
            if j >= 4 and c0 > 0:
                p.call("vector", "tensor_copy", dict(
                    out=ms.ln_r[j][:, 0:1], in_=xbnd[j - 4][:, c0 // TC - 1:c0 // TC]),
                    reads=[xbnd[j - 4]], writes=[ms.ln_r[j]])
        _ln_core(p, ms, TC)
        for j in range(ND):
            rc = ms.ps_rc[j % 2]
            p.call("tensor", "matmul", dict(
                out=rc[:, 0:TC], lhsT=ms.c["identity"].ap, rhs=ms.ln_r[j][:, 0:TC],
                start=True, stop=False), reads=[ms.ln_r[j], ms.c["identity"]], writes=[rc])
            p.call("tensor", "matmul", dict(
                out=rc[:, 0:TC], lhsT=ms.c["negones_div"].ap, rhs=ms.ln_m[:, 0:TC],
                start=False, stop=True), reads=[ms.ln_m, ms.c["negones_div"]], writes=[rc])
            p.call("vector", "tensor_tensor", dict(
                out=lnbuf[j].ap, in0=rc[:, 0:TC], in1=ms.ln_rstd[:, 0:TC], op=ALU.mult),
                reads=[rc, ms.ln_rstd], writes=[lnbuf[j]])
            p.call("vector", "tensor_scalar", dict(
                out=lnbuf[j].ap, in0=lnbuf[j].ap, scalar1=lng[:, l * 8 + j:l * 8 + j + 1],
                scalar2=None, op0=ALU.mult), reads=[lnbuf[j], lng], writes=[lnbuf[j]])
            p.call("vector", "tensor_scalar", dict(
                out=lnbuf[j].ap, in0=lnbuf[j].ap, scalar1=lnb[:, l * 8 + j:l * 8 + j + 1],
                scalar2=None, op0=ALU.add), reads=[lnbuf[j], lnb], writes=[lnbuf[j]])
        for m in range(HID22):
            sb = w1sb[m % 2]
            for hf in range(2):
                kt3 = sb.h[:, hf * 1024:(hf + 1) * 1024].rearrange("p (kt c) -> p kt c", c=128)
                p.call("sync", "dma_start", dict(
                    out=kt3,
                    in_=w1il[:, m * 256 + hf * 128:m * 256 + (hf + 1) * 128].rearrange(
                        "(kt p) c -> p kt c", p=128)),
                    reads=[f"w16_L2f_w1il{l}"], writes=[sb], dma=True)
            pa0 = psA[0] if m % 2 == 0 else psB[0]
            pa1 = psA[1] if m % 2 == 0 else psB[1]
            for blk, pa in ((0, pa0), (1, pa1)):
                for k in range(ND):
                    p.call("tensor", "matmul", dict(
                        out=pa[:, 0:TC], lhsT=sb[:, blk * 1024 + k * 128:blk * 1024 + (k + 1) * 128],
                        rhs=lnbuf[k].ap, start=(k == 0), stop=(k == ND - 1)),
                        reads=[sb, lnbuf[k]], writes=[pa])
            p.call("vector", "tensor_scalar", dict(
                out=t1[:, 0:TC], in0=pa0[:, 0:TC], scalar1=b1il[:, 2 * m:2 * m + 1],
                scalar2=None, op0=ALU.add), reads=[pa0, b1il], writes=[t1])
            p.call("scalar", "activation", dict(
                out=t2[:, 0:TC], in_=pa1[:, 0:TC], func=AF.Gelu, bias=b1il[:, 2 * m + 1:2 * m + 2]),
                reads=[pa1, b1il], writes=[t2])
            p.call("vector", "tensor_tensor", dict(
                out=hm[:, m * TC:(m + 1) * TC], in0=t1[:, 0:TC], in1=t2[:, 0:TC], op=ALU.mult),
                reads=[t1, t2], writes=[(hm.key, m)])
        for j in range(ND):
            po = psB[j % 2]
            for hf in range(2):
                kt3 = w2sb.h[:, hf * 1408:(hf + 1) * 1408].rearrange("p (kt c) -> p kt c", c=128)
                p.call("sync", "dma_start", dict(
                    out=kt3,
                    in_=w2re[hf * 1408:(hf + 1) * 1408, j * 128:(j + 1) * 128].rearrange(
                        "(kt p) c -> p kt c", p=128)),
                    reads=[f"w16_L2f_w2re{l}"], writes=[w2sb], dma=True)
            for k in range(HID22):
                p.call("tensor", "matmul", dict(
                    out=po[:, 0:TC], lhsT=w2sb[:, k * 128:(k + 1) * 128],
                    rhs=hm[:, k * TC:(k + 1) * TC], start=(k == 0), stop=(k == HID22 - 1)),
                    reads=[w2sb, (hm.key, k)], writes=[po])
            p.call("vector", "tensor_scalar", dict(
                out=t1[:, 0:TC], in0=po[:, 0:TC], scalar1=b2c[:, l * 8 + j:l * 8 + j + 1],
                scalar2=None, op0=ALU.add), reads=[po, b2c], writes=[t1])
            p.call("vector", "tensor_tensor", dict(
                out=ms.x_cm[j][:, PAD + c0:PAD + c0 + TC], in0=t1[:, 0:TC],
                in1=ms.x_cm[j][:, PAD + c0:PAD + c0 + TC], op=ALU.add),
                reads=[t1, ms.x_cm[j].key], writes=[ms.x_cm[j].key])


def bufs_wo(ms, l):
    return ms.l2f_wo[l]


def _ln_core(p, ms, w):
    ones = ms.c["ones128"]
    misc = ms.c["misc"]
    for j in range(ND):
        rq = ms.ln_rsq[j % 2]
        p.call("scalar", "activation", dict(
            out=rq[:, 0:w], in_=ms.ln_r[j][:, 0:w], func=AF.Square),
            reads=[ms.ln_r[j]], writes=[rq])
        p.call("tensor", "matmul", dict(
            out=ms.ps_s[:, 0:w], lhsT=ones.ap, rhs=ms.ln_r[j][:, 0:w],
            start=(j == 0), stop=(j == ND - 1)), reads=[ms.ln_r[j], ones], writes=[ms.ps_s])
        p.call("tensor", "matmul", dict(
            out=ms.ps_q[:, 0:w], lhsT=ones.ap, rhs=rq[:, 0:w],
            start=(j == 0), stop=(j == ND - 1)), reads=[rq, ones], writes=[ms.ps_q])
    p.call("vector", "tensor_scalar_mul", dict(
        out=ms.ln_m[:, 0:w], in0=ms.ps_s[:, 0:w], scalar1=1.0 / D),
        reads=[ms.ps_s], writes=[ms.ln_m])
    p.call("vector", "tensor_tensor", dict(
        out=ms.ln_msq[:, 0:w], in0=ms.ln_m[:, 0:w], in1=ms.ln_m[:, 0:w], op=ALU.mult),
        reads=[ms.ln_m], writes=[ms.ln_msq])
    p.call("vector", "scalar_tensor_tensor", dict(
        out=ms.ln_var[:, 0:w], in0=ms.ps_q[:, 0:w], scalar=1.0 / D, in1=ms.ln_msq[:, 0:w],
        op0=ALU.mult, op1=ALU.subtract), reads=[ms.ps_q, ms.ln_msq], writes=[ms.ln_var])
    p.call("scalar", "activation", dict(
        out=ms.ln_var[:, 0:w], in_=ms.ln_var[:, 0:w], func=AF.Ln, bias=misc[:, 0:1]),
        reads=[ms.ln_var, misc], writes=[ms.ln_var])
    p.call("scalar", "activation", dict(
        out=ms.ln_rstd[:, 0:w], in_=ms.ln_var[:, 0:w], func=AF.Exp, scale=-0.5),
        reads=[ms.ln_var], writes=[ms.ln_rstd])


def emit_l2_ffn(p, ms):
    wqkv = [ms.w16[f"L2f_wqkv{l}"] for l in range(2)]
    ms.l2f_wo = [ms.w16[f"L2f_wo{l}"] for l in range(2)]
    w1il = [ms.w16[f"L2f_w1il{l}"] for l in range(2)]
    w2re = [ms.w16[f"L2f_w2re{l}"] for l in range(2)]
    tabd = p.din("L2f_tabs", [128, 4 * T], BF16)
    xsp = p.dscratch("l2f_xorig", [ND * 128, T], F16)
    ms.l2f_mask = load_lazy_const(p, ms, "causaladd")
    ms.l2f_maskf = load_lazy_const(p, ms, "causaladd_far")
    qsc = ms.c["L2f_qs"]
    ksc = ms.c["L2f_ks"]
    lng = ms.c["L2f_lng"]
    lnb = ms.c["L2f_lnb"]
    b1il = ms.c["L2f_b1il"]
    b2c = ms.c["L2f_b2"]
    for j in range(ND):
        p.call("gpsimd", "dma_start", dict(
            out=xsp[j * 128:(j + 1) * 128, :], in_=ms.x_cm[j][:, PAD:PAD + T]),
            reads=[ms.x_cm[j].key], writes=["l2f_xorig"], dma=True)
    tabs = [p.sbuf([128, TC], BF16, "l2f_tabc"), p.sbuf([128, TC], BF16, "l2f_tabs")]
    ms.l2f_tabd = tabd
    qbuf = p.sbuf([128, T], BF16, "l2f_q")
    kbuf = p.sbuf([128, T], BF16, "l2f_k")
    vau = p.sbuf([128, NT * 128], BF16, "l2f_vau")
    obuf = p.sbuf([128, T], F16, "l2f_o")
    oscr = p.dscratch("l2f_oscr", [512, T], F16)
    ms.l2f_oscr = oscr

    ptb = p.sbuf([128, 8 * 512], BF16, "l2f_pt")
    qsq = p.sbuf([128, TC], BF16, "l2f_qsq")
    qnb = Buf(qsq.key, qsq.h)
    ms.l2f_smt = [p.sbuf([128, 512], F32, f"l2f_sm{i}") for i in range(3)]
    ms.l2f_xbnd = [p.sbuf([128, 8], F16, f"l2f_xb{i}") for i in range(4)]
    rstd = p.sbuf([128, TC], F32, "l2f_rstd")
    t1 = p.sbuf([128, TC], F32, "l2f_t1")
    t2 = p.sbuf([128, TC], F32, "l2f_t2")
    rot_ps = ms.pb[1]
    lnbuf = [p.sbuf([128, TC], F16, f"l2f_lnb{j}") for j in range(ND)]
    ms.l2f_ob4 = lnbuf[0:4]
    hm = p.sbuf([128, 22 * TC], F16, "l2f_hm")
    _w1st = p.sbuf([128, 1024], F32, "l2f_w1st0")
    w1st = [_w1st, _w1st]
    _w1sb = p.sbuf([128, 2048], F16, "l2f_w1sb0")
    w1sb = [_w1sb, _w1sb]
    w2st = p.sbuf([128, 11 * 128], F32, "l2f_w2st")
    w2sb = p.sbuf([128, 22 * 128], F16, "l2f_w2sb")
    lnn = p.sbuf([64, 512], F32, "l2f_lnn")
    recip = p.sbuf([64, 512], F32, "l2f_recip")

    wpq = [None, None, p.sbuf([128, 1024], F16, "l2f_wpq2")]
    mk_ln_bufs(p, ms)
    bufs = (None, None, None, qsc, ksc, lng, lnb, b1il, b2c, tabs,
            qbuf, kbuf, vau, obuf, ptb, qsq, rstd, t1, t2, rot_ps, lnbuf, hm,
            w1st, w1sb, w2st, w2sb, lnn, recip, wpq, None, qnb)
    import os as _os
    dbg = p.dout("l2f_dbg", [ND * 128, T], F32) if _os.environ.get("L2F_DBG") else None
    for l in range(2):
        bufs = (wqkv[l], w1il[l], w2re[l]) + bufs[3:]
        emit_l2f_inner(p, ms, l, bufs)
        if dbg is not None and l == 0:
            for j in range(ND):
                p.call("gpsimd", "dma_start", dict(
                    out=dbg[j * 128:(j + 1) * 128, :], in_=ms.x_cm[j][:, PAD:PAD + T]),
                    reads=[ms.x_cm[j].key], writes=["l2f_dbg"], dma=True)
    for c0 in range(0, T, TC):
        for j in range(ND):
            p.call("sync", "dma_start", dict(
                out=lnbuf[j].ap, in_=xsp[j * 128:(j + 1) * 128, c0:c0 + TC]),
                reads=["l2f_xorig"], writes=[lnbuf[j]], dma=True)
        ln_chunk(p, ms, c0, TC, [(lnbuf[j].ap, lnbuf[j]) for j in range(ND)])



def host_prep(inputs):
    cdict = dict(host_constants())
    bigs = {}
    f32 = np.float32

    def col128(v):
        return np.ascontiguousarray(v.reshape(-1, 128).T.astype(f32))

    bigs["L0_ffn_W1T"] = np.ascontiguousarray(inputs["L0_ffn_W1"].T.astype(f32))
    bigs["L0_ffn_W2T"] = np.ascontiguousarray(inputs["L0_ffn_W2"].T.astype(f32))
    cdict["L0f_b1"] = col128(inputs["L0_ffn_b1"])
    cdict["L0f_b2"] = col128(inputs["L0_ffn_b2"])
    bigs["L1_ffn_WrT"] = np.ascontiguousarray(inputs["L1_ffn_W_r"].T.astype(f32))
    bigs["L1_ffn_WkT"] = np.ascontiguousarray(inputs["L1_ffn_W_k"].T.astype(f32))
    bigs["L1_ffn_WvT"] = np.ascontiguousarray(inputs["L1_ffn_W_v"].T.astype(f32))
    cdict["L1f_mur"] = col128(inputs["L1_ffn_mu_r"])
    cdict["L1f_muk"] = col128(inputs["L1_ffn_mu_k"])
    W = inputs["L2_attn_MhWqkv"].astype(f32)
    cols = np.arange(3072)
    qcols = np.concatenate([(2 * pr + s // 64) * 192 + (s % 64) + 0 * 1
                            for pr in range(8) for s in [np.arange(128)]])
    kcols = np.concatenate([(2 * pr + s // 64) * 192 + 64 + (s % 64)
                            for pr in range(8) for s in [np.arange(128)]])
    vcols = np.concatenate([h * 192 + 128 + np.arange(64) for h in range(16)])
    perm = np.concatenate([qcols, kcols, vcols])
    bigs["L2_attn_Wre"] = np.ascontiguousarray(W[:, perm])
    bigs["L0_attn_WqT"] = np.ascontiguousarray(inputs["L0_attn_Wq"].T.astype(f32))
    bigs["L0_attn_WkT"] = np.ascontiguousarray(inputs["L0_attn_Wk"].T.astype(f32))
    bigs["L0_attn_WvT"] = np.ascontiguousarray(inputs["L0_attn_Wv"].T.astype(f32))
    bigs["L0_attn_WoT"] = np.ascontiguousarray(inputs["L0_attn_Wo"].T.astype(f32))
    wfT = np.ascontiguousarray(inputs["L0_attn_Wf"].T.astype(f32))
    cdict["L0a_wf2"] = np.concatenate([wfT, wfT], axis=0)
    for l in range(2):
        bigs[f"L2f_wqkv{l}"] = np.ascontiguousarray(inputs["L2_ffn_wqkv"][l].astype(f32))
        bigs[f"L2f_wo{l}"] = np.ascontiguousarray(inputs["L2_ffn_wo"][l].astype(f32))
        w1 = inputs["L2_ffn_w1"][l].astype(f32)
        w1i = np.zeros((1024, 44 * 128), f32)
        for mm in range(22):
            an = min(128, 2730 - mm * 128)
            w1i[:, mm * 256:mm * 256 + an] = w1[:, mm * 128:mm * 128 + an]
            w1i[:, mm * 256 + 128:mm * 256 + 128 + an] = w1[:, 2730 + mm * 128:2730 + mm * 128 + an]
        bigs[f"L2f_w1il{l}"] = w1i
        w2 = inputs["L2_ffn_w2"][l].astype(f32)
        w2r = np.zeros((2816, 1024), f32)
        w2r[0:2730] = w2
        bigs[f"L2f_w2re{l}"] = w2r
    b1 = inputs["L2_ffn_b1"].astype(f32)
    b1i = np.zeros((128, 2, 44), f32)
    for l in range(2):
        for mm in range(22):
            an = min(128, 2730 - mm * 128)
            b1i[0:an, l, 2 * mm] = b1[l, mm * 128:mm * 128 + an]
            b1i[0:an, l, 2 * mm + 1] = b1[l, 2730 + mm * 128:2730 + mm * 128 + an]
    cdict["L2f_b1il"] = np.ascontiguousarray(b1i.reshape(128, 88))
    cdict["L2f_b2"] = np.ascontiguousarray(
        inputs["L2_ffn_b2"].astype(f32).reshape(2, 8, 128).transpose(2, 0, 1).reshape(128, 16))
    qs = inputs["L2_ffn_q_scale"].astype(f32)
    ks = inputs["L2_ffn_k_scale"].astype(f32)
    cdict["L2f_qs"] = np.ascontiguousarray(np.tile(qs.T, (2, 1)))
    cdict["L2f_ks"] = np.ascontiguousarray(np.tile(ks.T, (2, 1)))
    lg = inputs["L2_ffn_ln_g"].astype(f32)
    lb = inputs["L2_ffn_ln_b"].astype(f32)
    cdict["L2f_lng"] = np.ascontiguousarray(lg.reshape(2, 8, 128).transpose(2, 0, 1).reshape(128, 16))
    cdict["L2f_lnb"] = np.ascontiguousarray(lb.reshape(2, 8, 128).transpose(2, 0, 1).reshape(128, 16))
    DIM_HEAD, WINDOW, n = 64, 512, 4096
    half_idx = np.arange(0, DIM_HEAD, 2, dtype=np.float64)
    inv_freq = 1.0 / 10000.0 ** (half_idx / DIM_HEAD)
    tt = np.arange(n, dtype=np.float64)
    freqs = tt[:, None] * inv_freq[None, :]
    freqs = np.concatenate([freqs, freqs], axis=-1)
    base = (half_idx + 0.4 * DIM_HEAD) / (1.4 * DIM_HEAD)
    power = (tt - n // 2) / float(WINDOW // 2)
    scale = base[None, :] ** power[:, None]
    scale = np.concatenate([scale, scale], axis=-1)
    cos, sin = np.cos(freqs), np.sin(freqs)
    cq8 = (8.0 * cos * scale).T
    sq8 = (8.0 * sin * scale).T
    ckd = (cos / scale).T
    skd = (sin / scale).T
    tab = np.concatenate([np.tile(cq8, (2, 1))[None], np.tile(sq8, (2, 1))[None],
                          np.tile(ckd, (2, 1))[None], np.tile(skd, (2, 1))[None]], axis=0)
    bigs["L2f_tabs"] = np.ascontiguousarray(tab.reshape(4, 128, n).transpose(1, 0, 2).reshape(128, 4 * n).astype(BF))
    rp = np.zeros((128, 128), dtype=BF)
    for mcol in range(128):
        if mcol % 64 < 32:
            rp[mcol + 32, mcol] = -1.0
        else:
            rp[mcol - 32, mcol] = 1.0
    cdict["rotPT"] = rp
    bigs["L3_attn_W_in"] = np.ascontiguousarray(inputs["L3_attn_W_in"].astype(f32))
    wx = inputs["L3_attn_W_x"].astype(f32)
    bigs["L3a_WxT"] = np.ascontiguousarray(wx[:, list(range(1, 33)) + [0]])
    bigs["L3_attn_W_out"] = np.ascontiguousarray(inputs["L3_attn_W_out"].astype(f32))
    cdict["L3a_convw"] = np.ascontiguousarray(
        inputs["L3_attn_conv_w"].astype(f32).reshape(16, 128, 4).transpose(1, 0, 2).reshape(128, 64))
    cdict["L3a_convb"] = col128(inputs["L3_attn_conv_b"])
    cdict["L3a_wdt"] = col128(inputs["L3_attn_W_dt"][:, 0])
    cdict["L3a_bdt"] = col128(inputs["L3_attn_b_dt"])
    cdict["L3a_alog"] = inputs["L3_attn_A_log"].astype(f32).reshape(1, 16)
    cdict["L3a_D"] = col128(inputs["L3_attn_D"])
    bigs["L3_ffn_wgT"] = np.ascontiguousarray(inputs["L3_ffn_w_gate"].T).astype(np.int32)
    bigs["L3_ffn_wuT"] = np.ascontiguousarray(inputs["L3_ffn_w_up"].T).astype(np.int32)
    bigs["L3_ffn_wdT"] = np.ascontiguousarray(inputs["L3_ffn_w_down"].T).astype(np.int32)
    return cdict, bigs


def emit_ln_only(p, ms):
    mk_ln_bufs(p, ms)
    for c0 in range(0, T, TC):
        ln_chunk(p, ms, c0, TC, None, skip_sub=True)


UNITS = {
    "l0_ffn": "emit_l0_ffn",
    "l1_ffn": "emit_l1_ffn",
    "l1_attn": "emit_l1_attn",
    "l2_attn": "emit_l2_attn",
    "l3_ffn": "emit_l3_ffn",
    "l0_attn": "emit_l0_attn",
    "l3_attn": "emit_l3_attn",
    "l2_ffn": "emit_l2_ffn",
    "ln_only": "emit_ln_only",
}


PRECAST = [
    ("L0_attn_WqT", [D, D]), ("L0_attn_WkT", [D, D]), ("L0_attn_WvT", [D, D]),
    ("L0_attn_WoT", [D, D]),
    ("L1_ffn_WrT", [D, D]), ("L1_ffn_WkT", [D, D]), ("L1_ffn_WvT", [D, D]),
    ("L2_attn_Wre", [D, 3 * D]),
    ("L2f_wqkv0", [D, 1536]), ("L2f_wo0", [512, D]), ("L2f_w1il0", [D, 44 * 128]),
    ("L2f_w2re0", [2816, D]),
    ("L2f_wqkv1", [D, 1536]), ("L2f_wo1", [512, D]), ("L2f_w1il1", [D, 44 * 128]),
    ("L2f_w2re1", [2816, D]),
    ("L3_attn_W_in", [D, 4096]), ("L3_attn_W_out", [2048, D]), ("L3a_WxT", [2048, 33]),
]
PRECAST_I32 = [
    ("L3_ffn_wgT", [D, 3072]), ("L3_ffn_wuT", [D, 3072]), ("L3_ffn_wdT", [3072, D]),
]


def emit_precast(p, ms):
    ms.w16 = {}
    for name, shape in PRECAST:
        din = p.din(name, shape, F32)
        scr = p.dscratch("w16_" + name, shape, F16)
        p.call("gpsimd", "dma_start", dict(out=scr, in_=din), writes=["w16_" + name], dma=True)
        ms.w16[name] = scr
    for name, shape in PRECAST_I32:
        din = p.din(name, shape, I32)
        scr = p.dscratch("w16_" + name, shape, F16)
        p.call("gpsimd", "dma_start", dict(out=scr, in_=din), writes=["w16_" + name], dma=True)
        ms.w16[name] = scr


def build_program(unit_names, cdict):
    nc = bass.Bass("TRN2", target_bir_lowering=False)
    with contextlib.ExitStack() as stack:
        p = Prog(nc, stack)
        ms = MS()
        xin = p.din("x", [T, D], F32)
        xout = p.dout("out", [T, D], F32)
        ms.xin, ms.xout = xin, xout
        ms.pb = [p.psum([128, 512], F32, f"pb{i}") for i in range(6)]
        ms.pbb = p.psum([128, 1024], F16, "pbb")
        p._mk_dummies()
        emit_load_consts(p, ms, cdict)
        emit_precast(p, ms)
        ms.x_cm = [p.sbuf([128, PAD + T], F16, f"xcm{i}") for i in range(ND)]
        with p.unit_scope():
            emit_prologue(p, ms, xin)
        p.barrier()
        mk_ln_bufs(p, ms)
        for u in unit_names:
            p.barrier()
            with p.unit_scope():
                globals()[UNITS[u]](p, ms)
            p.barrier()
        emit_epilogue(p, ms, xout)
        p.finalize()
    return nc


ALL_UNITS = ["l0_attn", "l0_ffn", "l1_attn", "l1_ffn", "l2_attn", "l2_ffn", "l3_attn", "l3_ffn"]
LAUNCH_GROUPS = [ALL_UNITS]


def kernel(**inputs):
    cdict, bigs = host_prep(inputs)
    x = np.ascontiguousarray(inputs["L0_attn_hidden_states"]).astype(np.float32)
    cur = x
    for group in LAUNCH_GROUPS:
        nc = build_program(group, cdict)
        in_maps = []
        for core in range(8):
            m = {"c_" + k: v for k, v in cdict.items()}
            m.update(bigs)
            m["x"] = np.ascontiguousarray(cur[core])
            in_maps.append(m)
        res = run_bass_kernel_spmd(nc, in_maps, core_ids=list(range(8)))
        cur = np.stack([r["out"] for r in res.results])
    return cur.astype(np.float32)


# version 13 (final)
# speedup vs baseline: 1.0634x; 1.0634x over previous
import contextlib
import numpy as np
import ml_dtypes
import concourse.bass as bass
import concourse.mybir as mybir
from concourse.bass_utils import run_bass_kernel_spmd

F32 = mybir.dt.float32
BF16 = mybir.dt.bfloat16
F16 = mybir.dt.float16
I32 = mybir.dt.int32
AF = mybir.ActivationFunctionType
ALU = mybir.AluOpType
BF = ml_dtypes.bfloat16

T = 4096
D = 1024
PAD = 4
NT = T // 128
ND = D // 128
TC = 512

ENGINES = ("tensor", "vector", "scalar", "gpsimd", "sync")


class ColView:
    def __init__(self, buf, c0):
        self.key = buf.key
        self.h = buf.h
        self.c0 = c0

    def __getitem__(self, idx):
        rows, cols = idx
        return self.h[rows, cols.start + self.c0:cols.stop + self.c0]

    @property
    def ap(self):
        raise RuntimeError("use sliced access")


class Buf:
    __slots__ = ("key", "h")

    def __init__(self, key, h):
        self.key = key
        self.h = h

    def __getitem__(self, idx):
        return self.h[idx]

    @property
    def ap(self):
        return self.h[:]


class Prog:

    def __init__(self, nc, stack):
        self.nc = nc
        self.stack = stack
        self.ops = {e: [] for e in ENGINES}
        self.count = {}
        self.state = {}
        self.waited = {e: {} for e in ENGINES}
        self.sems = {}
        self.dma_chan = {}
        self.n_dma_chans = 0
        self.n_names = 0
        self.same_engine_sync = True
        self.same_engine_window = 6
        self.pending_barrier = {}

    def _name(self, prefix):
        self.n_names += 1
        return f"{prefix}{self.n_names}"

    def sbuf(self, shape, dtype, name=None):
        stk = self.scopes[-1] if getattr(self, "scopes", None) else self.stack
        h = stk.enter_context(self.nc.sbuf_tensor(name or self._name("sb"), list(shape), dtype))
        return Buf(id(h), h)

    @contextlib.contextmanager
    def unit_scope(self):
        if not hasattr(self, "scopes"):
            self.scopes = []
        with contextlib.ExitStack() as stk:
            self.scopes.append(stk)
            try:
                yield
            finally:
                self.scopes.pop()

    def psum(self, shape, dtype, name=None):
        h = self.stack.enter_context(self.nc.psum_tensor(name or self._name("ps"), list(shape), dtype))
        return Buf(id(h), h)

    def din(self, name, shape, dtype):
        return self.nc.dram_tensor(name, list(shape), dtype, kind="ExternalInput").ap()

    def dout(self, name, shape, dtype):
        return self.nc.dram_tensor(name, list(shape), dtype, kind="ExternalOutput").ap()

    def dscratch(self, name, shape, dtype):
        return self.nc.dram_tensor(name, list(shape), dtype, kind="Internal").ap()

    def _keys(self, bufs):
        out = []
        for b in bufs:
            if b is None:
                continue
            out.append(getattr(b, "key", b) if not isinstance(b, (str, tuple)) else b)
        return out

    def _chan(self, key):
        if key not in self.dma_chan:
            self.dma_chan[key] = f"dma{len(self.dma_chan)}"
        return self.dma_chan[key]

    def call(self, eng, method, kwargs, reads=(), writes=(), dma=False, init=False):
        return self.op(eng, (method, kwargs), reads=reads, writes=writes, dma=dma, init=init)

    def op(self, eng, fn, reads=(), writes=(), dma=False, init=False):
        rk, wk = self._keys(reads), self._keys(writes)
        need = {}
        pend = self.pending_barrier.pop(eng, None)
        if pend:
            for stream, val in pend:
                if stream != eng and need.get(stream, 0) < val:
                    need[stream] = val
        for k in rk:
            st = self.state.get(k)
            if st and st[0]:
                self._need(eng, st[0], need)
        for k in wk:
            st = self.state.get(k)
            if st:
                if st[0]:
                    self._need(eng, st[0], need)
                for t in st[1]:
                    self._need(eng, t, need)
        waits = []
        wd = self.waited[eng]
        for stream, val in need.items():
            if wd.get(stream, 0) < val:
                wd[stream] = val
                waits.append((stream, val))
        if dma:
            if init:
                stream = "dinit"
            else:
                assert wk, "dma op must declare the written buffer"
                stream = self._chan(wk[0])
            val = self.count.get(stream, 0) + 16
            self.count[stream] = val
            token = (stream, val)
            incs = [(stream, 16)]
        else:
            val = self.count.get(eng, 0) + 1
            self.count[eng] = val
            token = (eng, val)
            incs = [(eng, 1)]
        self.ops[eng].append((waits, fn, incs))
        for k in rk:
            st = self.state.setdefault(k, (None, []))
            st[1].append(token)
        for k in wk:
            if dma and init:
                self.state[k] = (("dinit", self.count["dinit"]), [])
            else:
                self.state[k] = (token, [])
        return token

    def _need(self, eng, token, need):
        stream, val = token
        if stream == eng == "tensor" and not getattr(self, "pe_sync", False):
            return
        if stream == eng:
            if not self.same_engine_sync:
                return
            if self.count.get(eng, 0) - val >= self.same_engine_window:
                return
        if stream == "dinit":
            val = self.count.get("dinit", 0)
        if need.get(stream, 0) < val:
            need[stream] = val

    def _mk_dummies(self):
        if getattr(self, "_dumb", None) is None:
            self._dumb = {e: self.sbuf([1, 16], BF16, "dummy_" + e) for e in ("tensor", "vector", "scalar", "gpsimd")}
            self._dumps = self.psum([2, 8], F32, "dummyps")
            dt_ = self._dumb["tensor"]
            self._dumb_init = self.op("vector", lambda eh: eh.memset(dt_.ap, 0.0), writes=[dt_])

    def barrier(self):
        self._mk_dummies()
        participants = [e for e in ENGINES if self.ops[e]]
        if not participants:
            return
        dma_final = {}
        for k, chan in self.dma_chan.items():
            dma_final[chan] = self.count.get(chan, 0)
        if self.count.get("dinit", 0):
            dma_final["dinit"] = self.count["dinit"]
        ps = self._dumps
        for e in participants:
            wd = self.waited[e]
            waits = []
            if self.count.get(e, 0) > 0:
                waits.append((e, self.count[e]))
            if self.count.get("bar", 0) > 0:
                waits.append(("bar", self.count["bar"]))
            for chan, val in dma_final.items():
                if val > 0 and wd.get(chan, 0) < val:
                    wd[chan] = val
                    waits.append((chan, val))
            if e == "tensor":
                iv, it = self._dumb_init
                if wd.get(iv, 0) < it:
                    wd[iv] = it
                    waits.append((iv, it))
                dd = self._dumb[e]
                fn = lambda eh, dd=dd: eh.matmul(out=ps[0:1, 0:1], lhsT=dd[0:1, 0:1], rhs=dd[0:1, 0:1], start=True, stop=True)
            elif e == "vector":
                dd = self._dumb[e]
                fn = lambda eh, dd=dd: eh.memset(dd[0:1, 0:4], 0.0)
            elif e == "scalar":
                dd = self._dumb[e]
                fn = lambda eh, dd=dd: eh.memzero(dd[0:1, 0:4])
            elif e == "gpsimd":
                dd = self._dumb[e]
                fn = lambda eh, dd=dd: eh.memset(dd[0:1, 0:4], 0.0)
            else:
                fn = lambda eh: eh.nop()
            self.ops[e].append((waits, fn, [("bar", 1)]))
        self.count["bar"] = self.count.get("bar", 0) + len(participants)
        target = self.count["bar"]
        snapshot = [("bar", target)]
        for e in ENGINES:
            self.pending_barrier[e] = snapshot
        self.state = {}
        self.dma_chan = {}
        for e in ENGINES:
            self.waited[e] = dict(self.waited[e])
            self.waited[e]["bar"] = 0

    def finalize(self):
        fin = [(s, v) for s, v in self.count.items() if s != "sync" and v > 0]
        self.count["sync"] = self.count.get("sync", 0) + 1
        self.ops["sync"].append(
            ([(s, v) for (s, v) in fin], lambda eh: eh.nop(), [("sync", 1)]))
        for stream in self.count:
            self.sems[stream] = self.stack.enter_context(self.nc.semaphore("sem_" + stream))
        block = self.stack.enter_context(self.nc.Block())
        for e in ENGINES:
            if not self.ops[e]:
                continue
            ops = self.ops[e]
            sems = self.sems

            def body(engh, ops=ops, sems=sems):
                for waits, fn, incs in ops:
                    for stream, val in waits:
                        engh.wait_ge(sems[stream], val)
                    if isinstance(fn, tuple):
                        inst = getattr(engh, fn[0])(**fn[1])
                    else:
                        inst = fn(engh)
                    for stream, n in incs:
                        inst.then_inc(sems[stream], n)
            getattr(block, e)(body)



def host_constants():
    c = {}
    c["identity"] = np.eye(128, dtype=np.float16)
    c["ones128"] = np.ones((128, 128), dtype=BF)
    c["ones128f"] = np.ones((128, 128), dtype=np.float32)
    c["negones_div"] = np.full((128, 128), -1.0 / 128, dtype=BF)
    c["ones1"] = np.ones((1, 128), dtype=np.float32)
    c["ones1b"] = np.ones((1, 128), dtype=BF)
    misc = np.zeros((128, 8), dtype=np.float32)
    misc[:, 0] = 1e-5
    misc[:, 1] = 1e-24
    misc[:, 2] = 1e-6
    misc[:, 3] = 0.5
    misc[:, 4] = -0.5
    misc[:, 5] = -50.0
    misc[:, 6] = float(40 * np.log(2.0))
    misc[:, 7] = 1.0
    c["misc"] = misc
    c["identityf"] = np.eye(128, dtype=np.float32)
    selg = np.zeros((128, 16 * 128), dtype=BF)
    sumsel = np.zeros((128, 16 * 128), dtype=BF)
    for g in range(16):
        for s in range(16):
            for di in range(8):
                selg[g * 8 + di, g * 128 + s * 8 + di] = 1
                sumsel[s * 8 + di, g * 128 + g * 8 + di] = 1
    c["L3a_selg"] = selg
    c["L3a_sumsel"] = sumsel
    selbc = np.zeros((33, 3 * 128), dtype=BF)
    for mcol in range(128):
        selbc[32, 0 + mcol] = 1
        selbc[mcol // 8, 128 + mcol] = 1
        selbc[16 + mcol // 8, 256 + mcol] = 1
    c["L3a_selbc"] = selbc
    selB16 = np.zeros((16, 128), dtype=BF)
    for mcol in range(128):
        selB16[mcol // 8, mcol] = 1
    c["selB16"] = selB16
    m = np.zeros((128, 4 * 512), dtype=BF)
    kk = np.arange(128)[:, None]
    qq = np.arange(512)[None, :]
    for dq in range(4):
        blk = (kk + dq * 128 <= qq).astype(BF)
        m[:, dq * 512:(dq + 1) * 512] = blk
    c["causal01"] = m
    mf = np.zeros((128, 4 * 512), dtype=BF)
    for blk in range(4):
        mf[:, blk * 512:(blk + 1) * 512] = (kk + blk * 128 >= qq).astype(BF)
    c["causal01_far"] = mf
    ca = np.where(np.concatenate([(kk + blk * 128 <= qq) for blk in range(4)], axis=1), 0.0, -1e9).astype(BF)
    cfa = np.where(np.concatenate([(kk + blk * 128 >= qq) for blk in range(4)], axis=1), 0.0, -1e9).astype(BF)
    c["causaladd"] = ca
    c["causaladd_far"] = cfa
    return c



class MS:
    pass


LAZY_CONSTS = {"causal01", "causal01_far", "causaladd", "causaladd_far", "L3a_selg", "L3a_sumsel", "L3a_selbc"}


def load_lazy_const(p, ms, name):
    arr = ms.cdict[name]
    dt = {np.dtype(np.float32): F32, np.dtype(BF): BF16, np.dtype(np.int32): I32, np.dtype(np.float16): F16}[arr.dtype]
    dram = ms.c_drams[name]
    b = p.sbuf(arr.shape, dt)
    p.call("sync", "dma_start", dict(out=b.ap, in_=dram), writes=[b], dma=True, init=True)
    return b


def emit_load_consts(p, ms, consts):
    nc = p.nc
    ms.c = {}
    ms.cdict = consts
    ms.c_drams = {}
    for name, arr in consts.items():
        dt = {np.dtype(np.float32): F32, np.dtype(BF): BF16, np.dtype(np.int32): I32, np.dtype(np.float16): F16}[arr.dtype]
        dram = p.din("c_" + name, arr.shape, dt)
        ms.c_drams[name] = dram
        if name in LAZY_CONSTS:
            continue
        b = p.sbuf(arr.shape, dt, "cs_" + name)
        p.call("sync", "dma_start", dict(out=b.ap, in_=dram), reads=[], writes=[b], dma=True, init=True)
        ms.c[name] = b


def emit_prologue(p, ms, xin):
    for t in ms.x_cm:
        p.call("vector", "memset", dict(ap=t[:, 0:PAD], constant=0.0), writes=[t.key])
    xtm = p.sbuf([128, D], F16, "prx")
    xf = p.sbuf([128, D], F32, "prxf")
    ident = ms.c["identity"]
    ptt = ms.pbb
    for i in range(NT):
        p.call("sync", "dma_start", dict(out=xf.ap, in_=xin[i * 128:(i + 1) * 128, :]),
               writes=[xf], dma=True)
        p.call("scalar", "copy", dict(out=xtm.ap, in_=xf.ap), reads=[xf], writes=[xtm])
        for j in range(ND):
            p.call("tensor", "transpose", dict(
                out=ptt[:, j * 128:(j + 1) * 128], in_=xtm[:, j * 128:(j + 1) * 128],
                identity=ident.ap), reads=[xtm, ident], writes=[ptt])
        for j in range(ND):
            p.call("vector", "tensor_copy", dict(
                out=ms.x_cm[j][:, PAD + i * 128:PAD + (i + 1) * 128],
                in_=ptt[:, j * 128:(j + 1) * 128]),
                reads=[ptt], writes=[ms.x_cm[j].key])


def emit_epilogue(p, ms, xout):
    ident = ms.c["identity"]
    ptt = ms.pbb
    otile = p.sbuf([128, D], F32, "epo")
    for i in range(NT):
        for j in range(ND):
            p.call("tensor", "transpose", dict(
                out=ptt[:, j * 128:(j + 1) * 128],
                in_=ms.x_cm[j][:, PAD + i * 128:PAD + (i + 1) * 128], identity=ident.ap),
                reads=[ms.x_cm[j].key, ident], writes=[ptt])
        p.call("scalar", "copy", dict(out=otile.ap, in_=ptt.ap), reads=[ptt], writes=[otile])
        p.call("gpsimd", "dma_start", dict(out=xout[i * 128:(i + 1) * 128, :], in_=otile.ap),
               reads=[otile], writes=["xout"], dma=True)



def mk_ln_bufs(p, ms):
    if getattr(ms, "ln_r", None) is None:
        ms.ln_r = [p.sbuf([128, TC], F16, f"lnr{j}") for j in range(ND)]
        ms.ln_rsq = [p.sbuf([128, TC], BF16, "lnrsq0"), p.sbuf([128, TC], BF16, "lnrsq1")]
        ms.ln_m = p.sbuf([128, TC], F16, "lnm")
        ms.ln_msq = p.sbuf([128, TC], F32, "lnmsq")
        ms.ln_var = p.sbuf([128, TC], F32, "lnvar")
        ms.ln_rstd = p.sbuf([128, TC], F32, "lnrstd")
        ms.ps_s = ms.pb[0]
        ms.ps_q = ms.pb[1]
        ms.ps_rc = [ms.pb[2], ms.pb[3]]


def ln_chunk(p, ms, c0, w, subs, skip_sub=False):
    mk_ln_bufs(p, ms)
    ones = ms.c["ones128"]
    ident = ms.c["identity"]
    ndiv = ms.c["negones_div"]
    misc = ms.c["misc"]
    for j in range(ND):
        xs = ms.x_cm[j]
        rj = ms.ln_r[j]
        if skip_sub:
            p.call("vector", "tensor_copy", dict(
                out=rj[:, 0:w], in_=xs[:, PAD + c0:PAD + c0 + w]),
                reads=[xs.key], writes=[rj])
        else:
            sap, sdep = subs[j]
            p.call("vector", "tensor_tensor", dict(
                out=rj[:, 0:w], in0=xs[:, PAD + c0:PAD + c0 + w], in1=sap,
                op=ALU.add), reads=[xs.key, sdep], writes=[rj])
    for j in range(ND):
        rq = ms.ln_rsq[j % 2]
        p.call("scalar", "activation", dict(
            out=rq[:, 0:w], in_=ms.ln_r[j][:, 0:w], func=AF.Square),
            reads=[ms.ln_r[j]], writes=[rq])
        p.call("tensor", "matmul", dict(
            out=ms.ps_s[:, 0:w], lhsT=ones.ap, rhs=ms.ln_r[j][:, 0:w],
            start=(j == 0), stop=(j == ND - 1)),
            reads=[ms.ln_r[j], ones], writes=[ms.ps_s])
        p.call("tensor", "matmul", dict(
            out=ms.ps_q[:, 0:w], lhsT=ones.ap, rhs=rq[:, 0:w],
            start=(j == 0), stop=(j == ND - 1)),
            reads=[rq, ones], writes=[ms.ps_q])
    p.call("vector", "tensor_scalar_mul", dict(
        out=ms.ln_m[:, 0:w], in0=ms.ps_s[:, 0:w], scalar1=1.0 / D),
        reads=[ms.ps_s], writes=[ms.ln_m])
    p.call("vector", "tensor_tensor", dict(
        out=ms.ln_msq[:, 0:w], in0=ms.ln_m[:, 0:w], in1=ms.ln_m[:, 0:w], op=ALU.mult),
        reads=[ms.ln_m], writes=[ms.ln_msq])
    p.call("vector", "scalar_tensor_tensor", dict(
        out=ms.ln_var[:, 0:w], in0=ms.ps_q[:, 0:w], scalar=1.0 / D, in1=ms.ln_msq[:, 0:w],
        op0=ALU.mult, op1=ALU.subtract),
        reads=[ms.ps_q, ms.ln_msq], writes=[ms.ln_var])
    p.call("scalar", "activation", dict(
        out=ms.ln_var[:, 0:w], in_=ms.ln_var[:, 0:w], func=AF.Ln, bias=misc[:, 0:1]),
        reads=[ms.ln_var, misc], writes=[ms.ln_var])
    p.call("scalar", "activation", dict(
        out=ms.ln_rstd[:, 0:w], in_=ms.ln_var[:, 0:w], func=AF.Exp, scale=-0.5),
        reads=[ms.ln_var], writes=[ms.ln_rstd])
    for j in range(ND):
        rc = ms.ps_rc[j % 2]
        p.call("tensor", "matmul", dict(
            out=rc[:, 0:w], lhsT=ident.ap, rhs=ms.ln_r[j][:, 0:w], start=True, stop=False),
            reads=[ms.ln_r[j], ident], writes=[rc])
        p.call("tensor", "matmul", dict(
            out=rc[:, 0:w], lhsT=ndiv.ap, rhs=ms.ln_m[:, 0:w], start=False, stop=True),
            reads=[ms.ln_m, ndiv], writes=[rc])
        p.call("vector", "tensor_tensor", dict(
            out=ms.x_cm[j][:, PAD + c0:PAD + c0 + w], in0=rc[:, 0:w],
            in1=ms.ln_rstd[:, 0:w], op=ALU.mult),
            reads=[rc, ms.ln_rstd], writes=[ms.x_cm[j].key])



FP8 = mybir.dt.float8e4


def quantize_weight(p, ms, wT_dram, K, N, qtiles, wstage, ab):
    nk = K // 128
    NH = min(N, 512)
    nh = N // NH
    acc = p.sbuf([128, nk * nh], F32, p._name("qacc"))
    inv = p.sbuf([128, 1], F32, p._name("qinv"))
    tot = p.sbuf([128, 1], F32, p._name("qtot"))
    pst = ms.pb[5]
    for k in range(nk):
        for h in range(nh):
            st = wstage[(k * nh + h) % 2]
            p.call("sync", "dma_start", dict(
                out=st[:, 0:NH], in_=wT_dram[k * 128:(k + 1) * 128, h * NH:(h + 1) * NH]),
                writes=[st], dma=True)
            p.call("vector", "tensor_reduce", dict(
                out=acc[:, k * nh + h:k * nh + h + 1], in_=st[:, 0:NH],
                axis=mybir.AxisListType.X, op=ALU.add, apply_absolute_value=True),
                reads=[st], writes=[(acc.key, k * nh + h)])
    p.call("vector", "tensor_reduce", dict(
        out=tot.ap, in_=acc[:, 0:nk * nh], axis=mybir.AxisListType.X, op=ALU.add),
        reads=[acc] + [(acc.key, i) for i in range(nk * nh)], writes=[tot])
    p.call("tensor", "matmul", dict(out=pst[:, 0:1], lhsT=ms.c["ones128f"].ap, rhs=tot.ap,
                                    start=True, stop=True),
           reads=[tot, ms.c["ones128f"]], writes=[pst])
    p.call("vector", "tensor_scalar", dict(
        out=inv.ap, in0=pst[:, 0:1], scalar1=1.0 / (K * N), scalar2=1e-5, op0=ALU.mult, op1=ALU.add),
        reads=[pst], writes=[inv])
    p.call("vector", "reciprocal", dict(out=inv.ap, in_=inv.ap), reads=[inv], writes=[inv])
    for k in range(nk):
        for h in range(nh):
            st = wstage[(k * nh + h) % 2]
            p.call("sync", "dma_start", dict(
                out=st[:, 0:NH], in_=wT_dram[k * 128:(k + 1) * 128, h * NH:(h + 1) * NH]),
                writes=[st], dma=True)
            p.call("vector", "tensor_scalar", dict(
                out=ab[0][:, 0:NH], in0=st[:, 0:NH], scalar1=inv.ap, scalar2=0.5,
                op0=ALU.mult, op1=ALU.is_ge),
                reads=[st, inv], writes=[ab[0]])
            p.call("vector", "tensor_scalar", dict(
                out=ab[1][:, 0:NH], in0=st[:, 0:NH], scalar1=inv.ap, scalar2=-0.5,
                op0=ALU.mult, op1=ALU.is_le),
                reads=[st, inv], writes=[ab[1]])
            p.call("vector", "tensor_tensor", dict(
                out=qtiles[k][:, h * NH:(h + 1) * NH], in0=ab[0][:, 0:NH],
                in1=ab[1][:, 0:NH], op=ALU.subtract),
                reads=[ab[0], ab[1]], writes=[qtiles[k]])


def emit_l0_ffn(p, ms):
    FD = 4096
    mk_ln_bufs(p, ms)
    w1q = [p.sbuf([128, FD], FP8, f"w1q{k}") for k in range(ND)]
    w2q = [p.sbuf([128, D], FP8, f"w2q{k}") for k in range(FD // 128)]
    wstage = [p.sbuf([128, 512], F32, "l0fst0"), p.sbuf([128, 512], F32, "l0fst1")]
    w1T = p.din("L0_ffn_W1T", [D, FD], F32)
    w2T = p.din("L0_ffn_W2T", [FD, D], F32)
    b1 = ms.c["L0f_b1"]
    b2 = ms.c["L0f_b2"]
    CW = 512
    hbuf = p.sbuf([128, (FD // 128) * 512], BF16, "l0fh")
    ab = [p.sbuf([128, 512], BF16, "l0fab0"), p.sbuf([128, 512], BF16, "l0fab1")]
    quantize_weight(p, ms, w1T, D, FD, w1q, wstage, ab)
    quantize_weight(p, ms, w2T, FD, D, w2q, wstage, ab)
    xbf = [p.sbuf([128, CW], BF16, f"l0fxb{j}") for j in range(ND)]
    psh = [ms.pb[2], ms.pb[3], ms.pb[4]]
    pso = [ms.pb[5], ms.pb[0], ms.pb[1]]
    sub = [p.sbuf([128, CW], F16, f"l0fs{j}") for j in range(ND)]
    for c0 in range(0, T, CW):
        for k in range(ND):
            p.call("scalar", "copy", dict(out=xbf[k].ap, in_=ms.x_cm[k][:, PAD + c0:PAD + c0 + CW]),
                   reads=[ms.x_cm[k].key], writes=[xbf[k]])
        for ft in range(FD // 128):
            ph = psh[ft % 3]
            for k in range(ND):
                p.call("tensor", "matmul", dict(
                    out=ph[:, 0:CW], lhsT=w1q[k][:, ft * 128:(ft + 1) * 128],
                    rhs=xbf[k].ap,
                    start=(k == 0), stop=(k == ND - 1)),
                    reads=[w1q[k], xbf[k]], writes=[ph])
            p.call("scalar", "activation", dict(
                out=hbuf[:, ft * CW:(ft + 1) * CW], in_=ph[:, 0:CW], func=AF.Relu,
                bias=b1[:, ft:ft + 1]),
                reads=[ph, b1], writes=[(hbuf.key, ft)])
        for j in range(ND):
            po = pso[j % 3]
            for k in range(FD // 128):
                p.call("tensor", "matmul", dict(
                    out=po[:, 0:CW], lhsT=w2q[k][:, j * 128:(j + 1) * 128],
                    rhs=hbuf[:, k * CW:(k + 1) * CW],
                    start=(k == 0), stop=(k == FD // 128 - 1)),
                    reads=[w2q[k], (hbuf.key, k)], writes=[po])
            p.call("vector", "tensor_scalar", dict(
                out=sub[j].ap, in0=po[:, 0:CW], scalar1=b2[:, j:j + 1], scalar2=None,
                op0=ALU.add), reads=[po, b2], writes=[sub[j]])
        ln_chunk(p, ms, c0, CW, [(sub[j][:, 0:CW], sub[j]) for j in range(ND)])



def load_weight_bf16(p, ms, dram, K, N, tiles, wstage, tag):
    nk = K // 128
    for k in range(nk):
        st = wstage[k % 2]
        p.call("sync", "dma_start", dict(out=st[:, 0:N], in_=dram[k * 128:(k + 1) * 128, :]),
               writes=[st], dma=True)
        p.call("scalar", "copy", dict(out=tiles[k].ap, in_=st[:, 0:N]),
               reads=[st], writes=[tiles[k]])


def emit_l1_ffn(p, ms):
    wrT = ms.w16["L1_ffn_WrT"]
    wkT = ms.w16["L1_ffn_WkT"]
    wvT = ms.w16["L1_ffn_WvT"]
    mur = ms.c["L1f_mur"]
    muk = ms.c["L1f_muk"]
    wr = [p.sbuf([128, D], F16, f"l1fwr{k}") for k in range(ND)]
    wk = [p.sbuf([128, D], F16, f"l1fwk{k}") for k in range(ND)]
    wv = [p.sbuf([128, D], F16, f"l1fwv{k}") for k in range(ND)]
    for tiles, srcw, srck in ((wr, wrT, "w16_L1_ffn_WrT"), (wk, wkT, "w16_L1_ffn_WkT"),
                              (wv, wvT, "w16_L1_ffn_WvT")):
        for k in range(ND):
            p.call("sync", "dma_start", dict(out=tiles[k].ap, in_=srcw[k * 128:(k + 1) * 128, :]),
                   reads=[srck], writes=[tiles[k]], dma=True, init=True)
    CW = TC
    dbuf = p.sbuf([128, CW], F32, "l1fd")
    xr = [p.sbuf([128, CW], F16, f"l1fxr{j}") for j in range(ND)]
    xk = [p.sbuf([128, CW], F16, f"l1fxk{j}") for j in range(ND)]
    kact = [p.sbuf([128, CW], F16, f"l1fka{j}") for j in range(ND)]
    rbuf = [p.sbuf([128, CW], F16, f"l1fr{j}") for j in range(ND)]
    sub = [p.sbuf([128, CW], F16, f"l1fs{j}") for j in range(ND)]
    ps = [ms.pb[4], ms.pb[5], ms.pb[0]]
    xbnd = [p.sbuf([128, 8], F16, f"l1f_xb{j}") for j in range(ND)]
    for j in range(ND):
        for bi, b0 in enumerate(range(CW, T, CW)):
            p.call("vector", "tensor_copy", dict(
                out=xbnd[j][:, bi:bi + 1], in_=ms.x_cm[j][:, PAD + b0 - 1:PAD + b0]),
                reads=[ms.x_cm[j].key], writes=[xbnd[j]])
    prevb = p.sbuf([128, CW], F16, "l1f_prev")
    for c0 in range(0, T, CW):
        for j in range(ND):
            xs = ms.x_cm[j]
            cur = xs[:, PAD + c0:PAD + c0 + CW]
            p.call("vector", "tensor_copy", dict(
                out=prevb.ap, in_=xs[:, PAD + c0 - 1:PAD + c0 - 1 + CW]),
                reads=[xs.key], writes=[prevb])
            if c0 > 0:
                p.call("vector", "tensor_copy", dict(
                    out=prevb[:, 0:1], in_=xbnd[j][:, c0 // CW - 1:c0 // CW]),
                    reads=[xbnd[j]], writes=[prevb])
            prev = prevb.ap
            p.call("vector", "tensor_tensor", dict(
                out=dbuf.ap, in0=cur, in1=prev, op=ALU.subtract),
                reads=[xs.key, prevb], writes=[dbuf])
            p.call("vector", "scalar_tensor_tensor", dict(
                out=xr[j].ap, in0=dbuf.ap, scalar=mur[:, j:j + 1], in1=prev,
                op0=ALU.mult, op1=ALU.add),
                reads=[dbuf, mur, prevb], writes=[xr[j]])
            p.call("vector", "scalar_tensor_tensor", dict(
                out=xk[j].ap, in0=dbuf.ap, scalar=muk[:, j:j + 1], in1=prev,
                op0=ALU.mult, op1=ALU.add),
                reads=[dbuf, muk, prevb], writes=[xk[j]])
        for j in range(ND):
            po = ps[j % 3]
            for k in range(ND):
                p.call("tensor", "matmul", dict(
                    out=po[:, 0:CW], lhsT=wk[k][:, j * 128:(j + 1) * 128], rhs=xk[k].ap,
                    start=(k == 0), stop=(k == ND - 1)),
                    reads=[wk[k], xk[k]], writes=[po])
            p.call("scalar", "activation", dict(
                out=kact[j].ap, in_=po[:, 0:CW], func=AF.Relu),
                reads=[po], writes=[kact[j]])
            p.call("scalar", "activation", dict(
                out=kact[j].ap, in_=kact[j].ap, func=AF.Square),
                reads=[kact[j]], writes=[kact[j]])
        for j in range(ND):
            po = ps[j % 3]
            for k in range(ND):
                p.call("tensor", "matmul", dict(
                    out=po[:, 0:CW], lhsT=wr[k][:, j * 128:(j + 1) * 128], rhs=xr[k].ap,
                    start=(k == 0), stop=(k == ND - 1)),
                    reads=[wr[k], xr[k]], writes=[po])
            p.call("scalar", "activation", dict(
                out=rbuf[j].ap, in_=po[:, 0:CW], func=AF.Sigmoid),
                reads=[po], writes=[rbuf[j]])
        for j in range(ND):
            po = ps[j % 3]
            for k in range(ND):
                p.call("tensor", "matmul", dict(
                    out=po[:, 0:CW], lhsT=wv[k][:, j * 128:(j + 1) * 128], rhs=kact[k].ap,
                    start=(k == 0), stop=(k == ND - 1)),
                    reads=[wv[k], kact[k]], writes=[po])
            p.call("vector", "tensor_tensor", dict(
                out=sub[j].ap, in0=po[:, 0:CW], in1=rbuf[j].ap, op=ALU.mult),
                reads=[po, rbuf[j]], writes=[sub[j]])
        ln_chunk(p, ms, c0, CW, [(sub[j].ap, sub[j]) for j in range(ND)])



def emit_l1_attn(p, ms):
    H, HD = 16, 64
    QC = 512
    NKT = NT
    att = [p.sbuf([128, T], F16, f"l1a_att{j}") for j in range(ND)]
    ptb = p.sbuf([128, NKT * 256], BF16, "l1a_pt")
    vau = [p.sbuf([128, NKT * 128], F16, "l1a_vau0"), p.sbuf([128, NKT * 128], F16, "l1a_vau1")]
    lnn = p.sbuf([64, QC], F32, "l1a_lnn")
    recip = p.sbuf([64, QC], F32, "l1a_recip")
    ident = ms.c["identity"]
    misc = ms.c["misc"]
    mask = load_lazy_const(p, ms, "causal01")
    psS = [ms.pb[4], ms.pb[5], ms.pb[0]]
    psA = [ms.pb[2], ms.pb[3]]
    LN2_40 = float(40 * np.log(2.0))
    for jh in range(ND):
        for v in vau:
            p.call("vector", "memset", dict(ap=v.ap, constant=1.0), writes=[v])
        for i in range(NT):
            p.call("tensor", "transpose", dict(
                out=ms.pbb[:, 0:128], in_=ms.x_cm[jh][:, PAD + i * 128:PAD + (i + 1) * 128],
                identity=ident.ap), reads=[ms.x_cm[jh].key, ident], writes=[ms.pbb])
            for hp in range(2):
                p.call("vector", "tensor_copy", dict(
                    out=vau[hp][:, i * 128:i * 128 + 64],
                    in_=ms.pbb[:, hp * 64:hp * 64 + 64]),
                    reads=[ms.pbb], writes=[vau[hp]])
        for hp in range(2):
            h = 2 * jh + hp
            ro = hp * 64
            xh = ms.x_cm[jh]
            for q0 in range(0, T, QC):
                nkt = q0 // 128 + QC // 128
                pA = psA[(q0 // QC) % 2]
                for kh0 in range(0, nkt, 16):
                    kts = list(range(kh0, min(kh0 + 16, nkt)))
                    for kt in kts:
                        ki = kt - kh0
                        pS = psS[ki % 3]
                        p.call("tensor", "matmul", dict(
                            out=pS[:, 0:QC], lhsT=xh[ro:ro + 64, PAD + kt * 128:PAD + (kt + 1) * 128],
                            rhs=xh[ro:ro + 64, PAD + q0:PAD + q0 + QC], start=True, stop=True),
                            reads=[xh.key], writes=[pS])
                        p.call("scalar", "activation", dict(
                            out=ptb[:, ki * QC:(ki + 1) * QC], in_=pS[:, 0:QC], func=AF.Exp,
                            scale=0.125, bias=misc[:, 5:6]),
                            reads=[pS, misc], writes=[(ptb.key, ki)])
                        dq = kt * 128 - q0
                        if 0 <= dq < QC:
                            p.call("vector", "tensor_tensor", dict(
                                out=ptb[:, ki * QC:(ki + 1) * QC],
                                in0=ptb[:, ki * QC:(ki + 1) * QC],
                                in1=mask[:, (dq // 128) * 512:(dq // 128) * 512 + QC],
                                op=ALU.mult),
                                reads=[(ptb.key, ki), mask], writes=[(ptb.key, ki)])
                    for kt in kts:
                        ki = kt - kh0
                        p.call("tensor", "matmul", dict(
                            out=pA[:, 0:QC], lhsT=vau[hp][:, kt * 128:(kt + 1) * 128],
                            rhs=ptb[:, ki * QC:(ki + 1) * QC],
                            start=(kt == 0), stop=(kt == nkt - 1)),
                            reads=[vau[hp], (ptb.key, ki)], writes=[pA])
                p.call("scalar", "activation", dict(
                    out=lnn.ap, in_=pA[64:128, 0:QC], func=AF.Ln, scale=float(2.0 ** 40)),
                    reads=[pA], writes=[lnn])
                p.call("scalar", "activation", dict(
                    out=recip.ap, in_=lnn.ap, func=AF.Exp, scale=-1.0, bias=misc[0:64, 6:7]),
                    reads=[lnn, misc], writes=[recip])
                p.call("vector", "tensor_tensor", dict(
                    out=att[jh][ro:ro + 64, q0:q0 + QC], in0=pA[0:64, 0:QC],
                    in1=recip.ap, op=ALU.mult),
                    reads=[pA, recip], writes=[att[jh].key])
    for c0 in range(0, T, TC):
        ln_chunk(p, ms, c0, TC, [(att[j][:, c0:c0 + TC], att[j].key) for j in range(ND)])



def emit_l2_attn(p, ms):
    H, HD = 16, 64
    QC = 512
    wq = ms.w16["L2_attn_Wre"]
    wqk = "w16_L2_attn_Wre"
    att = [p.sbuf([128, T], F16, f"l2a_att{j}") for j in range(ND)]
    ptb = p.sbuf([128, 16 * QC], BF16, "l2a_pt")
    qbuf = p.sbuf([128, T], F16, "l2a_q")
    kbuf = p.sbuf([128, T], F16, "l2a_k")
    vau = p.sbuf([128, NT * 128], F16, "l2a_vau")
    wst = p.sbuf([128, 256], F32, "l2a_wst")
    wpair = [p.sbuf([128, 256], F16, f"l2a_wp{k}") for k in range(ND)]
    wvh = [p.sbuf([128, 64], F16, f"l2a_wv{k}") for k in range(ND)]
    lnn = p.sbuf([64, QC], F32, "l2a_lnn")
    recip = p.sbuf([64, QC], F32, "l2a_recip")
    misc = ms.c["misc"]
    psS = [ms.pb[4], ms.pb[5], ms.pb[1]]
    psA = [ms.pb[2], ms.pb[3]]
    psV = ms.pb[0]
    for pr in range(H // 2):
        for k in range(ND):
            p.call("sync", "dma_start", dict(
                out=wpair[k][:, 0:128], in_=wq[k * 128:(k + 1) * 128, pr * 128:(pr + 1) * 128]),
                reads=[wqk], writes=[(wpair[k].key, 0)], dma=True)
            p.call("sync", "dma_start", dict(
                out=wpair[k][:, 128:256], in_=wq[k * 128:(k + 1) * 128, D + pr * 128:D + (pr + 1) * 128]),
                reads=[wqk], writes=[(wpair[k].key, 1)], dma=True)
        for c0 in range(0, T, TC):
            for dst, co in ((qbuf, 0), (kbuf, 128)):
                ps = psS[co // 128]
                for k in range(ND):
                    p.call("tensor", "matmul", dict(
                        out=ps[:, 0:TC], lhsT=wpair[k][:, co:co + 128],
                        rhs=ms.x_cm[k][:, PAD + c0:PAD + c0 + TC],
                        start=(k == 0), stop=(k == ND - 1)),
                        reads=[(wpair[k].key, co // 128), ms.x_cm[k].key], writes=[ps])
                p.call("scalar", "copy", dict(out=dst[:, c0:c0 + TC], in_=ps[:, 0:TC]),
                       reads=[ps], writes=[dst.key])
        for hp in range(2):
            h = 2 * pr + hp
            ro = hp * 64
            p.call("vector", "memset", dict(ap=vau.ap, constant=1.0), writes=[vau])
            for k in range(ND):
                p.call("sync", "dma_start", dict(
                    out=wvh[k].ap, in_=wq[k * 128:(k + 1) * 128, 2 * D + h * 64:2 * D + (h + 1) * 64]),
                    reads=[wqk], writes=[wvh[k]], dma=True)
            for i in range(NT):
                for k in range(ND):
                    p.call("tensor", "matmul", dict(
                        out=psV[:, 0:64], lhsT=ms.x_cm[k][:, PAD + i * 128:PAD + (i + 1) * 128],
                        rhs=wvh[k].ap, start=(k == 0), stop=(k == ND - 1)),
                        reads=[ms.x_cm[k].key, wvh[k]], writes=[psV])
                p.call("vector", "tensor_copy", dict(
                    out=vau[:, i * 128:i * 128 + 64], in_=psV[:, 0:64]),
                    reads=[psV], writes=[vau])
            for q0 in range(0, T, QC):
                pA = psA[(q0 // QC) % 2]
                for kh in range(2):
                    for kti in range(16):
                        kt = kh * 16 + kti
                        pS = psS[kti % 3]
                        p.call("tensor", "matmul", dict(
                            out=pS[:, 0:QC], lhsT=kbuf[ro:ro + 64, kt * 128:(kt + 1) * 128],
                            rhs=qbuf[ro:ro + 64, q0:q0 + QC], start=True, stop=True),
                            reads=[kbuf.key], writes=[pS])
                        p.call("scalar", "activation", dict(
                            out=ptb[:, kti * QC:(kti + 1) * QC], in_=pS[:, 0:QC], func=AF.Exp,
                            scale=0.125, bias=misc[:, 5:6]),
                            reads=[pS, misc], writes=[(ptb.key, kti)])
                    for kti in range(16):
                        kt = kh * 16 + kti
                        p.call("tensor", "matmul", dict(
                            out=pA[:, 0:QC], lhsT=vau[:, kt * 128:(kt + 1) * 128],
                            rhs=ptb[:, kti * QC:(kti + 1) * QC],
                            start=(kt == 0), stop=(kt == NT - 1)),
                            reads=[vau, (ptb.key, kti)], writes=[pA])
                p.call("scalar", "activation", dict(
                    out=lnn.ap, in_=pA[64:128, 0:QC], func=AF.Ln, scale=float(2.0 ** 40)),
                    reads=[pA], writes=[lnn])
                p.call("scalar", "activation", dict(
                    out=recip.ap, in_=lnn.ap, func=AF.Exp, scale=-1.0, bias=misc[0:64, 6:7]),
                    reads=[lnn, misc], writes=[recip])
                p.call("vector", "tensor_tensor", dict(
                    out=att[pr][ro:ro + 64, q0:q0 + QC], in0=pA[0:64, 0:QC],
                    in1=recip.ap, op=ALU.mult),
                    reads=[pA, recip], writes=[att[pr].key])
    for c0 in range(0, T, TC):
        ln_chunk(p, ms, c0, TC, [(att[j][:, c0:c0 + TC], att[j].key) for j in range(ND)])



def emit_l3_ffn(p, ms):
    HID = 3072
    S127 = 1.0 / 127.0
    wgT = ms.w16["L3_ffn_wgT"]
    wuT = ms.w16["L3_ffn_wuT"]
    wdT = ms.w16["L3_ffn_wdT"]
    hscr = p.dscratch("l3f_h", [HID, T], F16)
    wg2 = [[p.sbuf([128, 128], F16, f"l3f_wg{par}_{k}") for k in range(ND)] for par in range(2)]
    wu2 = [[p.sbuf([128, 128], F16, f"l3f_wu{par}_{k}") for k in range(ND)] for par in range(2)]
    gb = p.sbuf([128, TC], F16, "l3f_g")
    ub = p.sbuf([128, TC], F16, "l3f_u")
    hb = [p.sbuf([128, TC], F16, "l3f_h0"), p.sbuf([128, TC], F16, "l3f_h1")]
    psGs, psUs = [ms.pb[4], ms.pb[2]], [ms.pb[5], ms.pb[3]]
    for m in range(HID // 128):
        wg = wg2[m % 2]
        wu = wu2[m % 2]
        for k in range(ND):
            p.call("sync", "dma_start", dict(
                out=wg[k].ap, in_=wgT[k * 128:(k + 1) * 128, m * 128:(m + 1) * 128]),
                reads=["w16_L3_ffn_wgT"], writes=[wg[k]], dma=True)
            p.call("sync", "dma_start", dict(
                out=wu[k].ap, in_=wuT[k * 128:(k + 1) * 128, m * 128:(m + 1) * 128]),
                reads=["w16_L3_ffn_wuT"], writes=[wu[k]], dma=True)
        for c0 in range(0, T, TC):
            psG, psU = psGs[(c0 // TC) % 2], psUs[(c0 // TC) % 2]
            for k in range(ND):
                p.call("tensor", "matmul", dict(
                    out=psG[:, 0:TC], lhsT=wg[k].ap, rhs=ms.x_cm[k][:, PAD + c0:PAD + c0 + TC],
                    start=(k == 0), stop=(k == ND - 1)),
                    reads=[wg[k], ms.x_cm[k].key], writes=[psG])
            for k in range(ND):
                p.call("tensor", "matmul", dict(
                    out=psU[:, 0:TC], lhsT=wu[k].ap, rhs=ms.x_cm[k][:, PAD + c0:PAD + c0 + TC],
                    start=(k == 0), stop=(k == ND - 1)),
                    reads=[wu[k], ms.x_cm[k].key], writes=[psU])
            p.call("scalar", "activation", dict(
                out=gb.ap, in_=psG[:, 0:TC], func=AF.Silu, scale=S127),
                reads=[psG], writes=[gb])
            p.call("scalar", "activation", dict(
                out=ub.ap, in_=psU[:, 0:TC], func=AF.Copy, scale=S127),
                reads=[psU], writes=[ub])
            h = hb[(c0 // TC) % 2]
            p.call("vector", "tensor_tensor", dict(
                out=h.ap, in0=gb.ap, in1=ub.ap, op=ALU.mult),
                reads=[gb, ub], writes=[h])
            p.call("gpsimd", "dma_start", dict(
                out=hscr[m * 128:(m + 1) * 128, c0:c0 + TC], in_=h.ap),
                reads=[h], writes=["l3f_h"], dma=True)
    wd = [p.sbuf([128, D], F16, f"l3f_wd{k}") for k in range(HID // 128)]
    for k in range(HID // 128):
        p.call("sync", "dma_start", dict(
            out=wd[k].ap, in_=wdT[k * 128:(k + 1) * 128, :]),
            reads=["w16_L3_ffn_wdT"], writes=[wd[k]], dma=True, init=True)
    hbuf = p.sbuf([128, (HID // 128) * TC], F16, "l3f_hb")
    sub = [p.sbuf([128, TC], F16, f"l3f_s{j}") for j in range(ND)]
    pso = [ms.pb[2], ms.pb[3], ms.pb[4]]
    for c0 in range(0, T, TC):
        for k in range(HID // 128):
            p.call("sync", "dma_start", dict(
                out=hbuf[:, k * TC:(k + 1) * TC], in_=hscr[k * 128:(k + 1) * 128, c0:c0 + TC]),
                reads=["l3f_h"], writes=[hbuf], dma=True)
        for j in range(ND):
            po = pso[j % 3]
            for k in range(HID // 128):
                p.call("tensor", "matmul", dict(
                    out=po[:, 0:TC], lhsT=wd[k][:, j * 128:(j + 1) * 128],
                    rhs=hbuf[:, k * TC:(k + 1) * TC],
                    start=(k == 0), stop=(k == HID // 128 - 1)),
                    reads=[wd[k], hbuf], writes=[po])
            p.call("scalar", "activation", dict(
                out=sub[j].ap, in_=po[:, 0:TC], func=AF.Copy, scale=S127),
                reads=[po], writes=[sub[j]])
        ln_chunk(p, ms, c0, TC, [(sub[j].ap, sub[j]) for j in range(ND)])



def _elu_p1(p, ms, ps_in, cols, tmp, out_ap, out_deps):
    p.call("vector", "tensor_scalar", dict(
        out=tmp[:, 0:cols], in0=ps_in, scalar1=0.0, scalar2=None, op0=ALU.min),
        reads=[out_deps[0]], writes=[tmp])
    p.call("scalar", "activation", dict(
        out=tmp[:, 0:cols], in_=tmp[:, 0:cols], func=AF.Exp),
        reads=[tmp], writes=[tmp])
    p.call("vector", "scalar_tensor_tensor", dict(
        out=out_ap, in0=ps_in, scalar=0.0, in1=tmp[:, 0:cols],
        op0=ALU.max, op1=ALU.add),
        reads=[out_deps[0], tmp], writes=[out_deps[1]])


def emit_l0_attn(p, ms):
    H, HD, FD0 = 16, 64, 256
    wqT = ms.w16["L0_attn_WqT"]
    wkT = ms.w16["L0_attn_WkT"]
    wvT = ms.w16["L0_attn_WvT"]
    woT = ms.w16["L0_attn_WoT"]
    wqTk, wkTk, wvTk, woTk = ("w16_L0_attn_WqT", "w16_L0_attn_WkT",
                              "w16_L0_attn_WvT", "w16_L0_attn_WoT")
    acat = [p.sbuf([128, T], F16, f"l0a_ac{j}") for j in range(ND)]
    inner = p.unit_scope(); inner.__enter__()
    qpair = p.sbuf([128, T], F16, "l0a_q")
    kpair = p.sbuf([128, T], F16, "l0a_k")
    vau = p.sbuf([128, NT * 66], F16, "l0a_vau")
    wst = p.sbuf([128, 256], F32, "l0a_wst")
    wpr = [p.sbuf([128, 256], F16, f"l0a_wp{k}") for k in range(ND)]
    wvh = [p.sbuf([128, 64], F16, f"l0a_wv{k}") for k in range(ND)]
    wf2 = p.sbuf([128, 256], F16, "l0a_wf2")
    kftr = [p.sbuf([128, 256], F16, "l0a_kft0"), p.sbuf([128, 256], F16, "l0a_kft1")]
    kvh = p.sbuf([128, 132], F16, "l0a_kv")
    tmpr = [p.sbuf([128, 256], F32, "l0a_tmp0"), p.sbuf([128, 256], F32, "l0a_tmp1")]
    nrmr = [p.sbuf([128, 1], F32, "l0a_nrm0"), p.sbuf([128, 1], F32, "l0a_nrm1")]
    atr = [p.sbuf([128, 64], F16, "l0a_at0"), p.sbuf([128, 64], F16, "l0a_at1")]
    ident = ms.c["identity"]
    misc = ms.c["misc"]
    p.call("scalar", "copy", dict(out=wf2.ap, in_=ms.c["L0a_wf2"].ap),
           reads=[ms.c["L0a_wf2"]], writes=[wf2])
    psS = [ms.pb[4], ms.pb[5]]
    psQ2 = [ms.pb[4], ms.pb[5]]
    psKV = [ms.pb[0], ms.pb[1]]
    psOr = [ms.pb[2], ms.pb[3]]
    for pr in range(H // 2):
        for srcw, srck, dst in ((wqT, wqTk, 0), (wkT, wkTk, 1)):
            for k in range(ND):
                p.call("sync", "dma_start", dict(
                    out=wpr[k][:, dst * 128:dst * 128 + 128],
                    in_=srcw[k * 128:(k + 1) * 128, pr * 128:(pr + 1) * 128]),
                    reads=[srck], writes=[(wpr[k].key, dst)], dma=True)
        for c0 in range(0, T, TC):
            for dst, buf in ((0, qpair), (1, kpair)):
                ps = psS[dst]
                for k in range(ND):
                    p.call("tensor", "matmul", dict(
                        out=ps[:, 0:TC], lhsT=wpr[k][:, dst * 128:dst * 128 + 128],
                        rhs=ms.x_cm[k][:, PAD + c0:PAD + c0 + TC],
                        start=(k == 0), stop=(k == ND - 1)),
                        reads=[(wpr[k].key, dst), ms.x_cm[k].key], writes=[ps])
                p.call("scalar", "copy", dict(out=buf[:, c0:c0 + TC], in_=ps[:, 0:TC]),
                       reads=[ps], writes=[buf.key])
        for hp in range(2):
            h = 2 * pr + hp
            ro = hp * 64
            p.call("vector", "memset", dict(ap=vau.ap, constant=1.0), writes=[vau])
            for k in range(ND):
                p.call("sync", "dma_start", dict(
                    out=wvh[k].ap, in_=wvT[k * 128:(k + 1) * 128, h * 64:(h + 1) * 64]),
                    reads=[wvTk], writes=[wvh[k]], dma=True)
            for i in range(NT):
                p_v = ms.pb[3]
                for k in range(ND):
                    p.call("tensor", "matmul", dict(
                        out=p_v[:, 0:64], lhsT=ms.x_cm[k][:, PAD + i * 128:PAD + (i + 1) * 128],
                        rhs=wvh[k].ap, start=(k == 0), stop=(k == ND - 1)),
                        reads=[ms.x_cm[k].key, wvh[k]], writes=[p_v])
                p.call("vector", "tensor_copy", dict(
                    out=vau[:, i * 66:i * 66 + 64], in_=p_v[:, 0:64]),
                    reads=[p_v], writes=[vau])
            for gg in range(NT + 1):
                if gg < NT:
                    pf = psS[gg % 2]
                    p.call("tensor", "matmul", dict(
                        out=pf[:, 0:256], lhsT=kpair[ro:ro + 64, gg * 128:(gg + 1) * 128],
                        rhs=wf2[ro:ro + 64, :], start=True, stop=True),
                        reads=[kpair.key, wf2], writes=[pf])
                if gg >= 1:
                    i = gg - 1
                    pf = psS[i % 2]
                    kf_t = kftr[i % 2]
                    tmp_t = tmpr[i % 2]
                    _elu_p1(p, ms, pf[:, 0:256], 256, tmp_t, kf_t.ap, [pf, kf_t])
                    for fk in range(2):
                        p.call("tensor", "matmul", dict(
                            out=psKV[fk][:, 0:65], lhsT=kf_t[:, fk * 128:(fk + 1) * 128],
                            rhs=vau[:, i * 66:i * 66 + 65],
                            start=(i == 0), stop=(i == NT - 1)),
                            reads=[kf_t, vau], writes=[psKV[fk]])
            for fk in range(2):
                p.call("vector", "tensor_copy", dict(
                    out=kvh[:, fk * 66:fk * 66 + 65], in_=psKV[fk][:, 0:65]),
                    reads=[psKV[fk]], writes=[kvh])
            for gg in range(NT + 1):
                if gg < NT:
                    i = gg
                    pf2 = psQ2[i % 2]
                    for fk in range(2):
                        p.call("tensor", "matmul", dict(
                            out=pf2[:, fk * 128:(fk + 1) * 128],
                            lhsT=wf2[ro:ro + 64, fk * 128:(fk + 1) * 128],
                            rhs=qpair[ro:ro + 64, i * 128:(i + 1) * 128], start=True, stop=True),
                            reads=[qpair.key, wf2], writes=[pf2])
                if gg >= 1:
                    i = gg - 1
                    pf2 = psQ2[i % 2]
                    qf_t = kftr[i % 2]
                    tmp_t = tmpr[i % 2]
                    pO = psOr[i % 2]
                    nr = nrmr[i % 2]
                    at = atr[i % 2]
                    _elu_p1(p, ms, pf2[:, 0:256], 256, tmp_t, qf_t.ap, [pf2, qf_t])
                    for fk in range(2):
                        p.call("tensor", "matmul", dict(
                            out=pO[:, 0:65], lhsT=qf_t[:, fk * 128:(fk + 1) * 128],
                            rhs=kvh[:, fk * 66:fk * 66 + 65],
                            start=(fk == 0), stop=(fk == 1)),
                            reads=[qf_t, kvh], writes=[pO])
                    p.call("vector", "tensor_scalar", dict(
                        out=nr.ap, in0=pO[:, 64:65], scalar1=1e-6, scalar2=None, op0=ALU.add),
                        reads=[pO], writes=[nr])
                    p.call("vector", "reciprocal", dict(out=nr.ap, in_=nr.ap), reads=[nr], writes=[nr])
                    p.call("vector", "tensor_scalar", dict(
                        out=at.ap, in0=pO[:, 0:64], scalar1=nr.ap, scalar2=None, op0=ALU.mult),
                        reads=[pO, nr], writes=[at])
                    p.call("tensor", "transpose", dict(
                        out=ms.pbb[0:64, (i % 4) * 128:(i % 4) * 128 + 128], in_=at.ap, identity=ident.ap),
                        reads=[at, ident], writes=[(ms.pbb.key, i % 4)])
                    p.call("vector", "tensor_copy", dict(
                        out=acat[pr][ro:ro + 64, i * 128:(i + 1) * 128],
                        in_=ms.pbb[0:64, (i % 4) * 128:(i % 4) * 128 + 128]),
                        reads=[(ms.pbb.key, i % 4)], writes=[acat[pr].key])
    p.barrier()
    inner.__exit__(None, None, None)
    wo = [p.sbuf([128, D], F16, f"l0a_wo{k}") for k in range(ND)]
    for k in range(ND):
        p.call("sync", "dma_start", dict(out=wo[k].ap, in_=woT[k * 128:(k + 1) * 128, :]),
               reads=[woTk], writes=[wo[k]], dma=True, init=True)
    sub = [p.sbuf([128, TC], F16, f"l0a_s{j}") for j in range(ND)]
    pso = [ms.pb[2], ms.pb[3]]
    for c0 in range(0, T, TC):
        for j in range(ND):
            po = pso[j % 2]
            for k in range(ND):
                p.call("tensor", "matmul", dict(
                    out=po[:, 0:TC], lhsT=wo[k][:, j * 128:(j + 1) * 128],
                    rhs=acat[k][:, c0:c0 + TC],
                    start=(k == 0), stop=(k == ND - 1)),
                    reads=[wo[k], acat[k].key], writes=[po])
            p.call("scalar", "copy", dict(out=sub[j].ap, in_=po[:, 0:TC]),
                   reads=[po], writes=[sub[j]])
        ln_chunk(p, ms, c0, TC, [(sub[j].ap, sub[j]) for j in range(ND)])



def emit_l3_attn(p, ms):
    p.pe_sync = True
    DI, S = 2048, 16
    CW = 512
    NDI = DI // 128
    win = ms.w16["L3_attn_W_in"]
    wxre = ms.w16["L3a_WxT"]
    wout = ms.w16["L3_attn_W_out"]
    convw = ms.c["L3a_convw"]
    convb = ms.c["L3a_convb"]
    wdt = ms.c["L3a_wdt"]
    bdt = ms.c["L3a_bdt"]
    alog = ms.c["L3a_alog"]
    dvec = ms.c["L3a_D"]
    selg = load_lazy_const(p, ms, "L3a_selg")
    sumsel = load_lazy_const(p, ms, "L3a_sumsel")
    selbc = load_lazy_const(p, ms, "L3a_selbc")
    ident_f = ms.c["identityf"]
    misc = ms.c["misc"]
    ones1 = ms.c["ones1"]

    a_sb = p.sbuf([1, 16], F32, "l3a_asb")
    app = p.sbuf([128, 1], F32, "l3a_app")
    arow = p.sbuf([1, 128], F32, "l3a_arow")
    apat = p.sbuf([16, 128], BF16, "l3a_apat")
    selA = p.sbuf([128, 16 * 128], BF16, "l3a_selA")
    p.call("scalar", "activation", dict(out=a_sb.ap, in_=alog.ap, func=AF.Exp),
           reads=[alog], writes=[a_sb])
    p.call("vector", "tensor_scalar_mul", dict(out=a_sb.ap, in0=a_sb.ap, scalar1=-1.0),
           reads=[a_sb], writes=[a_sb])
    p.call("tensor", "matmul", dict(out=ms.pb[1][0:16, 0:1], lhsT=a_sb.ap, rhs=ones1[0:1, 0:1],
                                    start=True, stop=True), reads=[a_sb, ones1], writes=[ms.pb[1]])
    a16 = p.sbuf([16, 1], F32, "l3a_a16")
    p.call("vector", "tensor_copy", dict(out=a16.ap, in_=ms.pb[1][0:16, 0:1]),
           reads=[ms.pb[1]], writes=[a16])
    p.call("vector", "tensor_scalar", dict(
        out=apat.ap, in0=ms.c["selB16"].ap, scalar1=a16.ap, scalar2=None, op0=ALU.mult),
        reads=[a16, ms.c["selB16"]], writes=[apat])
    p.call("tensor", "matmul", dict(out=ms.pb[1][:, 0:1], lhsT=apat.ap, rhs=ms.c["ones128"][0:16, 0:1],
                                    start=True, stop=True), reads=[apat, ms.c["ones128"]], writes=[ms.pb[1]])
    p.call("vector", "tensor_copy", dict(out=app.ap, in_=ms.pb[1][:, 0:1]), reads=[ms.pb[1]], writes=[app])
    p.call("tensor", "matmul", dict(out=ms.pb[1][0:1, 0:128], lhsT=app.ap, rhs=ident_f.ap,
                                    start=True, stop=True), reads=[app, ident_f], writes=[ms.pb[1]])
    p.call("vector", "tensor_copy", dict(out=arow.ap, in_=ms.pb[1][0:1, 0:128]), reads=[ms.pb[1]], writes=[arow])
    p.call("tensor", "matmul", dict(out=ms.pb[1][:, 0:128], lhsT=ones1.ap, rhs=arow.ap,
                                    start=True, stop=True), reads=[ones1, arow], writes=[ms.pb[1]])
    for g in range(16):
        p.call("vector", "tensor_tensor", dict(
            out=selA[:, g * 128:(g + 1) * 128], in0=selg[:, g * 128:(g + 1) * 128],
            in1=ms.pb[1][:, 0:128], op=ALU.mult),
            reads=[selg, ms.pb[1]], writes=[selA])

    wmt = [p.sbuf([128, 1024], F16, f"l3a_wmt{i}") for i in range(2)]
    _wjt = p.sbuf([128, 16 * 128], F16, "l3a_wjt0")
    wjt = [_wjt, _wjt]
    wxs = [p.sbuf([128, 33], F16, f"l3a_wx{k}") for k in range(NDI)]
    raw = [p.sbuf([128, 3 + CW], F16, "l3a_raw0"), p.sbuf([128, 3 + CW], F16, "l3a_raw1")]
    tails = p.sbuf([128, NDI * 3], F16, "l3a_tails")
    xip = p.sbuf([128, NDI * CW], F16, "l3a_xip")
    zbuf = p.sbuf([128, NDI * CW], F16, "l3a_z")
    delta = p.sbuf([128, NDI * CW], F16, "l3a_delta")
    dtxm = p.sbuf([128, CW], F16, "l3a_dtxm")
    ygate = p.sbuf([128, NDI * CW], F16, "l3a_yg")
    xpsb = p.sbuf([33, CW], F16, "l3a_xp")
    bbc = p.sbuf([128, CW], F16, "l3a_bbc")
    cbc = p.sbuf([128, CW], F16, "l3a_cbc")
    dtb = p.sbuf([128, CW], F32, "l3a_dtb")
    t2 = p.sbuf([128, CW], F32, "l3a_t2")
    acc = p.sbuf([128, CW], F32, "l3a_acc")
    t3 = Buf(acc.key, acc.h)
    asb = [Buf(dtb.key, dtb.h), Buf(t2.key, t2.h)]
    usb = [p.sbuf([128, CW], F16, "l3a_u0"), p.sbuf([128, CW], F16, "l3a_u1")]
    hsb = [p.sbuf([128, CW], F32, "l3a_h0"), p.sbuf([128, CW], F32, "l3a_h1")]
    hcs = [p.sbuf([128, CW], F16, "l3a_hc0"), p.sbuf([128, CW], F16, "l3a_hc1")]
    carry = p.sbuf([128, 256], F32, "l3a_carry")
    szb = p.sbuf([128, CW], F16, "l3a_sz")
    sub = [p.sbuf([128, CW], F16, f"l3a_s{j}") for j in range(ND)]
    p.call("vector", "memset", dict(ap=carry.ap, constant=0.0), writes=[carry])
    p.call("vector", "memset", dict(ap=tails.ap, constant=0.0), writes=[tails])

    psXZ = [ms.pb[4], ms.pb[5], ms.pb[1]]
    psU2 = [ms.pb[1], ms.pb[0]]
    psB = ms.pb[0]
    psY = [ms.pb[2], ms.pb[3]]
    for ci, c0 in enumerate(range(0, T, CW)):
        for m in range(32):
            ph = psXZ[m % 3]
            wm = wmt[m % 2]
            kt3 = wm.h.rearrange("p (kt c) -> p kt c", c=128)
            p.call("sync", "dma_start", dict(
                out=kt3, in_=win[:, m * 128:(m + 1) * 128].rearrange("(kt p) c -> p kt c", p=128)),
                reads=["w16_L3_attn_W_in"], writes=[wm], dma=True)
            for k in range(ND):
                p.call("tensor", "matmul", dict(
                    out=ph[:, 0:CW], lhsT=wm[:, k * 128:(k + 1) * 128],
                    rhs=ms.x_cm[k][:, PAD + c0:PAD + c0 + CW],
                    start=(k == 0), stop=(k == ND - 1)),
                    reads=[wm, ms.x_cm[k].key], writes=[ph])
            if m < NDI:
                rb = raw[m % 2]
                p.call("vector", "tensor_copy", dict(out=rb[:, 0:3], in_=tails[:, m * 3:m * 3 + 3]),
                       reads=[tails], writes=[rb])
                p.call("scalar", "copy", dict(out=rb[:, 3:3 + CW], in_=ph[:, 0:CW]),
                       reads=[ph], writes=[(rb.key, 1)])
                p.call("vector", "tensor_copy", dict(out=tails[:, m * 3:m * 3 + 3], in_=rb[:, CW:CW + 3]),
                       reads=[(rb.key, 1)], writes=[tails])
                p.call("vector", "tensor_scalar", dict(
                    out=acc.ap, in0=rb[:, 0:CW], scalar1=convw[:, m * 4:m * 4 + 1], scalar2=None,
                    op0=ALU.mult), reads=[rb, (rb.key, 1), convw], writes=[acc])
                for j in range(1, 4):
                    p.call("vector", "scalar_tensor_tensor", dict(
                        out=acc.ap, in0=rb[:, j:j + CW], scalar=convw[:, m * 4 + j:m * 4 + j + 1],
                        in1=acc.ap, op0=ALU.mult, op1=ALU.add),
                        reads=[rb, (rb.key, 1), convw, acc], writes=[acc])
                p.call("scalar", "activation", dict(
                    out=xip[:, m * CW:(m + 1) * CW], in_=acc.ap, func=AF.Silu,
                    bias=convb[:, m:m + 1]),
                    reads=[acc, convb], writes=[(xip.key, m)])
            else:
                mz = m - NDI
                p.call("scalar", "copy", dict(out=zbuf[:, mz * CW:(mz + 1) * CW], in_=ph[:, 0:CW]),
                       reads=[ph], writes=[(zbuf.key, mz)])
        for k in range(NDI):
            if ci == 0:
                p.call("sync", "dma_start", dict(
                    out=wxs[k].ap, in_=wxre[k * 128:(k + 1) * 128, :]),
                    reads=["w16_L3a_WxT"], writes=[wxs[k]], dma=True, init=True)
            p.call("tensor", "matmul", dict(
                out=psB[0:33, 0:CW], lhsT=wxs[k].ap, rhs=xip[:, k * CW:(k + 1) * CW],
                start=(k == 0), stop=(k == NDI - 1)),
                reads=[wxs[k], (xip.key, k)], writes=[psB])
        p.call("scalar", "copy", dict(out=xpsb.ap, in_=psB[0:33, 0:CW]), reads=[psB], writes=[xpsb])
        p.call("tensor", "matmul", dict(
            out=psB[:, 0:CW], lhsT=selbc[:, 0:128], rhs=xpsb.ap, start=True, stop=True),
            reads=[selbc, xpsb], writes=[psB])
        p.call("vector", "tensor_copy", dict(out=dtb.ap, in_=psB[:, 0:CW]), reads=[psB], writes=[dtb])
        p.call("tensor", "matmul", dict(
            out=psB[:, 0:CW], lhsT=selbc[:, 128:256], rhs=xpsb.ap, start=True, stop=True),
            reads=[selbc, xpsb], writes=[psB])
        p.call("vector", "tensor_copy", dict(out=bbc.ap, in_=psB[:, 0:CW]), reads=[psB], writes=[bbc])
        p.call("tensor", "matmul", dict(
            out=psB[:, 0:CW], lhsT=selbc[:, 256:384], rhs=xpsb.ap, start=True, stop=True),
            reads=[selbc, xpsb], writes=[psB])
        p.call("vector", "tensor_copy", dict(out=cbc.ap, in_=psB[:, 0:CW]), reads=[psB], writes=[cbc])
        for m in range(NDI):
            p.call("vector", "tensor_scalar", dict(
                out=t2.ap, in0=dtb.ap, scalar1=wdt[:, m:m + 1], scalar2=None, op0=ALU.mult),
                reads=[dtb, wdt], writes=[t2])
            p.call("vector", "tensor_scalar", dict(
                out=t2.ap, in0=t2.ap, scalar1=bdt[:, m:m + 1], scalar2=None, op0=ALU.add),
                reads=[t2, bdt], writes=[t2])
            p.call("scalar", "activation", dict(out=t3.ap, in_=t2.ap, func=AF.Abs),
                reads=[t2], writes=[t3])
            p.call("scalar", "activation", dict(out=t3.ap, in_=t3.ap, func=AF.Exp, scale=-1.0),
                   reads=[t3], writes=[t3])
            p.call("scalar", "activation", dict(out=t3.ap, in_=t3.ap, func=AF.Ln, bias=misc[:, 7:8]),
                   reads=[t3, misc], writes=[t3])
            p.call("vector", "scalar_tensor_tensor", dict(
                out=delta[:, m * CW:(m + 1) * CW], in0=t2.ap, scalar=0.0, in1=t3.ap,
                op0=ALU.max, op1=ALU.add), reads=[t2, t3], writes=[(delta.key, m)])
        for m in range(NDI):
            pY = psY[m % 2]
            p.call("vector", "tensor_tensor", dict(
                out=dtxm.ap, in0=delta[:, m * CW:(m + 1) * CW],
                in1=xip[:, m * CW:(m + 1) * CW], op=ALU.mult),
                reads=[(delta.key, m), (xip.key, m)], writes=[dtxm])
            p.same_engine_window = 1 << 30
            for gg in range(17):
                if gg < 16:
                    g = gg
                    psa = psXZ[g % 2]
                    psu = psU2[g % 2]
                    p.call("tensor", "matmul", dict(
                        out=psa[:, 0:CW], lhsT=selA[:, g * 128:(g + 1) * 128],
                        rhs=delta[:, m * CW:(m + 1) * CW], start=True, stop=True),
                        reads=[selA, (delta.key, m)], writes=[psa])
                    p.call("tensor", "matmul", dict(
                        out=psu[:, 0:CW], lhsT=selg[:, g * 128:(g + 1) * 128],
                        rhs=dtxm.ap, start=True, stop=True),
                        reads=[selg, dtxm], writes=[psu])
                if 1 <= gg:
                    g = gg - 1
                    psa = psXZ[g % 2]
                    psu = psU2[g % 2]
                    a_t = asb[g % 2]
                    u_t = usb[g % 2]
                    h_t = hsb[g % 2]
                    hc_t = hcs[g % 2]
                    col = m * 16 + g
                    p.call("scalar", "activation", dict(out=a_t.ap, in_=psa[:, 0:CW], func=AF.Exp),
                           reads=[psa], writes=[a_t])
                    p.call("vector", "tensor_tensor", dict(
                        out=u_t.ap, in0=psu[:, 0:CW], in1=bbc.ap, op=ALU.mult),
                        reads=[psu, bbc], writes=[u_t])
                    p.call("vector", "tensor_tensor_scan", dict(
                        out=h_t.ap, data0=a_t.ap, data1=u_t.ap, initial=carry[:, col:col + 1],
                        op0=ALU.mult, op1=ALU.add),
                        reads=[a_t, u_t, (carry.key, col)], writes=[h_t])
                    p.call("vector", "tensor_copy", dict(
                        out=carry[:, col:col + 1], in_=h_t[:, CW - 1:CW]),
                        reads=[h_t], writes=[(carry.key, col)])
                    p.call("gpsimd", "tensor_tensor", dict(
                        out=hc_t.ap, in0=h_t.ap, in1=cbc.ap, op=ALU.mult),
                        reads=[h_t, cbc], writes=[hc_t])
                    p.call("tensor", "matmul", dict(
                        out=pY[:, 0:CW], lhsT=sumsel[:, g * 128:(g + 1) * 128], rhs=hc_t.ap,
                        start=(g == 0), stop=(g == 15)),
                        reads=[sumsel, hc_t], writes=[pY])
            p.same_engine_window = 6
            p.call("scalar", "activation", dict(
                out=szb.ap, in_=zbuf[:, m * CW:(m + 1) * CW], func=AF.Silu),
                reads=[(zbuf.key, m)], writes=[szb])
            p.call("vector", "scalar_tensor_tensor", dict(
                out=acc.ap, in0=xip[:, m * CW:(m + 1) * CW], scalar=dvec[:, m:m + 1],
                in1=pY[:, 0:CW], op0=ALU.mult, op1=ALU.add),
                reads=[(xip.key, m), dvec, pY], writes=[acc])
            p.call("vector", "tensor_tensor", dict(
                out=ygate[:, m * CW:(m + 1) * CW], in0=acc.ap, in1=szb.ap, op=ALU.mult),
                reads=[acc, szb], writes=[(ygate.key, m)])
        for j in range(ND):
            po = psY[j % 2]
            wj = wjt[j % 2]
            kt3 = wj.h.rearrange("p (kt c) -> p kt c", c=128)
            p.call("sync", "dma_start", dict(
                out=kt3, in_=wout[:, j * 128:(j + 1) * 128].rearrange("(kt p) c -> p kt c", p=128)),
                reads=["w16_L3_attn_W_out"], writes=[wj], dma=True)
            for k in range(NDI):
                p.call("tensor", "matmul", dict(
                    out=po[:, 0:CW], lhsT=wj[:, k * 128:(k + 1) * 128],
                    rhs=ygate[:, k * CW:(k + 1) * CW],
                    start=(k == 0), stop=(k == NDI - 1)),
                    reads=[wj, (ygate.key, k)], writes=[po])
            p.call("scalar", "copy", dict(out=sub[j].ap, in_=po[:, 0:CW]), reads=[po], writes=[sub[j]])
        ln_chunk(p, ms, c0, CW, [(sub[j].ap, sub[j]) for j in range(ND)])
    p.pe_sync = False





def dma3(p, dstbuf, ncols_total, csz, src2d, reads=(), eng="sync"):
    kt = ncols_total // csz
    out3 = dstbuf.h[:, 0:ncols_total].rearrange("p (kt c) -> p kt c", c=csz)
    in3 = src2d.rearrange("(kt p) c -> p kt c", p=128)
    p.call(eng, "dma_start", dict(out=out3, in_=in3), reads=list(reads), writes=[dstbuf], dma=True)

def _l2f_shift_col(k, c):
    return c - 1 if k >= 4 else c


def emit_l2f_inner(p, ms, l, bufs):
    (wqkv, w1il, w2re, qsc, ksc, lng, lnb, b1il, b2c, tabs,
     qbuf, kbuf, vau, obuf, ptb, qsq, rstd, t1, t2, rot_ps, lnbuf, hm,
     w1st, w1sb, w2st, w2sb, lnn, recip, wpq, wvh, qnb) = bufs
    maskf = ms.l2f_maskf
    smt = ms.l2f_smt
    H, HD, W = 8, 64, 512
    INNER = 512
    HID22 = 22
    misc = ms.c["misc"]
    ones = ms.c["ones128"]
    mask = ms.l2f_mask
    rotP = ms.c["rotPT"]
    psA = [ms.pb[4], ms.pb[5]]
    psB = [ms.pb[2], ms.pb[3]]
    psN = ms.pb[0]
    psR = ms.pb[1]
    for pr in range(H // 2):
        for dst, buf, co in ((0, qbuf, pr * 128), (1, kbuf, INNER + pr * 128)):
            dma3(p, wpq[2], 1024, 128, wqkv[:, co:co + 128], reads=[f"w16_L2f_wqkv{l}"])
            for c0 in range(0, T, TC):
                ps = psA[dst]
                for k in range(ND):
                    sc = _l2f_shift_col(k, PAD + c0)
                    p.call("tensor", "matmul", dict(
                        out=ps[:, 0:TC], lhsT=wpq[2][:, k * 128:(k + 1) * 128],
                        rhs=ms.x_cm[k][:, sc:sc + TC],
                        start=(k == 0), stop=(k == ND - 1)),
                        reads=[wpq[2], ms.x_cm[k].key], writes=[ps])
                p.call("scalar", "copy", dict(out=buf[:, c0:c0 + TC], in_=ps[:, 0:TC]),
                       reads=[ps], writes=[buf.key])
        for qk, buf, sct, ctab, stab in ((0, qbuf, qsc, 0, 1), (1, kbuf, ksc, 2, 3)):
            for c0 in range(0, T, TC):
                p.call("sync", "dma_start", dict(
                    out=tabs[0].ap, in_=ms.l2f_tabd[:, (2 * qk) * T + c0:(2 * qk) * T + c0 + TC]),
                    writes=[tabs[0]], dma=True)
                p.call("sync", "dma_start", dict(
                    out=tabs[1].ap, in_=ms.l2f_tabd[:, (2 * qk + 1) * T + c0:(2 * qk + 1) * T + c0 + TC]),
                    writes=[tabs[1]], dma=True)
                p.call("scalar", "activation", dict(
                    out=qsq.ap, in_=buf[:, c0:c0 + TC], func=AF.Square),
                    reads=[buf.key], writes=[qsq])
                for hp in range(2):
                    ro = hp * 64
                    p.call("tensor", "matmul", dict(
                        out=psN[:, 0:TC], lhsT=ones[ro:ro + 64, :], rhs=qsq[ro:ro + 64, 0:TC],
                        start=True, stop=True),
                        reads=[qsq, ones], writes=[psN])
                    p.call("scalar", "activation", dict(
                        out=t1.ap, in_=psN[:, 0:TC], func=AF.Ln, bias=misc[:, 1:2]),
                        reads=[psN, misc], writes=[t1])
                    p.call("scalar", "activation", dict(
                        out=rstd.ap, in_=t1.ap, func=AF.Exp, scale=-0.5),
                        reads=[t1], writes=[rstd])
                    p.call("vector", "scalar_tensor_tensor", dict(
                        out=qnb[ro:ro + 64, 0:TC], in0=buf[ro:ro + 64, c0:c0 + TC],
                        scalar=sct[ro:ro + 64, l:l + 1], in1=rstd[ro:ro + 64, 0:TC],
                        op0=ALU.mult, op1=ALU.mult),
                        reads=[buf.key, sct, rstd], writes=[qnb])
                p.call("tensor", "matmul", dict(
                    out=rot_ps[:, 0:TC], lhsT=rotP.ap, rhs=qnb[:, 0:TC],
                    start=True, stop=True),
                    reads=[rotP, qnb], writes=[rot_ps])
                p.call("vector", "tensor_tensor", dict(
                    out=t1.ap, in0=qnb[:, 0:TC], in1=tabs[0].ap, op=ALU.mult),
                    reads=[qnb, tabs[0]], writes=[t1])
                p.call("vector", "tensor_tensor", dict(
                    out=t2.ap, in0=rot_ps[:, 0:TC], in1=tabs[1].ap, op=ALU.mult),
                    reads=[rot_ps, tabs[1]], writes=[t2])
                p.call("vector", "tensor_tensor", dict(
                    out=buf[:, c0:c0 + TC], in0=t1.ap, in1=t2.ap, op=ALU.add),
                    reads=[t1, t2], writes=[buf.key])
        for hp in range(2):
            h = 2 * pr + hp
            ro = hp * 64
            dma3(p, wpq[2], 512, 64, wqkv[:, 2 * INNER + h * 64:2 * INNER + (h + 1) * 64],
                 reads=[f"w16_L2f_wqkv{l}"])
            p.call("vector", "memset", dict(ap=vau.ap, constant=1.0), writes=[vau])
            for i in range(NT):
                for k in range(ND):
                    sc = _l2f_shift_col(k, PAD + i * 128)
                    p.call("tensor", "matmul", dict(
                        out=psN[:, 0:64], lhsT=ms.x_cm[k][:, sc:sc + 128],
                        rhs=wpq[2][:, k * 64:(k + 1) * 64], start=(k == 0), stop=(k == ND - 1)),
                        reads=[ms.x_cm[k].key, wpq[2]], writes=[psN])
                p.call("vector", "tensor_copy", dict(
                    out=vau[:, i * 128:i * 128 + 64], in_=psN[:, 0:64]),
                    reads=[psN], writes=[vau])
            for w in range(T // W):
                q0 = w * W
                kts = [kt for kt in range((w - 1) * 4, w * 4 + 4) if kt >= 0]
                for ki, kt in enumerate(kts):
                    pS = (psA + [psN])[ki % 3]
                    st_ = smt[ki % 3]
                    p.call("tensor", "matmul", dict(
                        out=pS[:, 0:W], lhsT=kbuf[ro:ro + 64, kt * 128:(kt + 1) * 128],
                        rhs=qbuf[ro:ro + 64, q0:q0 + W], start=True, stop=True),
                        reads=[kbuf.key], writes=[pS])
                    dq = kt * 128 - q0
                    if 0 <= dq < W:
                        madd = mask[:, (dq // 128) * 512:(dq // 128) * 512 + W]
                        mkey = mask
                    else:
                        blk = (kt * 128 - (q0 - W)) // 128
                        madd = maskf[:, blk * 512:blk * 512 + W]
                        mkey = maskf
                    p.call("vector", "tensor_tensor", dict(
                        out=st_[:, 0:W], in0=pS[:, 0:W], in1=madd, op=ALU.add),
                        reads=[pS, mkey], writes=[st_])
                    p.call("scalar", "activation", dict(
                        out=ptb[:, ki * W:(ki + 1) * W], in_=st_[:, 0:W], func=AF.Exp,
                        bias=misc[:, 5:6]),
                        reads=[st_, misc], writes=[(ptb.key, ki)])
                for ki, kt in enumerate(kts):
                    p.call("tensor", "matmul", dict(
                        out=psB[w % 2][:, 0:W], lhsT=vau[:, kt * 128:(kt + 1) * 128],
                        rhs=ptb[:, ki * W:(ki + 1) * W],
                        start=(ki == 0), stop=(ki == len(kts) - 1)),
                        reads=[vau, (ptb.key, ki)], writes=[psB[w % 2]])
                p.call("scalar", "activation", dict(
                    out=lnn.ap, in_=psB[w % 2][64:128, 0:W], func=AF.Ln, scale=float(2.0 ** 40)),
                    reads=[psB[w % 2]], writes=[lnn])
                p.call("scalar", "activation", dict(
                    out=recip.ap, in_=lnn.ap, func=AF.Exp, scale=-1.0, bias=misc[0:64, 6:7]),
                    reads=[lnn, misc], writes=[recip])
                p.call("vector", "tensor_tensor", dict(
                    out=obuf[ro:ro + 64, q0:q0 + W], in0=psB[w % 2][0:64, 0:W],
                    in1=recip.ap, op=ALU.mult),
                    reads=[psB[w % 2], recip], writes=[obuf.key])
        p.call("gpsimd", "dma_start", dict(
            out=ms.l2f_oscr[pr * 128:(pr + 1) * 128, :], in_=obuf.ap),
            reads=[obuf.key], writes=["l2f_oscr"], dma=True)
    ob4 = ms.l2f_ob4
    for c0 in range(0, T, TC):
        for k in range(4):
            p.call("sync", "dma_start", dict(
                out=ob4[k].ap, in_=ms.l2f_oscr[k * 128:(k + 1) * 128, c0:c0 + TC]),
                reads=["l2f_oscr"], writes=[ob4[k]], dma=True)
        for j in range(ND):
            po = psB[j % 2]
            dma3(p, w2sb, 512, 128, bufs_wo(ms, l)[:, j * 128:(j + 1) * 128],
                 reads=[f"w16_L2f_wo{l}"])
            for k in range(4):
                p.call("tensor", "matmul", dict(
                    out=po[:, 0:TC], lhsT=w2sb[:, k * 128:(k + 1) * 128],
                    rhs=ob4[k].ap, start=(k == 0), stop=(k == 3)),
                    reads=[w2sb, ob4[k]], writes=[po])
            p.call("vector", "tensor_tensor", dict(
                out=ms.x_cm[j][:, PAD + c0:PAD + c0 + TC],
                in0=po[:, 0:TC], in1=ms.x_cm[j][:, PAD + c0:PAD + c0 + TC], op=ALU.add),
                reads=[po, ms.x_cm[j].key], writes=[ms.x_cm[j].key])
    xbnd = ms.l2f_xbnd
    for j in range(4, ND):
        for bi, c0 in enumerate(range(TC, T, TC)):
            p.call("vector", "tensor_copy", dict(
                out=xbnd[j - 4][:, bi:bi + 1], in_=ms.x_cm[j][:, PAD + c0 - 1:PAD + c0]),
                reads=[ms.x_cm[j].key], writes=[xbnd[j - 4]])
    for c0 in range(0, T, TC):
        for j in range(ND):
            sc = _l2f_shift_col(j, PAD + c0)
            p.call("vector", "tensor_copy", dict(
                out=ms.ln_r[j][:, 0:TC], in_=ms.x_cm[j][:, sc:sc + TC]),
                reads=[ms.x_cm[j].key], writes=[ms.ln_r[j]])
            if j >= 4 and c0 > 0:
                p.call("vector", "tensor_copy", dict(
                    out=ms.ln_r[j][:, 0:1], in_=xbnd[j - 4][:, c0 // TC - 1:c0 // TC]),
                    reads=[xbnd[j - 4]], writes=[ms.ln_r[j]])
        _ln_core(p, ms, TC)
        for j in range(ND):
            rc = ms.ps_rc[j % 2]
            p.call("tensor", "matmul", dict(
                out=rc[:, 0:TC], lhsT=ms.c["identity"].ap, rhs=ms.ln_r[j][:, 0:TC],
                start=True, stop=False), reads=[ms.ln_r[j], ms.c["identity"]], writes=[rc])
            p.call("tensor", "matmul", dict(
                out=rc[:, 0:TC], lhsT=ms.c["negones_div"].ap, rhs=ms.ln_m[:, 0:TC],
                start=False, stop=True), reads=[ms.ln_m, ms.c["negones_div"]], writes=[rc])
            p.call("vector", "tensor_tensor", dict(
                out=lnbuf[j].ap, in0=rc[:, 0:TC], in1=ms.ln_rstd[:, 0:TC], op=ALU.mult),
                reads=[rc, ms.ln_rstd], writes=[lnbuf[j]])
            p.call("vector", "tensor_scalar", dict(
                out=lnbuf[j].ap, in0=lnbuf[j].ap, scalar1=lng[:, l * 8 + j:l * 8 + j + 1],
                scalar2=None, op0=ALU.mult), reads=[lnbuf[j], lng], writes=[lnbuf[j]])
            p.call("vector", "tensor_scalar", dict(
                out=lnbuf[j].ap, in0=lnbuf[j].ap, scalar1=lnb[:, l * 8 + j:l * 8 + j + 1],
                scalar2=None, op0=ALU.add), reads=[lnbuf[j], lnb], writes=[lnbuf[j]])
        for m in range(HID22):
            sb = w1sb[m % 2]
            for hf in range(2):
                kt3 = sb.h[:, hf * 1024:(hf + 1) * 1024].rearrange("p (kt c) -> p kt c", c=128)
                p.call("sync", "dma_start", dict(
                    out=kt3,
                    in_=w1il[:, m * 256 + hf * 128:m * 256 + (hf + 1) * 128].rearrange(
                        "(kt p) c -> p kt c", p=128)),
                    reads=[f"w16_L2f_w1il{l}"], writes=[sb], dma=True)
            pa0 = psA[0] if m % 2 == 0 else psB[0]
            pa1 = psA[1] if m % 2 == 0 else psB[1]
            for blk, pa in ((0, pa0), (1, pa1)):
                for k in range(ND):
                    p.call("tensor", "matmul", dict(
                        out=pa[:, 0:TC], lhsT=sb[:, blk * 1024 + k * 128:blk * 1024 + (k + 1) * 128],
                        rhs=lnbuf[k].ap, start=(k == 0), stop=(k == ND - 1)),
                        reads=[sb, lnbuf[k]], writes=[pa])
            p.call("vector", "tensor_scalar", dict(
                out=t1[:, 0:TC], in0=pa0[:, 0:TC], scalar1=b1il[:, 2 * m:2 * m + 1],
                scalar2=None, op0=ALU.add), reads=[pa0, b1il], writes=[t1])
            p.call("scalar", "activation", dict(
                out=t2[:, 0:TC], in_=pa1[:, 0:TC], func=AF.Gelu, bias=b1il[:, 2 * m + 1:2 * m + 2]),
                reads=[pa1, b1il], writes=[t2])
            p.call("vector", "tensor_tensor", dict(
                out=hm[:, m * TC:(m + 1) * TC], in0=t1[:, 0:TC], in1=t2[:, 0:TC], op=ALU.mult),
                reads=[t1, t2], writes=[(hm.key, m)])
        for j in range(ND):
            po = psB[j % 2]
            for hf in range(2):
                kt3 = w2sb.h[:, hf * 1408:(hf + 1) * 1408].rearrange("p (kt c) -> p kt c", c=128)
                p.call("sync", "dma_start", dict(
                    out=kt3,
                    in_=w2re[hf * 1408:(hf + 1) * 1408, j * 128:(j + 1) * 128].rearrange(
                        "(kt p) c -> p kt c", p=128)),
                    reads=[f"w16_L2f_w2re{l}"], writes=[w2sb], dma=True)
            for k in range(HID22):
                p.call("tensor", "matmul", dict(
                    out=po[:, 0:TC], lhsT=w2sb[:, k * 128:(k + 1) * 128],
                    rhs=hm[:, k * TC:(k + 1) * TC], start=(k == 0), stop=(k == HID22 - 1)),
                    reads=[w2sb, (hm.key, k)], writes=[po])
            p.call("vector", "tensor_scalar", dict(
                out=t1[:, 0:TC], in0=po[:, 0:TC], scalar1=b2c[:, l * 8 + j:l * 8 + j + 1],
                scalar2=None, op0=ALU.add), reads=[po, b2c], writes=[t1])
            p.call("vector", "tensor_tensor", dict(
                out=ms.x_cm[j][:, PAD + c0:PAD + c0 + TC], in0=t1[:, 0:TC],
                in1=ms.x_cm[j][:, PAD + c0:PAD + c0 + TC], op=ALU.add),
                reads=[t1, ms.x_cm[j].key], writes=[ms.x_cm[j].key])


def bufs_wo(ms, l):
    return ms.l2f_wo[l]


def _ln_core(p, ms, w):
    ones = ms.c["ones128"]
    misc = ms.c["misc"]
    for j in range(ND):
        rq = ms.ln_rsq[j % 2]
        p.call("scalar", "activation", dict(
            out=rq[:, 0:w], in_=ms.ln_r[j][:, 0:w], func=AF.Square),
            reads=[ms.ln_r[j]], writes=[rq])
        p.call("tensor", "matmul", dict(
            out=ms.ps_s[:, 0:w], lhsT=ones.ap, rhs=ms.ln_r[j][:, 0:w],
            start=(j == 0), stop=(j == ND - 1)), reads=[ms.ln_r[j], ones], writes=[ms.ps_s])
        p.call("tensor", "matmul", dict(
            out=ms.ps_q[:, 0:w], lhsT=ones.ap, rhs=rq[:, 0:w],
            start=(j == 0), stop=(j == ND - 1)), reads=[rq, ones], writes=[ms.ps_q])
    p.call("vector", "tensor_scalar_mul", dict(
        out=ms.ln_m[:, 0:w], in0=ms.ps_s[:, 0:w], scalar1=1.0 / D),
        reads=[ms.ps_s], writes=[ms.ln_m])
    p.call("vector", "tensor_tensor", dict(
        out=ms.ln_msq[:, 0:w], in0=ms.ln_m[:, 0:w], in1=ms.ln_m[:, 0:w], op=ALU.mult),
        reads=[ms.ln_m], writes=[ms.ln_msq])
    p.call("vector", "scalar_tensor_tensor", dict(
        out=ms.ln_var[:, 0:w], in0=ms.ps_q[:, 0:w], scalar=1.0 / D, in1=ms.ln_msq[:, 0:w],
        op0=ALU.mult, op1=ALU.subtract), reads=[ms.ps_q, ms.ln_msq], writes=[ms.ln_var])
    p.call("scalar", "activation", dict(
        out=ms.ln_var[:, 0:w], in_=ms.ln_var[:, 0:w], func=AF.Ln, bias=misc[:, 0:1]),
        reads=[ms.ln_var, misc], writes=[ms.ln_var])
    p.call("scalar", "activation", dict(
        out=ms.ln_rstd[:, 0:w], in_=ms.ln_var[:, 0:w], func=AF.Exp, scale=-0.5),
        reads=[ms.ln_var], writes=[ms.ln_rstd])


def emit_l2_ffn(p, ms):
    wqkv = [ms.w16[f"L2f_wqkv{l}"] for l in range(2)]
    ms.l2f_wo = [ms.w16[f"L2f_wo{l}"] for l in range(2)]
    w1il = [ms.w16[f"L2f_w1il{l}"] for l in range(2)]
    w2re = [ms.w16[f"L2f_w2re{l}"] for l in range(2)]
    tabd = p.din("L2f_tabs", [128, 4 * T], BF16)
    xsp = p.dscratch("l2f_xorig", [ND * 128, T], F16)
    ms.l2f_mask = load_lazy_const(p, ms, "causaladd")
    ms.l2f_maskf = load_lazy_const(p, ms, "causaladd_far")
    qsc = ms.c["L2f_qs"]
    ksc = ms.c["L2f_ks"]
    lng = ms.c["L2f_lng"]
    lnb = ms.c["L2f_lnb"]
    b1il = ms.c["L2f_b1il"]
    b2c = ms.c["L2f_b2"]
    for j in range(ND):
        p.call("gpsimd", "dma_start", dict(
            out=xsp[j * 128:(j + 1) * 128, :], in_=ms.x_cm[j][:, PAD:PAD + T]),
            reads=[ms.x_cm[j].key], writes=["l2f_xorig"], dma=True)
    tabs = [p.sbuf([128, TC], BF16, "l2f_tabc"), p.sbuf([128, TC], BF16, "l2f_tabs")]
    ms.l2f_tabd = tabd
    qbuf = p.sbuf([128, T], BF16, "l2f_q")
    kbuf = p.sbuf([128, T], BF16, "l2f_k")
    vau = p.sbuf([128, NT * 128], BF16, "l2f_vau")
    obuf = p.sbuf([128, T], F16, "l2f_o")
    oscr = p.dscratch("l2f_oscr", [512, T], F16)
    ms.l2f_oscr = oscr

    ptb = p.sbuf([128, 8 * 512], BF16, "l2f_pt")
    qsq = p.sbuf([128, TC], BF16, "l2f_qsq")
    qnb = Buf(qsq.key, qsq.h)
    ms.l2f_smt = [p.sbuf([128, 512], F32, f"l2f_sm{i}") for i in range(3)]
    ms.l2f_xbnd = [p.sbuf([128, 8], F16, f"l2f_xb{i}") for i in range(4)]
    rstd = p.sbuf([128, TC], F32, "l2f_rstd")
    t1 = p.sbuf([128, TC], F32, "l2f_t1")
    t2 = p.sbuf([128, TC], F32, "l2f_t2")
    rot_ps = ms.pb[1]
    lnbuf = [p.sbuf([128, TC], F16, f"l2f_lnb{j}") for j in range(ND)]
    ms.l2f_ob4 = lnbuf[0:4]
    hm = p.sbuf([128, 22 * TC], F16, "l2f_hm")
    _w1st = p.sbuf([128, 1024], F32, "l2f_w1st0")
    w1st = [_w1st, _w1st]
    _w1sb = p.sbuf([128, 2048], F16, "l2f_w1sb0")
    w1sb = [_w1sb, _w1sb]
    w2st = p.sbuf([128, 11 * 128], F32, "l2f_w2st")
    w2sb = p.sbuf([128, 22 * 128], F16, "l2f_w2sb")
    lnn = p.sbuf([64, 512], F32, "l2f_lnn")
    recip = p.sbuf([64, 512], F32, "l2f_recip")

    wpq = [None, None, p.sbuf([128, 1024], F16, "l2f_wpq2")]
    mk_ln_bufs(p, ms)
    bufs = (None, None, None, qsc, ksc, lng, lnb, b1il, b2c, tabs,
            qbuf, kbuf, vau, obuf, ptb, qsq, rstd, t1, t2, rot_ps, lnbuf, hm,
            w1st, w1sb, w2st, w2sb, lnn, recip, wpq, None, qnb)
    import os as _os
    dbg = p.dout("l2f_dbg", [ND * 128, T], F32) if _os.environ.get("L2F_DBG") else None
    for l in range(2):
        bufs = (wqkv[l], w1il[l], w2re[l]) + bufs[3:]
        emit_l2f_inner(p, ms, l, bufs)
        if dbg is not None and l == 0:
            for j in range(ND):
                p.call("gpsimd", "dma_start", dict(
                    out=dbg[j * 128:(j + 1) * 128, :], in_=ms.x_cm[j][:, PAD:PAD + T]),
                    reads=[ms.x_cm[j].key], writes=["l2f_dbg"], dma=True)
    for c0 in range(0, T, TC):
        for j in range(ND):
            p.call("sync", "dma_start", dict(
                out=lnbuf[j].ap, in_=xsp[j * 128:(j + 1) * 128, c0:c0 + TC]),
                reads=["l2f_xorig"], writes=[lnbuf[j]], dma=True)
        ln_chunk(p, ms, c0, TC, [(lnbuf[j].ap, lnbuf[j]) for j in range(ND)])



def host_prep(inputs):
    cdict = dict(host_constants())
    bigs = {}
    f32 = np.float32

    def col128(v):
        return np.ascontiguousarray(v.reshape(-1, 128).T.astype(f32))

    bigs["L0_ffn_W1T"] = np.ascontiguousarray(inputs["L0_ffn_W1"].T.astype(f32))
    bigs["L0_ffn_W2T"] = np.ascontiguousarray(inputs["L0_ffn_W2"].T.astype(f32))
    cdict["L0f_b1"] = col128(inputs["L0_ffn_b1"])
    cdict["L0f_b2"] = col128(inputs["L0_ffn_b2"])
    bigs["L1_ffn_WrT"] = np.ascontiguousarray(inputs["L1_ffn_W_r"].T.astype(f32))
    bigs["L1_ffn_WkT"] = np.ascontiguousarray(inputs["L1_ffn_W_k"].T.astype(f32))
    bigs["L1_ffn_WvT"] = np.ascontiguousarray(inputs["L1_ffn_W_v"].T.astype(f32))
    cdict["L1f_mur"] = col128(inputs["L1_ffn_mu_r"])
    cdict["L1f_muk"] = col128(inputs["L1_ffn_mu_k"])
    W = inputs["L2_attn_MhWqkv"].astype(f32)
    cols = np.arange(3072)
    qcols = np.concatenate([(2 * pr + s // 64) * 192 + (s % 64) + 0 * 1
                            for pr in range(8) for s in [np.arange(128)]])
    kcols = np.concatenate([(2 * pr + s // 64) * 192 + 64 + (s % 64)
                            for pr in range(8) for s in [np.arange(128)]])
    vcols = np.concatenate([h * 192 + 128 + np.arange(64) for h in range(16)])
    perm = np.concatenate([qcols, kcols, vcols])
    bigs["L2_attn_Wre"] = np.ascontiguousarray(W[:, perm])
    bigs["L0_attn_WqT"] = np.ascontiguousarray(inputs["L0_attn_Wq"].T.astype(f32))
    bigs["L0_attn_WkT"] = np.ascontiguousarray(inputs["L0_attn_Wk"].T.astype(f32))
    bigs["L0_attn_WvT"] = np.ascontiguousarray(inputs["L0_attn_Wv"].T.astype(f32))
    bigs["L0_attn_WoT"] = np.ascontiguousarray(inputs["L0_attn_Wo"].T.astype(f32))
    wfT = np.ascontiguousarray(inputs["L0_attn_Wf"].T.astype(f32))
    cdict["L0a_wf2"] = np.concatenate([wfT, wfT], axis=0)
    for l in range(2):
        bigs[f"L2f_wqkv{l}"] = np.ascontiguousarray(inputs["L2_ffn_wqkv"][l].astype(f32))
        bigs[f"L2f_wo{l}"] = np.ascontiguousarray(inputs["L2_ffn_wo"][l].astype(f32))
        w1 = inputs["L2_ffn_w1"][l].astype(f32)
        w1i = np.zeros((1024, 44 * 128), f32)
        for mm in range(22):
            an = min(128, 2730 - mm * 128)
            w1i[:, mm * 256:mm * 256 + an] = w1[:, mm * 128:mm * 128 + an]
            w1i[:, mm * 256 + 128:mm * 256 + 128 + an] = w1[:, 2730 + mm * 128:2730 + mm * 128 + an]
        bigs[f"L2f_w1il{l}"] = w1i
        w2 = inputs["L2_ffn_w2"][l].astype(f32)
        w2r = np.zeros((2816, 1024), f32)
        w2r[0:2730] = w2
        bigs[f"L2f_w2re{l}"] = w2r
    b1 = inputs["L2_ffn_b1"].astype(f32)
    b1i = np.zeros((128, 2, 44), f32)
    for l in range(2):
        for mm in range(22):
            an = min(128, 2730 - mm * 128)
            b1i[0:an, l, 2 * mm] = b1[l, mm * 128:mm * 128 + an]
            b1i[0:an, l, 2 * mm + 1] = b1[l, 2730 + mm * 128:2730 + mm * 128 + an]
    cdict["L2f_b1il"] = np.ascontiguousarray(b1i.reshape(128, 88))
    cdict["L2f_b2"] = np.ascontiguousarray(
        inputs["L2_ffn_b2"].astype(f32).reshape(2, 8, 128).transpose(2, 0, 1).reshape(128, 16))
    qs = inputs["L2_ffn_q_scale"].astype(f32)
    ks = inputs["L2_ffn_k_scale"].astype(f32)
    cdict["L2f_qs"] = np.ascontiguousarray(np.tile(qs.T, (2, 1)))
    cdict["L2f_ks"] = np.ascontiguousarray(np.tile(ks.T, (2, 1)))
    lg = inputs["L2_ffn_ln_g"].astype(f32)
    lb = inputs["L2_ffn_ln_b"].astype(f32)
    cdict["L2f_lng"] = np.ascontiguousarray(lg.reshape(2, 8, 128).transpose(2, 0, 1).reshape(128, 16))
    cdict["L2f_lnb"] = np.ascontiguousarray(lb.reshape(2, 8, 128).transpose(2, 0, 1).reshape(128, 16))
    DIM_HEAD, WINDOW, n = 64, 512, 4096
    half_idx = np.arange(0, DIM_HEAD, 2, dtype=np.float64)
    inv_freq = 1.0 / 10000.0 ** (half_idx / DIM_HEAD)
    tt = np.arange(n, dtype=np.float64)
    freqs = tt[:, None] * inv_freq[None, :]
    freqs = np.concatenate([freqs, freqs], axis=-1)
    base = (half_idx + 0.4 * DIM_HEAD) / (1.4 * DIM_HEAD)
    power = (tt - n // 2) / float(WINDOW // 2)
    scale = base[None, :] ** power[:, None]
    scale = np.concatenate([scale, scale], axis=-1)
    cos, sin = np.cos(freqs), np.sin(freqs)
    cq8 = (8.0 * cos * scale).T
    sq8 = (8.0 * sin * scale).T
    ckd = (cos / scale).T
    skd = (sin / scale).T
    tab = np.concatenate([np.tile(cq8, (2, 1))[None], np.tile(sq8, (2, 1))[None],
                          np.tile(ckd, (2, 1))[None], np.tile(skd, (2, 1))[None]], axis=0)
    bigs["L2f_tabs"] = np.ascontiguousarray(tab.reshape(4, 128, n).transpose(1, 0, 2).reshape(128, 4 * n).astype(BF))
    rp = np.zeros((128, 128), dtype=BF)
    for mcol in range(128):
        if mcol % 64 < 32:
            rp[mcol + 32, mcol] = -1.0
        else:
            rp[mcol - 32, mcol] = 1.0
    cdict["rotPT"] = rp
    bigs["L3_attn_W_in"] = np.ascontiguousarray(inputs["L3_attn_W_in"].astype(f32))
    wx = inputs["L3_attn_W_x"].astype(f32)
    bigs["L3a_WxT"] = np.ascontiguousarray(wx[:, list(range(1, 33)) + [0]])
    bigs["L3_attn_W_out"] = np.ascontiguousarray(inputs["L3_attn_W_out"].astype(f32))
    cdict["L3a_convw"] = np.ascontiguousarray(
        inputs["L3_attn_conv_w"].astype(f32).reshape(16, 128, 4).transpose(1, 0, 2).reshape(128, 64))
    cdict["L3a_convb"] = col128(inputs["L3_attn_conv_b"])
    cdict["L3a_wdt"] = col128(inputs["L3_attn_W_dt"][:, 0])
    cdict["L3a_bdt"] = col128(inputs["L3_attn_b_dt"])
    cdict["L3a_alog"] = inputs["L3_attn_A_log"].astype(f32).reshape(1, 16)
    cdict["L3a_D"] = col128(inputs["L3_attn_D"])
    bigs["L3_ffn_wgT"] = np.ascontiguousarray(inputs["L3_ffn_w_gate"].T).astype(np.int32)
    bigs["L3_ffn_wuT"] = np.ascontiguousarray(inputs["L3_ffn_w_up"].T).astype(np.int32)
    bigs["L3_ffn_wdT"] = np.ascontiguousarray(inputs["L3_ffn_w_down"].T).astype(np.int32)
    return cdict, bigs


def emit_ln_only(p, ms):
    mk_ln_bufs(p, ms)
    for c0 in range(0, T, TC):
        ln_chunk(p, ms, c0, TC, None, skip_sub=True)


UNITS = {
    "l0_ffn": "emit_l0_ffn",
    "l1_ffn": "emit_l1_ffn",
    "l1_attn": "emit_l1_attn",
    "l2_attn": "emit_l2_attn",
    "l3_ffn": "emit_l3_ffn",
    "l0_attn": "emit_l0_attn",
    "l3_attn": "emit_l3_attn",
    "l2_ffn": "emit_l2_ffn",
    "ln_only": "emit_ln_only",
}


PRECAST = [
    ("L0_attn_WqT", [D, D]), ("L0_attn_WkT", [D, D]), ("L0_attn_WvT", [D, D]),
    ("L0_attn_WoT", [D, D]),
    ("L1_ffn_WrT", [D, D]), ("L1_ffn_WkT", [D, D]), ("L1_ffn_WvT", [D, D]),
    ("L2_attn_Wre", [D, 3 * D]),
    ("L2f_wqkv0", [D, 1536]), ("L2f_wo0", [512, D]), ("L2f_w1il0", [D, 44 * 128]),
    ("L2f_w2re0", [2816, D]),
    ("L2f_wqkv1", [D, 1536]), ("L2f_wo1", [512, D]), ("L2f_w1il1", [D, 44 * 128]),
    ("L2f_w2re1", [2816, D]),
    ("L3_attn_W_in", [D, 4096]), ("L3_attn_W_out", [2048, D]), ("L3a_WxT", [2048, 33]),
]
PRECAST_I32 = [
    ("L3_ffn_wgT", [D, 3072]), ("L3_ffn_wuT", [D, 3072]), ("L3_ffn_wdT", [3072, D]),
]


def emit_precast(p, ms):
    ms.w16 = {}
    for name, shape in PRECAST:
        din = p.din(name, shape, F32)
        scr = p.dscratch("w16_" + name, shape, F16)
        p.call("gpsimd", "dma_start", dict(out=scr, in_=din), writes=["w16_" + name], dma=True)
        ms.w16[name] = scr
    for name, shape in PRECAST_I32:
        din = p.din(name, shape, I32)
        scr = p.dscratch("w16_" + name, shape, F16)
        p.call("gpsimd", "dma_start", dict(out=scr, in_=din), writes=["w16_" + name], dma=True)
        ms.w16[name] = scr


def build_program(unit_names, cdict):
    nc = bass.Bass("TRN2", target_bir_lowering=False)
    with contextlib.ExitStack() as stack:
        p = Prog(nc, stack)
        ms = MS()
        xin = p.din("x", [T, D], F32)
        xout = p.dout("out", [T, D], F32)
        ms.xin, ms.xout = xin, xout
        ms.pb = [p.psum([128, 512], F32, f"pb{i}") for i in range(6)]
        ms.pbb = p.psum([128, 1024], F16, "pbb")
        p._mk_dummies()
        emit_load_consts(p, ms, cdict)
        emit_precast(p, ms)
        ms.x_cm = [p.sbuf([128, PAD + T], F16, f"xcm{i}") for i in range(ND)]
        with p.unit_scope():
            emit_prologue(p, ms, xin)
        p.barrier()
        mk_ln_bufs(p, ms)
        for u in unit_names:
            p.barrier()
            with p.unit_scope():
                globals()[UNITS[u]](p, ms)
            p.barrier()
        emit_epilogue(p, ms, xout)
        p.finalize()
    return nc


ALL_UNITS = ["l0_attn", "l0_ffn", "l1_attn", "l1_ffn", "l2_attn", "l2_ffn", "l3_attn", "l3_ffn"]
LAUNCH_GROUPS = [ALL_UNITS]


def kernel(**inputs):
    cdict, bigs = host_prep(inputs)
    x = np.ascontiguousarray(inputs["L0_attn_hidden_states"]).astype(np.float32)
    cur = x
    for group in LAUNCH_GROUPS:
        nc = build_program(group, cdict)
        in_maps = []
        for core in range(8):
            m = {"c_" + k: v for k, v in cdict.items()}
            m.update(bigs)
            m["x"] = np.ascontiguousarray(cur[core])
            in_maps.append(m)
        res = run_bass_kernel_spmd(nc, in_maps, core_ids=list(range(8)))
        cur = np.stack([r["out"] for r in res.results])
    return cur.astype(np.float32)
